# version 01
# baseline (speedup 1.0000x reference)
import math
import os
import numpy as np
import concourse.bass as bass
import concourse.mybir as mybir
from concourse.bass_utils import run_bass_kernel_spmd

F32 = mybir.dt.float32
BF16 = mybir.dt.bfloat16
AF = mybir.ActivationFunctionType
ALU = mybir.AluOpType
AX = mybir.AxisListType

B, T, D = 4, 4096, 1024
TC = 2048
GH = 16
TT = TC + GH
EPS = 1e-5
C23 = 3.0 * 2.0 ** 22
PI = math.pi

W512 = [(0, 512), (512, 512), (1024, 512), (1536, 512), (2048, 16)]
W256 = [(i * 256, 256) for i in range(8)] + [(2048, 16)]

STAGE_N = int(os.environ.get("BASSMK_STAGES", "99"))
DBG_TAP = os.environ.get("BASSMK_TAP", "")

ENGINES = ("sync", "tensor", "vector", "scalar", "gpsimd")


def lhsT_host(wT):
    d_in, M = wT.shape
    return np.ascontiguousarray(wT.reshape(d_in // 128, 128, M).transpose(1, 0, 2))


def vec_host(v):
    n = v.shape[0]
    return np.ascontiguousarray(v.reshape(n // 128, 128).T)


def fm_host(x2d):
    tn, d = x2d.shape
    return np.ascontiguousarray(x2d.T.reshape(d // 128, 128, tn).transpose(1, 0, 2))


class Sem:
    def __init__(self, nc, name):
        self.h = nc.alloc_semaphore(name)
        self.n = 0


def _ap_range(a):
    if not hasattr(a, "ap") or not hasattr(a, "offset"):
        return None
    lo = hi = a.offset
    for st, cnt in a.ap:
        d = st * (cnt - 1)
        if d >= 0:
            hi += d
        else:
            lo += d
    return (a.name, lo, hi)


class _Rec:
    __slots__ = ("ins", "writes", "inc")

    def __init__(self, ins, writes):
        self.ins = ins
        self.writes = writes
        self.inc = None


class _TIWrap:
    def __init__(self, guard, rec):
        self._g = guard
        self._r = rec

    def then_inc(self, semh, k):
        ins = self._r.ins.then_inc(semh, k)
        key = id(semh)
        self._g.semcnt[key] = self._g.semcnt.get(key, 0) + k
        self._r.inc = (semh, self._g.semcnt[key])
        return ins


class HazardGuard:

    HZK = 4

    def __init__(self, cx, eng):
        self.cx = cx
        self.eng = eng
        self.hist = []
        self.semcnt = {}

    def run_op(self, h, name, args, kwargs):
        writes = []
        reads = []
        for i, a in enumerate(args):
            r = _ap_range(a)
            if r is None:
                continue
            (writes if i == 0 else reads).append(r)
        for k, a in kwargs.items():
            r = _ap_range(a)
            if r is None:
                continue
            (writes if k in ("out", "accum_out") else reads).append(r)
        needs = []
        for rec in self.hist[-self.HZK:]:
            hit = False
            for (rn, rlo, rhi) in reads:
                for (wn, wlo, whi) in rec.writes:
                    if rn == wn and rlo <= whi and wlo <= rhi:
                        hit = True
                        break
                if hit:
                    break
            if hit:
                needs.append(rec)
        for rec in needs:
            if rec.inc is None:
                hz = self.hz
                hz.n += 1
                rec.ins.then_inc(hz.h, 1)
                key = id(hz.h)
                self.semcnt[key] = self.semcnt.get(key, 0) + 1
                rec.inc = (hz.h, self.semcnt[key])
                self.cx.n_hz_guards += 1
        waits = {}
        for rec in needs:
            semh, v = rec.inc
            key = id(semh)
            waits[key] = (semh, max(v, waits[key][1]) if key in waits else v)
        for semh, v in waits.values():
            h.wait_ge(semh, v)
        ins = getattr(h, name)(*args, **kwargs)
        rec = _Rec(ins, writes)
        self.hist.append(rec)
        if len(self.hist) > self.HZK:
            self.hist = self.hist[-self.HZK:]
        return _TIWrap(self, rec)


class _EngProxy:
    _PASS = ("wait_ge",)

    def __init__(self, h, guard):
        self._h = h
        self._g = guard

    def wait_ge(self, *a, **k):
        return self._h.wait_ge(*a, **k)

    def engine_nop(self, *a, **k):
        ins = self._h.engine_nop(*a, **k)
        rec = _Rec(ins, [])
        self._g.hist.append(rec)
        return _TIWrap(self._g, rec)

    def __getattr__(self, name):
        real = getattr(self._h, name)
        if not callable(real):
            return real
        g = self._g

        def call(*args, **kwargs):
            return g.run_op(self._h, name, args, kwargs)
        return call


class Phase:
    def __init__(self, cx, name):
        self.cx = cx
        self.name = name
        self.qs = {e: [] for e in ENGINES}

    def add(self, eng, build):
        self.qs[eng].append(build)

    def addi(self, eng, build, sem, k=1):
        sem.n += k
        self.qs[eng].append(lambda h, b=build, s=sem.h, kk=k: b(h).then_inc(s, kk))
        return sem.n

    def wait(self, eng, sem, val=None):
        v = sem.n if val is None else val
        self.qs[eng].append(lambda h, s=sem.h, vv=v: h.wait_ge(s, vv))

    def run(self):
        cx = self.cx
        with cx.nc.Block(self.name) as blk:
            for e in ENGINES:
                fns = self.qs[e]
                if fns:
                    if e in ("vector", "scalar"):
                        gd_ = cx.guards[e]
                        getattr(blk, e)(lambda h, fns=fns, gd_=gd_: [
                            f(_EngProxy(h, gd_)) for f in fns])
                    else:
                        getattr(blk, e)(lambda h, fns=fns: [f(h) for f in fns])


class Ctx:
    def __init__(self, nc):
        self.nc = nc
        self.t = {}
        self.sd = Sem(nc, "sd")
        self.gd = Sem(nc, "gd")
        self.pe = Sem(nc, "pe")
        self.dv = Sem(nc, "dv")
        self.sc = Sem(nc, "sc")
        self.gp = Sem(nc, "gp")
        self.n_hz_guards = 0
        self.guards = {"vector": HazardGuard(self, "vector"),
                       "scalar": HazardGuard(self, "scalar")}
        self.guards["vector"].hz = Sem(nc, "hzv")
        self.guards["scalar"].hz = Sem(nc, "hzs")
        self.dbg = []

    def row(self, tensor, part, c0, n):
        th = self.t[tensor]
        pitch = th.shape[1]
        return bass.AP(th, part * pitch + c0, [[pitch, 1], [1, n]])

    def wr(self, slot, cols=1024):
        return self.t["WR"][:, slot * 1024: slot * 1024 + cols]

    def res(self, w0, wn, c):
        return self.t["RES"][:, c * TT + w0: c * TT + w0 + wn]

    def xng(self, w0, wn, c):
        return self.t["XNG"][:, c * TT + w0: c * TT + w0 + wn]

    def beta_pre(self, w0, wn):
        return bass.AP(self.t["WR"], 10 * 1024 + w0, [[14336, 128], [1, wn]])

    def beta_main(self, w0, wn):
        return bass.AP(self.t["WR"], 10 * 1024 + 2048 + w0, [[14336, 128], [1, wn]])

    def beta_chunk(self, n):
        return bass.AP(self.t["WR"], 10 * 1024 + 64 * n, [[14336, 128], [1, 64]])


def build_program():
    nc = bass.Bass("TRN2", target_bir_lowering=False, detect_race_conditions=False)
    cx = Ctx(nc)
    t = cx.t

    def din(name, shape, dtype=F32):
        t[name] = nc.dram_tensor(name, list(shape), dtype, kind="ExternalInput")

    def sb(name, shape, dtype):
        t[name] = nc.alloc_sbuf_tensor(name, list(shape), dtype)

    din("xmain", (128, 8, TT))
    din("xpre", (128, 8, TC))
    din("flags", (128, 2))
    din("eyef", (128, 128)); din("eyeb", (128, 128))
    din("onesS", (128, 128)); din("onesSb", (128, 128)); din("ones1", (128, 128))
    din("refq", (128, 17))
    din("w_inT", (128, 8, 128))
    din("w_gateT", (128, 8, 1024))
    din("w_outT", (64, 2048))
    din("bdcoupT", (128, 64))
    din("log_decay", (64, 1)); din("frequency", (64, 1))
    din("w_mlp1T", (32, 128, 8, 128))
    din("w_mlp2T", (32, 128, 1024))
    din("ln1_g", (128, 8)); din("ln1_b", (128, 8))
    din("ln2_g", (128, 8)); din("ln2_b", (128, 8))
    din("b_gate", (128, 8)); din("b_mlp1", (128, 32)); din("b_mlp2", (128, 8))
    din("WgT", (128, 8, 16))
    din("Ws1T", (16, 128, 8, 128))
    din("Ws2T", (8, 128, 1024))
    din("W1e", (16, 8, 128, 8, 128))
    din("W2e", (16, 4, 128, 1024))
    din("W_valueT", (8, 128, 8, 128))
    din("W_oaT", (2, 128, 8, 128))
    din("b_value", (128, 8)); din("b_off", (128, 1)); din("b_attn", (64, 1))
    din("W_out1T", (8, 128, 8, 128)); din("b_out1", (128, 8))
    din("W_projT", (8, 128, 8, 128)); din("b_proj", (128, 8))
    din("wq_gateT", (8, 128, 8, 512))
    din("wq_upT", (8, 128, 8, 512))
    din("wq_downT", (32, 128, 1024))
    t["out"] = nc.dram_tensor("out", [128, 8, TC], F32, kind="ExternalOutput")

    sb("RES", (128, 8 * TT), F32)
    sb("XNG", (128, 8 * TT), BF16)
    sb("SCB1", (128, 16640), BF16)
    sb("SCB2", (128, 4096), BF16)
    sb("SCF", (128, 2048), F32)
    sb("WR", (128, 14336), BF16)
    sb("CRCI", (128, 4112), BF16)
    sb("CC", (128, 512), BF16)
    sb("ROWS", (128, 2064), F32)
    sb("ROW2", (128, 4), F32)
    sb("SCFX", (128, 1024), F32)
    sb("VEC", (128, 48), F32)
    sb("VECB", (128, 32), F32)
    sb("VEC2", (128, 32), F32)
    sb("EYEF", (128, 128), F32)
    sb("EYEB", (128, 128), BF16)
    sb("ONESF128", (128, 128), F32)
    sb("ONESB128", (128, 128), BF16)
    sb("ONES1", (128, 128), F32)
    sb("FLG", (128, 2), F32)
    sb("REFQ", (128, 17), F32)
    sb("EKER", (64, 400), F32)
    sb("PKF", (64, 576), BF16)
    sb("BDC", (128, 64), BF16)
    sb("V14", (16, 1024), BF16)
    sb("BDV", (128, 1024), BF16)
    sb("WOUT", (64, 2048), BF16)

    t["PSA"] = nc.alloc_psum_tensor("PSA", [128, 2048], F32)
    t["PSB"] = nc.alloc_psum_tensor("PSB", [128, 512], F32)
    t["PSC"] = nc.alloc_psum_tensor("PSC", [128, 512], F32)
    t["PST"] = nc.alloc_psum_tensor("PST", [128, 512], BF16)
    t["PST2"] = nc.alloc_psum_tensor("PST2", [128, 512], BF16)
    assert nc.psum_base <= 8, nc.psum_base

    stages = [(0, emit_p0), (1, emit_p1), (2, emit_p2), (3, emit_p3),
              (4, emit_p4a), (4, emit_p4b), (5, emit_p5), (6, emit_p6), (7, emit_p7)]
    for sn, fn in stages:
        if sn <= STAGE_N:
            p = Phase(cx, fn.__name__)
            fn(cx, p)
            p.run()

    if DBG_TAP:
        emit_tap(cx)
    p = Phase(cx, "pf")
    emit_pf(cx, p)
    p.run()
    nc.finalize()
    return nc



def emit_tap(cx):
    nc, t = cx.nc, cx.t
    spec = DBG_TAP.split(",")
    name = spec[0]
    th = t[name]
    shape = list(th.shape)
    c0, c1 = (int(spec[1]), int(spec[2])) if len(spec) >= 3 else (0, shape[1])
    p0, p1 = (int(spec[3]), int(spec[4])) if len(spec) >= 5 else (0, shape[0])
    dt = th.dtype
    t["dbg"] = nc.dram_tensor("dbg", [p1 - p0, c1 - c0], dt, kind="ExternalOutput")
    p = Phase(cx, "ptap")
    p.addi("sync", lambda h: h.dma_start(t["dbg"][:], th[p0:p1, c0:c1]), cx.sd, 16)
    p.wait("sync", cx.sd)
    p.run()



def ln_rep(cx, ph, srcs, wn, bf16_src=False, rho=False):
    t = cx.t
    pe, dv, sc = cx.pe, cx.dv, cx.sc
    ph.wait("scalar", pe)
    ph.wait("scalar", dv)
    for c in range(8):
        b = lambda h, c=c: h.activation(
            t["SCB1"][:, c * wn: (c + 1) * wn], srcs[c], AF.Square)
        (ph.addi("scalar", b, sc) if c == 7 else ph.add("scalar", b))
    ph.wait("tensor", sc)
    ph.wait("tensor", dv)
    pm = t["PSB"][:, 0:wn]
    pq = t["PSC"][:, 0:wn]
    lhs_m = t["ONESB128"][:] if bf16_src else t["ONESF128"][:]
    for c in range(8):
        ph.add("tensor", lambda h, c=c, lhs_m=lhs_m: h.matmul(
            pm, lhs_m, srcs[c], start=(c == 0), stop=(c == 7)))
    for c in range(8):
        b = lambda h, c=c: h.matmul(
            pq, t["ONESB128"][:], t["SCB1"][:, c * wn: (c + 1) * wn],
            start=(c == 0), stop=(c == 7))
        (ph.addi("tensor", b, pe) if c == 7 else ph.add("tensor", b))
    ph.wait("vector", pe)
    mb = t["SCB2"][:, 2048:2048 + wn]
    var = t["SCFX"][:, 0:wn]
    rstd = t["SCB2"][:, 2560:2560 + wn]
    ph.add("vector", lambda h: h.tensor_copy(mb, pm))
    ph.add("vector", lambda h: h.tensor_tensor(var, mb, mb, ALU.mult))
    ph.add("vector", lambda h: h.tensor_tensor(var, pq, var, ALU.subtract))
    ph.addi("vector", lambda h: h.tensor_scalar(var, var, EPS, None, ALU.add), dv)
    ph.wait("scalar", dv)
    ph.add("scalar", lambda h: h.activation(t["SCFX"][:, 512:512 + wn], var, AF.Ln))
    ph.addi("scalar", lambda h: h.activation(
        rstd, t["SCFX"][:, 512:512 + wn], AF.Exp, scale=-0.5), sc)
    out = dict(m_rep=pm, mb=mb, rstd=rstd)
    if rho:
        vq = t["SCFX"][:, 512:512 + wn]
        rr = t["SCB2"][:, 3072:3072 + wn]
        rhot = t["SCB2"][:, 3584:3584 + wn]
        ph.wait("vector", sc)
        ph.add("vector", lambda h: h.tensor_tensor(vq, rstd, rstd, ALU.mult))
        ph.add("vector", lambda h: h.tensor_tensor(vq, var, vq, ALU.mult))
        ph.addi("vector", lambda h: h.tensor_scalar(vq, vq, EPS, None, ALU.add), dv)
        ph.wait("scalar", dv)
        ph.add("scalar", lambda h: h.activation(vq, vq, AF.Ln))
        ph.addi("scalar", lambda h: h.activation(rr, vq, AF.Exp, scale=-0.5), sc)
        ph.wait("vector", sc)
        ph.addi("vector", lambda h: h.tensor_tensor(rhot, rstd, rr, ALU.mult), dv)
        out["rho"] = rhot
    return out


def emit_plain_ln(cx, ph, store_rows=None):
    t = cx.t
    pe, dv, sc, sd, gd = cx.pe, cx.dv, cx.sc, cx.sd, cx.gd
    for w0, wn in W512:
        r = ln_rep(cx, ph, [cx.res(w0, wn, c) for c in range(8)], wn,
                   rho=bool(store_rows and "rho" in store_rows))
        ph.wait("vector", sc)
        for c in range(8):
            ph.add("vector", lambda h, c=c, r=r, w0=w0, wn=wn: h.tensor_tensor(
                t["SCB1"][:, 8448 + c * wn: 8448 + (c + 1) * wn],
                cx.res(w0, wn, c), r["m_rep"], ALU.subtract))
        for c in range(8):
            b = lambda h, c=c, r=r, w0=w0, wn=wn: h.tensor_tensor(
                cx.xng(w0, wn, c), t["SCB1"][:, 8448 + c * wn: 8448 + (c + 1) * wn],
                r["rstd"], ALU.mult)
            (ph.addi("vector", b, dv) if c == 7 else ph.add("vector", b))
        if store_rows:
            ph.wait("gpsimd", dv)
            ph.wait("gpsimd", sc)
            for q, part in store_rows.items():
                src = {"m": r["mb"], "rstd": r["rstd"], "rho": r.get("rho")}[q]
                ap = bass.AP(src.tensor, src.offset, [[src.ap[0][0], 1], [1, wn]])
                dst = cx.row("ROWS", part, w0, wn)
                ph.addi("gpsimd", lambda h, d=dst, s2=ap: h.dma_start(d, s2), gd, 16)
            ph.wait("scalar", gd)
            ph.wait("vector", gd)



def emit_p0(cx, ph):
    t = cx.t
    sd, gd, dv, sc, gp = cx.sd, cx.gd, cx.dv, cx.sc, cx.gp

    def s_dma(dst, src):
        ph.addi("sync", lambda h, d=dst, s=src: h.dma_start(d, s), sd, 16)

    def g_dma(dst, src):
        ph.addi("gpsimd", lambda h, d=dst, s=src: h.dma_start(d, s), gd, 16)

    s_dma(t["EKER"][:, 256:257], t["log_decay"][:])
    s_dma(t["EKER"][:, 257:258], t["frequency"][:])
    cx.n_eker_dma = sd.n
    s_dma(t["RES"][:].rearrange("p (c n) -> p c n", c=8), t["xmain"][:])
    s_dma(t["FLG"][:], t["flags"][:])
    s_dma(t["REFQ"][:], t["refq"][:])
    s_dma(t["EYEF"][:], t["eyef"][:])
    s_dma(t["ONESF128"][:], t["onesS"][:])
    s_dma(t["ONES1"][:], t["ones1"][:])
    s_dma(t["VEC"][:, 0:8], t["ln1_g"][:])
    s_dma(t["VEC"][:, 8:16], t["ln1_b"][:])
    s_dma(t["VEC"][:, 16:24], t["ln2_g"][:])
    s_dma(t["VEC"][:, 24:32], t["ln2_b"][:])
    s_dma(t["VEC"][:, 32:40], t["b_gate"][:])
    s_dma(t["VEC"][:, 40:48], t["b_mlp2"][:])
    s_dma(t["VECB"][:, 0:32], t["b_mlp1"][:])
    s_dma(t["VEC2"][:, 0:8], t["b_value"][:])
    s_dma(t["VEC2"][:, 8:16], t["b_out1"][:])
    s_dma(t["VEC2"][:, 16:24], t["b_proj"][:])
    s_dma(t["VEC2"][:, 24:25], t["b_off"][:])
    s_dma(bass.AP(t["VEC2"], 25, [[32, 64], [1, 1]]), t["b_attn"][:])

    g_dma(t["EYEB"][:], t["eyeb"][:])
    g_dma(t["ONESB128"][:], t["onesSb"][:])
    g_dma(t["BDC"][:], t["bdcoupT"][:])
    g_dma(cx.wr(0, 1024), t["w_inT"][:].rearrange("p c m -> p (c m)"))
    for c in range(8):
        g_dma(cx.wr(1 + c, 1024), t["w_gateT"][:, c, :])
    g_dma(t["WOUT"][:], t["w_outT"][:])
    ph.addi("gpsimd", lambda h: h.memset(t["PKF"][:], 0.0), gp)

    ek = t["EKER"]
    ph.wait("scalar", sd, cx.n_eker_dma * 1)
    ph.addi("scalar", lambda h: h.activation(ek[:, 258:259], ek[:, 256:257], AF.Sigmoid), sc)
    ph.wait("vector", sd, cx.n_eker_dma * 1)
    ph.add("vector", lambda h: h.tensor_scalar(ek[:, 259:260], ek[:, 257:258], 1.0 / (2 * PI), None, ALU.mult))
    ph.add("vector", lambda h: h.tensor_scalar(ek[:, 260:261], ek[:, 259:260], C23, C23, ALU.add, ALU.subtract))
    ph.add("vector", lambda h: h.scalar_tensor_tensor(ek[:, 261:262], ek[:, 260:261], -1.0, ek[:, 259:260], ALU.mult, ALU.add))
    ph.add("vector", lambda h: h.tensor_scalar(ek[:, 261:262], ek[:, 261:262], 2 * PI, None, ALU.mult))
    ph.add("vector", lambda h: h.tensor_scalar(ek[:, 262:263], ek[:, 259:260], 1.0, 0.25, ALU.mult, ALU.add))
    ph.add("vector", lambda h: h.tensor_scalar(ek[:, 263:264], ek[:, 262:263], C23, C23, ALU.add, ALU.subtract))
    ph.add("vector", lambda h: h.scalar_tensor_tensor(ek[:, 264:265], ek[:, 263:264], -1.0, ek[:, 262:263], ALU.mult, ALU.add))
    ph.addi("vector", lambda h: h.tensor_scalar(ek[:, 264:265], ek[:, 264:265], 2 * PI, None, ALU.mult), dv)
    ph.wait("scalar", dv)
    ph.add("scalar", lambda h: h.activation(ek[:, 265:266], ek[:, 261:262], AF.Sin))
    ph.addi("scalar", lambda h: h.activation(ek[:, 266:267], ek[:, 264:265], AF.Sin), sc)
    ph.wait("vector", sc)
    ph.add("vector", lambda h: h.tensor_tensor(ek[:, 267:268], ek[:, 258:259], ek[:, 266:267], ALU.mult))
    ph.add("vector", lambda h: h.tensor_tensor(ek[:, 268:269], ek[:, 258:259], ek[:, 265:266], ALU.mult))
    ph.add("vector", lambda h: h.memset(ek[:, 0:1], 1.0))
    ph.add("vector", lambda h: h.memset(ek[:, 64:65], 0.0))
    ph.add("vector", lambda h: h.tensor_copy(ek[:, 1:2], ek[:, 267:268]))
    ph.add("vector", lambda h: h.tensor_copy(ek[:, 65:66], ek[:, 268:269]))
    q = 1
    while q < 64:
        n = min(q, 64 - q)
        ph.add("vector", lambda h, q=q: h.tensor_tensor(
            ek[:, 396:397], ek[:, q - 1:q], ek[:, 267:268], ALU.mult))
        ph.add("vector", lambda h, q=q: h.tensor_tensor(
            ek[:, 397:398], ek[:, 64 + q - 1:64 + q], ek[:, 268:269], ALU.mult))
        ph.add("vector", lambda h: h.tensor_tensor(
            ek[:, 398:399], ek[:, 396:397], ek[:, 397:398], ALU.subtract))
        ph.add("vector", lambda h, q=q: h.tensor_tensor(
            ek[:, 396:397], ek[:, q - 1:q], ek[:, 268:269], ALU.mult))
        ph.add("vector", lambda h, q=q: h.tensor_tensor(
            ek[:, 397:398], ek[:, 64 + q - 1:64 + q], ek[:, 267:268], ALU.mult))
        ph.add("vector", lambda h: h.tensor_tensor(
            ek[:, 399:400], ek[:, 396:397], ek[:, 397:398], ALU.add))
        ph.add("vector", lambda h, n=n: h.tensor_scalar(
            ek[:, 270:270 + n], ek[:, 0:n], ek[:, 398:399], None, ALU.mult))
        ph.add("vector", lambda h, n=n: h.tensor_scalar(
            ek[:, 334:334 + n], ek[:, 64:64 + n], ek[:, 399:400], None, ALU.mult))
        ph.add("vector", lambda h, n=n: h.tensor_scalar(
            ek[:, 334:334 + n], ek[:, 334:334 + n], -1.0, None, ALU.mult))
        ph.add("vector", lambda h, n=n: h.tensor_scalar(
            ek[:, 302:302 + n], ek[:, 0:n], ek[:, 399:400], None, ALU.mult))
        ph.add("vector", lambda h, n=n: h.tensor_scalar(
            ek[:, 366:366 + n], ek[:, 64:64 + n], ek[:, 398:399], None, ALU.mult))
        ph.add("vector", lambda h, q=q, n=n: h.tensor_tensor(
            ek[:, q:q + n], ek[:, 270:270 + n], ek[:, 334:334 + n], ALU.add))
        ph.add("vector", lambda h, q=q, n=n: h.tensor_tensor(
            ek[:, 64 + q:64 + q + n], ek[:, 302:302 + n], ek[:, 366:366 + n], ALU.add))
        q *= 2
    ph.wait("vector", gp)
    rev_r = bass.AP(ek.handle if hasattr(ek, "handle") else t["EKER"], 63, [[400, 64], [-1, 64]])
    rev_i = bass.AP(t["EKER"], 64 + 63, [[400, 64], [-1, 64]])
    ph.add("vector", lambda h: h.tensor_copy(t["PKF"][:, 0:64], rev_r))
    ph.add("vector", lambda h: h.tensor_copy(t["PKF"][:, 192:192 + 64], rev_i))
    ph.addi("vector", lambda h: h.tensor_scalar(
        t["PKF"][:, 384:384 + 64], rev_i, -1.0, None, ALU.mult), dv)
    cx.n_pkf_done = dv.n



def emit_p1(cx, ph):
    t = cx.t
    ph.wait("scalar", cx.sd)
    ph.wait("tensor", cx.sd)
    ph.wait("tensor", cx.gd)
    for w0, wn in W512:
        r = ln_rep(cx, ph, [cx.res(w0, wn, c) for c in range(8)], wn, rho=True)
        ph.wait("gpsimd", cx.dv)
        ph.wait("gpsimd", cx.sc)
        for q, part in (("m", 0), ("rstd", 32), ("rho", 64)):
            src = {"m": r["mb"], "rstd": r["rstd"], "rho": r["rho"]}[q]
            ap = bass.AP(src.tensor, src.offset, [[src.ap[0][0], 1], [1, wn]])
            dst = cx.row("ROWS", part, w0, wn)
            ph.addi("gpsimd", lambda h, d=dst, s2=ap: h.dma_start(d, s2), cx.gd, 16)
        ph.wait("scalar", cx.gd)
        ph.wait("vector", cx.gd)



def emit_p2(cx, ph):
    t = cx.t
    pe, dv, sc, sd, gd = cx.pe, cx.dv, cx.sc, cx.sd, cx.gd
    ph.wait("tensor", gd)

    def beta_mm(dst_ap, wn):
        pb = t["PSC"][:, 0:wn]
        ph.wait("tensor", dv)
        for c in range(8):
            b = lambda h, c=c, pb=pb, wn=wn: h.matmul(
                pb, cx.wr(0)[:, c * 128:(c + 1) * 128],
                t["SCB2"][:, c * wn:(c + 1) * wn], start=(c == 0), stop=(c == 7))
            (ph.addi("tensor", b, pe) if c == 7 else ph.add("tensor", b))
        ph.wait("vector", pe)
        ph.addi("vector", lambda h, pb=pb, d=dst_ap: h.tensor_copy(d, pb), dv)

    def gate_mm(w0, wn):
        for half in range(2):
            if half == 0:
                ph.wait("tensor", dv)
            ph.wait("tensor", sc)
            for mm in range(4):
                m = half * 4 + mm
                for c in range(8):
                    b = lambda h, m=m, mm=mm, c=c, wn=wn: h.matmul(
                        t["PSA"][:, mm * 512: mm * 512 + wn],
                        cx.wr(1 + c)[:, m * 128:(m + 1) * 128],
                        t["SCB2"][:, c * wn:(c + 1) * wn], start=(c == 0), stop=(c == 7))
                    (ph.addi("tensor", b, pe) if (mm == 3 and c == 7) else ph.add("tensor", b))
            ph.wait("scalar", pe)
            for mm in range(4):
                m = half * 4 + mm
                b = lambda h, m=m, mm=mm, w0=w0, wn=wn: h.activation(
                    cx.xng(w0, wn, m), t["PSA"][:, mm * 512: mm * 512 + wn],
                    AF.Sigmoid, bias=t["VEC"][:, 32 + m: 33 + m])
                (ph.addi("scalar", b, sc) if mm == 3 else ph.add("scalar", b))

    for w0, wn in W512:
        ph.wait("tensor", dv)
        pm = t["PSB"][:, 0:wn]
        pb = t["PSC"][:, 0:wn]
        ph.add("tensor", lambda h, pm=pm, w0=w0, wn=wn: h.matmul(
            pm, t["ONES1"][0:1, :], cx.row("ROWS", 0, w0, wn), start=True, stop=True))
        ph.addi("tensor", lambda h, pb=pb, w0=w0, wn=wn: h.matmul(
            pb, t["ONES1"][64:65, :], cx.row("ROWS", 64, w0, wn), start=True, stop=True,
            tile_position=(64, 0)), pe)
        ph.wait("vector", pe)
        for c in range(8):
            ph.add("vector", lambda h, c=c, pm=pm, w0=w0, wn=wn: h.tensor_tensor(
                t["SCB2"][:, c * wn: c * wn + wn], cx.res(w0, wn, c), pm, ALU.subtract))
        for c in range(8):
            ph.add("vector", lambda h, c=c, pb=pb, wn=wn: h.tensor_tensor(
                t["SCB2"][:, c * wn: c * wn + wn], t["SCB2"][:, c * wn: c * wn + wn], pb, ALU.mult))
        for c in range(8):
            b = lambda h, c=c, wn=wn: h.tensor_scalar(
                t["SCB2"][:, c * wn: (c + 1) * wn], t["SCB2"][:, c * wn: (c + 1) * wn],
                t["VEC"][:, 0 + c: 1 + c], t["VEC"][:, 8 + c: 9 + c], ALU.mult, ALU.add)
            (ph.addi("vector", b, dv) if c == 7 else ph.add("vector", b))
        if w0 < TC:
            beta_mm(cx.beta_main(w0, wn), wn)
        gate_mm(w0, wn)

    for wi in range(8):
        w0 = wi * 256
        ph.wait("sync", sc)
        ph.wait("sync", dv)
        ph.addi("sync", lambda h, w0=w0: h.dma_start(
            t["SCF"][:, 0:2048].rearrange("p (c n) -> p c n", c=8),
            t["xpre"][:, :, w0:w0 + 256]), sd, 16)
        ph.wait("scalar", sd)
        ph.wait("vector", sd)
        ph.wait("tensor", sd)
        srcs = [t["SCF"][:, c * 256:(c + 1) * 256] for c in range(8)]
        r = ln_rep(cx, ph, srcs, 256, rho=True)
        ph.wait("vector", sc)
        for c in range(8):
            ph.add("vector", lambda h, c=c, r=r: h.tensor_tensor(
                t["SCB2"][:, c * 256: (c + 1) * 256], srcs[c], r["m_rep"], ALU.subtract))
        for c in range(8):
            ph.add("vector", lambda h, c=c, r=r: h.tensor_tensor(
                t["SCB2"][:, c * 256: (c + 1) * 256], t["SCB2"][:, c * 256: (c + 1) * 256],
                r["rho"], ALU.mult))
        for c in range(8):
            b = lambda h, c=c: h.tensor_scalar(
                t["SCB2"][:, c * 256: (c + 1) * 256], t["SCB2"][:, c * 256: (c + 1) * 256],
                t["VEC"][:, 0 + c: 1 + c], t["VEC"][:, 8 + c: 9 + c], ALU.mult, ALU.add)
            (ph.addi("vector", b, dv) if c == 7 else ph.add("vector", b))
        beta_mm(cx.beta_pre(w0, 256), 256)



def emit_p3(cx, ph):
    t = cx.t
    pe, dv, sc, sd, gd, gp = cx.pe, cx.dv, cx.sc, cx.sd, cx.gd, cx.gp

    evac_n = []
    for n in range(64):
        bank = "PST" if n % 2 == 0 else "PST2"
        pst = t[bank][0:64, 0:128]
        if n >= 2:
            ph.wait("tensor", dv, evac_n[n - 2])
        ph.addi("tensor", lambda h, n=n, pst=pst: h.transpose(
            pst, cx.beta_chunk(n), t["EYEB"][:]), pe)
        ph.wait("vector", pe)
        evac_n.append(ph.addi("vector", lambda h, n=n, pst=pst: h.tensor_copy(
            t["SCB1"][0:64, n * 128:(n + 1) * 128], pst), dv))
    bett_done = cx.dv.n

    ph.wait("gpsimd", cx.dv, cx.n_pkf_done)

    def toep_dma(grp):
        base = 8192 + (grp % 2) * 3072
        for st in range(3):
            for kk in range(16):
                k = grp * 16 + kk
                src = bass.AP(t["PKF"], k * 576 + st * 192 + 63, [[576, 1], [-1, 64], [1, 64]])
                dst = bass.AP(t["SCB1"], base + st * 1024 + kk * 64, [[16640, 64], [1, 64]])
                ph.addi("gpsimd", lambda h, s2=src, d=dst: h.dma_start(d, s2), gd, 16)
        return cx.gd.n

    grp_dma_done = {0: toep_dma(0), 1: toep_dma(1)}

    intra_r_ap = lambda k: bass.AP(t["WR"], k * 64, [[14336, 64], [1, 64]])
    intra_i_ap = lambda k: bass.AP(t["WR"], 4096 + k * 64, [[14336, 64], [1, 64]])
    ph.wait("tensor", dv, bett_done)
    evac_k = []
    for grp in range(4):
        base = 8192 + (grp % 2) * 3072
        ph.wait("tensor", gd, grp_dma_done[grp])
        for kk in range(16):
            k = grp * 16 + kk
            lr = t["SCB1"][0:64, base + kk * 64: base + kk * 64 + 64]
            li = t["SCB1"][0:64, base + 1024 + kk * 64: base + 1024 + kk * 64 + 64]
            nli = t["SCB1"][0:64, base + 2048 + kk * 64: base + 2048 + kk * 64 + 64]
            br = bass.AP(t["SCB1"], k, [[16640, 64], [128, 64]])
            bi = bass.AP(t["SCB1"], 64 + k, [[16640, 64], [128, 64]])
            bb = 2 * (k % 2)
            ppr = t["PSA"][0:64, bb * 512: bb * 512 + 64]
            ppi = t["PSA"][0:64, (bb + 1) * 512: (bb + 1) * 512 + 64]
            if k >= 2:
                ph.wait("tensor", dv, evac_k[k - 2])
            ph.add("tensor", lambda h, a=ppr, l=lr, r=br: h.matmul(a, l, r, start=True, stop=False))
            ph.add("tensor", lambda h, a=ppr, l=nli, r=bi: h.matmul(a, l, r, start=False, stop=True))
            ph.add("tensor", lambda h, a=ppi, l=lr, r=bi: h.matmul(a, l, r, start=True, stop=False))
            ph.addi("tensor", lambda h, a=ppi, l=li, r=br: h.matmul(a, l, r, start=False, stop=True), pe)
            ph.wait("vector", pe)
            ph.add("vector", lambda h, k=k, a=ppr: h.tensor_copy(intra_r_ap(k), a))
            evac_k.append(ph.addi("vector", lambda h, k=k, a=ppi: h.tensor_copy(intra_i_ap(k), a), dv))
        if grp + 2 <= 3:
            ph.wait("gpsimd", pe, cx.pe.n)
            grp_dma_done[grp + 2] = toep_dma(grp + 2)

    ph.wait("tensor", dv)
    copy_done = []
    for n in range(64):
        bank = "PST" if n % 2 == 0 else "PST2"
        pt_ = t[bank][:, 0:64]
        in_ri = bass.AP(t["WR"], n, [[14336, 64], [64, 128]])
        if n >= 2:
            ph.wait("tensor", dv, copy_done[n - 2])
        ph.addi("tensor", lambda h, o=pt_, i=in_ri: h.transpose(
            o, i, t["EYEB"][0:64, 0:64]), pe)
        ph.wait("vector", pe)
        crn = t["CRCI"][0:64, n * 64:(n + 1) * 64]
        cin = t["CRCI"][64:128, n * 64:(n + 1) * 64]
        ph.add("vector", lambda h, o=crn, p=pt_: h.tensor_copy(o, p[0:64, :]))
        copy_done.append(ph.addi("vector", lambda h, o=cin, p=pt_: h.tensor_copy(
            o, p[64:128, :]), dv))
        if n == 0:
            continue
        sr = t["CRCI"][0:64, n * 64 - 1: n * 64]
        si = t["CRCI"][64:128, n * 64 - 1: n * 64]
        c0r = t["CRCI"][0:64, n * 64: n * 64 + 1]
        c0i = t["CRCI"][64:128, n * 64: n * 64 + 1]
        if n == 32:
            ph.add("vector", lambda h, a=sr, o=c0r: h.scalar_tensor_tensor(
                o, a, t["FLG"][0:64, 0:1], o, ALU.mult, ALU.add))
            ph.add("vector", lambda h, a=si, o=c0i: h.scalar_tensor_tensor(
                o, a, t["FLG"][64:128, 0:1], o, ALU.mult, ALU.add))
        else:
            ph.add("vector", lambda h, a=sr, o=c0r: h.tensor_tensor(o, o, a, ALU.add))
            ph.add("vector", lambda h, a=si, o=c0i: h.tensor_tensor(o, o, a, ALU.add))

    ph.add("vector", lambda h: h.tensor_scalar(
        t["CRCI"][:, 4096:4112], t["CRCI"][:, 0:16], t["FLG"][:, 0:1], None, ALU.mult))
    ph.addi("vector", lambda h: h.scalar_tensor_tensor(
        t["CRCI"][:, 4096:4112], t["CRCI"][:, 2048:2064], t["FLG"][:, 1:2],
        t["CRCI"][:, 4096:4112], ALU.mult, ALU.add), dv)


def emit_p4a(cx, ph):
    t = cx.t
    pe, dv, sc, sd, gd = cx.pe, cx.dv, cx.sc, cx.sd, cx.gd
    ph.wait("tensor", gd)

    for w0, wn in W256:
        ccol = (2048 + w0) if w0 < TC else 4096
        ph.wait("tensor", dv)
        pr = t["PSB"][0:64, 0:wn]
        pi = t["PSC"][0:64, 0:wn]
        ph.add("tensor", lambda h, p=pr, c=ccol, wn=wn: h.matmul(
            p, t["BDC"][0:64, :], t["CRCI"][0:64, c:c + wn], start=True, stop=True))
        ph.addi("tensor", lambda h, p=pi, c=ccol, wn=wn: h.matmul(
            p, t["BDC"][64:128, :], t["CRCI"][64:128, c:c + wn], start=True, stop=True), pe)
        ph.wait("vector", pe)
        ph.add("vector", lambda h, p=pr, wn=wn: h.tensor_copy(t["CC"][0:64, 0:wn], p))
        ph.addi("vector", lambda h, p=pi, wn=wn: h.tensor_copy(
            t["CC"][0:64, 256:256 + wn], p), dv)
        ph.wait("tensor", dv)
        pm = t["PSB"][:, 0:wn]
        ps_ = t["PSC"][:, 0:wn]
        ph.add("tensor", lambda h, p=pm, w0=w0, wn=wn: h.matmul(
            p, t["ONES1"][0:1, :], cx.row("ROWS", 0, w0, wn), start=True, stop=True))
        ph.addi("tensor", lambda h, p=ps_, w0=w0, wn=wn: h.matmul(
            p, t["ONES1"][32:33, :], cx.row("ROWS", 32, w0, wn), start=True, stop=True), pe)
        ph.wait("vector", pe)
        for c in range(8):
            ph.add("vector", lambda h, c=c, p=pm, w0=w0, wn=wn: h.tensor_tensor(
                t["SCF"][:, c * 256: c * 256 + wn], cx.res(w0, wn, c), p, ALU.subtract))
        for c in range(8):
            b = lambda h, c=c, p=ps_, wn=wn: h.tensor_tensor(
                t["SCF"][:, c * 256: c * 256 + wn], t["SCF"][:, c * 256: c * 256 + wn], p, ALU.mult)
            (ph.addi("vector", b, dv) if c == 7 else ph.add("vector", b))
        for half in range(2):
            ph.wait("tensor", dv)
            for mm in range(4):
                m = half * 4 + mm
                pa = t["PSA"][:, mm * 512: mm * 512 + wn]
                ph.add("tensor", lambda h, p=pa, m=m, wn=wn: h.matmul(
                    p, t["WOUT"][:, m * 128: (m + 1) * 128], t["CC"][0:64, 0:wn],
                    start=True, stop=False))
                b = lambda h, p=pa, m=m, wn=wn: h.matmul(
                    p, t["WOUT"][:, 1024 + m * 128: 1024 + (m + 1) * 128],
                    t["CC"][0:64, 256:256 + wn], start=False, stop=True)
                (ph.addi("tensor", b, pe) if mm == 3 else ph.add("tensor", b))
            ph.wait("vector", pe)
            for mm in range(4):
                m = half * 4 + mm
                pa = t["PSA"][:, mm * 512: mm * 512 + wn]
                ph.add("vector", lambda h, p=pa, m=m, w0=w0, wn=wn: h.tensor_tensor(
                    t["CC"][:, 0:wn] if False else t["SCB2"][:, 2048:2048 + wn],
                    cx.xng(w0, wn, m), p, ALU.mult))
                b = lambda h, m=m, wn=wn: h.tensor_tensor(
                    t["SCF"][:, m * 256: m * 256 + wn], t["SCF"][:, m * 256: m * 256 + wn],
                    t["SCB2"][:, 2048:2048 + wn], ALU.add)
                (ph.addi("vector", b, dv) if mm == 3 else ph.add("vector", b))
        resw = bass.AP(t["RES"], w0, [[8 * TT, 128], [TT, 8], [1, wn]])
        scfw = t["SCF"][:, 0:2048].rearrange("p (c n) -> p c n", c=8)[:, :, 0:wn]
        ph.add("vector", lambda h, r=resw, s2=scfw: h.tensor_tensor(r, r, s2, ALU.add))
        for c in range(8):
            b = lambda h, c=c, wn=wn: h.tensor_copy(
                t["SCB2"][:, c * wn: (c + 1) * wn], t["SCF"][:, c * 256: c * 256 + wn])
            (ph.addi("vector", b, dv) if c == 7 else ph.add("vector", b))
        x1b = [t["SCB2"][:, c * wn: (c + 1) * wn] for c in range(8)]
        r2 = ln_rep(cx, ph, x1b, wn, bf16_src=True)
        ph.wait("vector", cx.sc)
        for c in range(8):
            ph.add("vector", lambda h, c=c, r2=r2, x1b=x1b, wn=wn: h.tensor_tensor(
                t["SCB1"][:, 8448 + c * wn: 8448 + (c + 1) * wn], x1b[c],
                r2["m_rep"], ALU.subtract))
        for c in range(8):
            ph.add("vector", lambda h, c=c, r2=r2, wn=wn: h.tensor_tensor(
                t["SCB1"][:, 8448 + c * wn: 8448 + (c + 1) * wn],
                t["SCB1"][:, 8448 + c * wn: 8448 + (c + 1) * wn], r2["rstd"], ALU.mult))
        for c in range(8):
            b = lambda h, c=c, w0=w0, wn=wn: h.tensor_scalar(
                cx.xng(w0, wn, c), t["SCB1"][:, 8448 + c * wn: 8448 + (c + 1) * wn],
                t["VEC"][:, 16 + c: 17 + c], t["VEC"][:, 24 + c: 25 + c], ALU.mult, ALU.add)
            (ph.addi("vector", b, dv) if c == 7 else ph.add("vector", b))


W512M = [(0, 512), (512, 512), (1024, 512), (1536, 512), (2048, 16)]


def emit_p4b(cx, ph):
    t = cx.t
    pe, dv, sc, sd, gd = cx.pe, cx.dv, cx.sc, cx.sd, cx.gd

    def gdma(slot, src):
        return ph.addi("gpsimd", lambda h, d=cx.wr(slot), s2=src: h.dma_start(d, s2), gd, 16)

    for g in range(4):
        ph.wait("gpsimd", pe)
        dma_done = []
        for mi in range(8):
            m = g * 8 + mi
            dma_done.append(gdma(mi, t["w_mlp1T"][m].rearrange("p c m -> p (c m)")))
        hid_done = []
        gidx = 0
        for mi in range(8):
            m = g * 8 + mi
            ph.wait("tensor", gd, dma_done[mi])
            for wi, (w0, wn) in enumerate(W512M):
                pa = t["PSA"][:, (gidx % 4) * 512: (gidx % 4) * 512 + wn]
                gidx += 1
                if len(hid_done) >= 4:
                    ph.wait("tensor", sc, hid_done[-4])
                for c in range(8):
                    b = lambda h, p=pa, mi=mi, c=c, w0=w0, wn=wn: h.matmul(
                        p, cx.wr(mi)[:, c * 128: (c + 1) * 128],
                        cx.xng(w0, wn, c), start=(c == 0), stop=(c == 7))
                    (ph.addi("tensor", b, pe) if c == 7 else ph.add("tensor", b))
                ph.wait("scalar", pe)
                hid_done.append(ph.addi("scalar", lambda h, p=pa, mi=mi, m=m, w0=w0, wn=wn: h.activation(
                    t["SCB1"][:, mi * 2080 + w0: mi * 2080 + w0 + wn], p, AF.Silu,
                    bias=t["VECB"][:, m: m + 1]), sc))
        dma2 = []
        ph.wait("gpsimd", sc, hid_done[-1])
        for mi in range(8):
            kc = g * 8 + mi
            dma2.append(gdma(mi, t["w_mlp2T"][kc]))
        ph.wait("tensor", gd, dma2[-1])
        ph.wait("tensor", sc, hid_done[-1])
        for w0, wn in W512M:
            for mo in range(8):
                bank = "PSB" if mo % 2 == 0 else "PSC"
                pz = t[bank][:, 0:wn]
                ph.wait("tensor", dv)
                for mi in range(8):
                    b = lambda h, p=pz, mi=mi, mo=mo, w0=w0, wn=wn: h.matmul(
                        p, cx.wr(mi)[:, mo * 128: (mo + 1) * 128],
                        t["SCB1"][:, mi * 2080 + w0: mi * 2080 + w0 + wn],
                        start=(mi == 0), stop=(mi == 7))
                    (ph.addi("tensor", b, pe) if mi == 7 else ph.add("tensor", b))
                ph.wait("vector", pe)
                resw = t["RES"][:, mo * TT + w0: mo * TT + w0 + wn]
                if g == 0:
                    b = lambda h, p=pz, r=resw, mo=mo: h.scalar_tensor_tensor(
                        r, p, t["VEC"][:, 40 + mo: 41 + mo], r, ALU.add, ALU.add)
                else:
                    b = lambda h, p=pz, r=resw: h.tensor_tensor(r, r, p, ALU.add)
                ph.addi("vector", b, dv)


def emit_p5(cx, ph):
    t = cx.t
    pe, dv, sc, sd, gd = cx.pe, cx.dv, cx.sc, cx.sd, cx.gd

    emit_plain_ln(cx, ph)

    ph.wait("gpsimd", pe)
    wg_done = ph.addi("gpsimd", lambda h: h.dma_start(
        cx.wr(13, 128), t["WgT"][:].rearrange("p c m -> p (c m)")), gd, 16)
    ph.wait("tensor", gd, wg_done)
    for wi, (w0, wn) in enumerate(W512):
        ph.wait("tensor", dv)
        psc = t["PSC"][0:16, 0:wn]
        for c in range(8):
            b = lambda h, c=c, p=psc, w0=w0, wn=wn: h.matmul(
                p, cx.wr(13, 128)[:, c * 16:(c + 1) * 16],
                cx.xng(w0, wn, c), start=(c == 0), stop=(c == 7))
            (ph.addi("tensor", b, pe) if c == 7 else ph.add("tensor", b))
        ph.wait("vector", pe)
        ph.addi("vector", lambda h, p=psc, wn=wn: h.tensor_copy(t["SCF"][0:16, 0:wn], p), dv)
        ntile = (wn + 127) // 128
        for tt_ in range(ntile):
            tn = min(128, wn - tt_ * 128)
            ph.wait("tensor", dv)
            psb = t["PSB"][0:tn, 0:16]
            ph.addi("tensor", lambda h, p=psb, tt_=tt_, tn=tn: h.transpose(
                p, t["SCF"][0:16, tt_ * 128: tt_ * 128 + tn], t["EYEF"][0:16, 0:16]), pe)
            ph.wait("vector", pe)
            TOK = t["SCF"][0:tn, 512:528]
            CW = t["SCF"][0:tn, 528:544]
            R1 = t["SCF"][0:tn, 544:545]
            R2 = t["SCF"][0:tn, 545:546]
            ph.add("vector", lambda h, p=psb, TOK=TOK: h.tensor_copy(TOK, p))
            ph.add("vector", lambda h, TOK=TOK, R1=R1: h.reduce_max(R1, TOK, AX.X))
            ph.addi("vector", lambda h, TOK=TOK, CW=CW, R1=R1: h.tensor_scalar(
                CW, TOK, R1, None, ALU.subtract), dv)
            ph.wait("scalar", dv)
            ph.addi("scalar", lambda h, CW=CW: h.activation(CW, CW, AF.Exp), cx.sc)
            ph.wait("vector", cx.sc)
            ph.add("vector", lambda h, CW=CW, R2=R2: h.tensor_reduce(R2, CW, AX.X, ALU.add))
            ph.add("vector", lambda h, R1=R1, R2=R2: h.reciprocal(R1, R2))
            ph.add("vector", lambda h, TOK=TOK, CW=CW, R1=R1: h.tensor_scalar(TOK, CW, R1, None, ALU.mult))
            ph.add("vector", lambda h, TOK=TOK, R1=R1: h.reduce_max(R1, TOK, AX.X))
            ph.add("vector", lambda h, TOK=TOK, CW=CW, R1=R1: h.tensor_scalar(CW, TOK, R1, None, ALU.subtract))
            ph.add("vector", lambda h, CW=CW: h.tensor_scalar(CW, CW, 1e38, None, ALU.mult))
            ph.add("vector", lambda h, CW=CW: h.tensor_scalar(CW, CW, -1.0, 1.0, ALU.max, ALU.add))
            ph.add("vector", lambda h, TOK=TOK, CW=CW: h.scalar_tensor_tensor(CW, CW, -1e9, TOK, ALU.mult, ALU.add))
            ph.add("vector", lambda h, CW=CW, R2=R2: h.reduce_max(R2, CW, AX.X))
            ph.add("vector", lambda h, TOK=TOK, CW=CW, R2=R2: h.tensor_scalar(CW, TOK, R2, None, ALU.subtract))
            ph.add("vector", lambda h, CW=CW: h.tensor_scalar(CW, CW, 1e38, None, ALU.mult))
            ph.add("vector", lambda h, CW=CW: h.tensor_scalar(CW, CW, -1.0, 1.0, ALU.max, ALU.add))
            ph.add("vector", lambda h, CW=CW: h.tensor_scalar(CW, CW, 1.0, None, ALU.min))
            ph.addi("vector", lambda h, TOK=TOK, CW=CW: h.tensor_tensor(CW, CW, TOK, ALU.mult), dv)
            pst = t["PSB"][0:16, 256:256 + tn]
            ph.wait("tensor", dv)
            ph.addi("tensor", lambda h, p=pst, tn=tn: h.transpose(
                p, bass.AP(t["SCF"], 528, [[2048, tn], [1, 16]]), t["EYEF"][0:tn, 0:tn]), pe)
            ph.wait("vector", pe)
            col = w0 + tt_ * 128
            ph.addi("vector", lambda h, p=pst, col=col, tn=tn: h.tensor_copy(
                t["SCB2"][0:16, col:col + tn], p), dv)

    ys_dma = {}
    for j in range(8):
        su, sg = (j % 2) * 2, (j % 2) * 2 + 1
        if j >= 2:
            ph.wait("gpsimd", pe)
        ys_dma[j] = (ph.addi("gpsimd", lambda h, j=j, su=su: h.dma_start(
            cx.wr(su), t["Ws1T"][j].rearrange("p c m -> p (c m)")), gd, 16),
            ph.addi("gpsimd", lambda h, j=j, sg=sg: h.dma_start(
                cx.wr(sg), t["Ws1T"][j + 8].rearrange("p c m -> p (c m)")), gd, 16))
        su_, sg_ = su, sg
        ph.wait("tensor", gd, ys_dma[j][1])
        for w0, wn in W512:
            ph.wait("tensor", cx.sc)
            ph.wait("tensor", dv)
            pu = t["PSA"][:, 0:wn]
            pg = t["PSA"][:, 512:512 + wn]
            for c in range(8):
                ph.add("tensor", lambda h, c=c, p=pu, su=su_, w0=w0, wn=wn: h.matmul(
                    p, cx.wr(su)[:, c * 128:(c + 1) * 128], cx.xng(w0, wn, c),
                    start=(c == 0), stop=(c == 7)))
            for c in range(8):
                b = lambda h, c=c, p=pg, sg=sg_, w0=w0, wn=wn: h.matmul(
                    p, cx.wr(sg)[:, c * 128:(c + 1) * 128], cx.xng(w0, wn, c),
                    start=(c == 0), stop=(c == 7))
                (ph.addi("tensor", b, pe) if c == 7 else ph.add("tensor", b))
            ph.wait("scalar", pe)
            ph.addi("scalar", lambda h, p=pg, wn=wn: h.activation(
                t["SCB2"][:, 2176:2176 + wn], p, AF.Silu), cx.sc)
            ph.wait("vector", cx.sc)
            b = lambda h, p=pu, j=j, w0=w0, wn=wn: h.tensor_tensor(
                t["SCB1"][:, j * 2080 + w0: j * 2080 + w0 + wn], p,
                t["SCB2"][:, 2176:2176 + wn], ALU.mult)
            ph.addi("vector", b, dv)
    ws2_dma = []
    ph.wait("gpsimd", dv)
    for kc in range(8):
        ws2_dma.append(ph.addi("gpsimd", lambda h, kc=kc: h.dma_start(
            cx.wr(4 + kc), t["Ws2T"][kc]), gd, 16))
    ph.wait("tensor", gd, ws2_dma[-1])
    ph.wait("tensor", dv)
    for w0, wn in W512:
        for mo in range(8):
            bank = "PSB" if mo % 2 == 0 else "PSC"
            pz = t[bank][:, 0:wn]
            ph.wait("tensor", dv)
            for kc in range(8):
                b = lambda h, p=pz, kc=kc, mo=mo, w0=w0, wn=wn: h.matmul(
                    p, cx.wr(4 + kc)[:, mo * 128:(mo + 1) * 128],
                    t["SCB1"][:, kc * 2080 + w0: kc * 2080 + w0 + wn],
                    start=(kc == 0), stop=(kc == 7))
                (ph.addi("tensor", b, pe) if kc == 7 else ph.add("tensor", b))
            ph.wait("vector", pe)
            resw = t["RES"][:, mo * TT + w0: mo * TT + w0 + wn]
            ph.addi("vector", lambda h, p=pz, r=resw: h.tensor_tensor(r, r, p, ALU.add), dv)

    for e in range(16):
        ph.wait("gpsimd", pe)
        d1 = []
        for mp in range(8):
            d1.append(ph.addi("gpsimd", lambda h, e=e, mp=mp: h.dma_start(
                cx.wr(mp), t["W1e"][e, mp].rearrange("p c m -> p (c m)")), gd, 16))
        d2 = []
        for kp in range(4):
            d2.append(ph.addi("gpsimd", lambda h, e=e, kp=kp: h.dma_start(
                cx.wr(8 + kp), t["W2e"][e, kp]), gd, 16))
        for w0, wn in W512:
            cwb = t["SCB2"][:, 2688:2688 + wn]
            ph.wait("gpsimd", dv)
            cwb_done = ph.addi("gpsimd", lambda h, e=e, w0=w0, wn=wn, d=cwb: h.dma_start(
                d, bass.AP(t["SCB2"], 2048 * 0 + e * 4096 + w0, [[4096, 1], [0, 128], [1, wn]])), gd, 16)
            for j in range(4):
                ph.wait("tensor", gd, d1[j + 4])
                ph.wait("tensor", cx.sc)
                ph.wait("tensor", dv)
                pu = t["PSA"][:, (j % 2) * 1024: (j % 2) * 1024 + wn]
                pg = t["PSA"][:, (j % 2) * 1024 + 512: (j % 2) * 1024 + 512 + wn]
                for c in range(8):
                    ph.add("tensor", lambda h, c=c, p=pu, j=j, w0=w0, wn=wn: h.matmul(
                        p, cx.wr(j)[:, c * 128:(c + 1) * 128], cx.xng(w0, wn, c),
                        start=(c == 0), stop=(c == 7)))
                for c in range(8):
                    b = lambda h, c=c, p=pg, j=j, w0=w0, wn=wn: h.matmul(
                        p, cx.wr(j + 4)[:, c * 128:(c + 1) * 128], cx.xng(w0, wn, c),
                        start=(c == 0), stop=(c == 7))
                    (ph.addi("tensor", b, pe) if c == 7 else ph.add("tensor", b))
                ph.wait("scalar", pe)
                ph.wait("scalar", gd, cwb_done)
                ph.addi("scalar", lambda h, p=pg, wn=wn: h.activation(
                    t["SCB2"][:, 3584:3584 + wn], p, AF.Silu), cx.sc)
                ph.wait("vector", cx.sc)
                ph.add("vector", lambda h, p=pu, j=j, wn=wn: h.tensor_tensor(
                    t["SCB1"][:, j * 512: j * 512 + wn], p,
                    t["SCB2"][:, 3584:3584 + wn], ALU.mult))
                ph.addi("vector", lambda h, j=j, wn=wn, d=cwb: h.tensor_tensor(
                    t["SCB1"][:, j * 512: j * 512 + wn],
                    t["SCB1"][:, j * 512: j * 512 + wn], d, ALU.mult), dv)
            ph.wait("tensor", gd, d2[-1])
            ph.wait("tensor", dv)
            for mo in range(8):
                bank = "PSB" if mo % 2 == 0 else "PSC"
                pz = t[bank][:, 0:wn]
                ph.wait("tensor", dv)
                for kc in range(4):
                    b = lambda h, p=pz, kc=kc, mo=mo, wn=wn: h.matmul(
                        p, cx.wr(8 + kc)[:, mo * 128:(mo + 1) * 128],
                        t["SCB1"][:, kc * 512: kc * 512 + wn],
                        start=(kc == 0), stop=(kc == 3))
                    (ph.addi("tensor", b, pe) if kc == 3 else ph.add("tensor", b))
                ph.wait("vector", pe)
                resw = t["RES"][:, mo * TT + w0: mo * TT + w0 + wn]
                ph.addi("vector", lambda h, p=pz, r=resw: h.tensor_tensor(r, r, p, ALU.add), dv)


def emit_p6(cx, ph):
    t = cx.t
    pe, dv, sc, sd, gd = cx.pe, cx.dv, cx.sc, cx.sd, cx.gd
    cgrid = cx.wr(13, 1024)
    ph.addi("gpsimd", lambda h: h.iota(cgrid, [[0, 16], [1, 16], [0, 4]],
                                       base=-1, channel_multiplier=0,
                                       allow_small_or_imprecise_dtypes=True), cx.gp)

    emit_plain_ln(cx, ph)

    ph.wait("gpsimd", pe)
    dval = []
    for mp in range(8):
        dval.append(ph.addi("gpsimd", lambda h, mp=mp: h.dma_start(
            cx.wr(mp), t["W_valueT"][mp].rearrange("p c m -> p (c m)")), gd, 16))
    VT = t["SCB2"][:, 3900:4012]
    for mp in range(8):
        ph.wait("tensor", gd, dval[mp])
        ph.wait("tensor", dv)
        pv = t["PSB"][:, 0:14]
        for c in range(8):
            b = lambda h, c=c, mp=mp, p=pv: h.matmul(
                p, cx.wr(mp)[:, c * 128:(c + 1) * 128],
                t["XNG"][:, c * TT + 2048: c * TT + 2062], start=(c == 0), stop=(c == 7))
            (ph.addi("tensor", b, pe) if c == 7 else ph.add("tensor", b))
        ph.wait("vector", pe)
        ph.addi("vector", lambda h, p=pv, mp=mp: h.tensor_scalar(
            VT[:, mp * 14:(mp + 1) * 14], p, t["VEC2"][:, mp: mp + 1], None, ALU.add), dv)
    for c in range(8):
        pst = t["PST"][0:14, 0:128]
        ph.wait("tensor", dv)
        ph.addi("tensor", lambda h, c=c, p=pst: h.transpose(
            p, VT[:, c * 14:(c + 1) * 14], t["EYEB"][:]), pe)
        ph.wait("vector", pe)
        ph.addi("vector", lambda h, c=c, p=pst: h.tensor_copy(
            t["V14"][0:14, c * 128:(c + 1) * 128], p), dv)
    ph.addi("gpsimd", lambda h: h.memset(t["BDV"][:], 0.0), cx.gp)
    ph.wait("sync", dv)
    ph.wait("sync", cx.gp)
    for a in range(4):
        src0 = bass.AP(t["V14"], 0, [[1024, 14], [128, 8], [1, 64]])
        dst0 = bass.AP(t["BDV"], a * 32 * 1024, [[1024, 14], [128, 8], [1, 64]])
        src1 = bass.AP(t["V14"], 64, [[1024, 14], [128, 8], [1, 64]])
        dst1 = bass.AP(t["BDV"], (a * 32 + 16) * 1024 + 64, [[1024, 14], [128, 8], [1, 64]])
        ph.addi("sync", lambda h, d=dst0, s2=src0: h.dma_start(d, s2), sd, 16)
        ph.addi("sync", lambda h, d=dst1, s2=src1: h.dma_start(d, s2), sd, 16)
    bdv_done = cx.sd.n

    doa = []
    for pcs in range(2):
        doa.append(ph.addi("gpsimd", lambda h, pcs=pcs: h.dma_start(
            cx.wr(8 + pcs), t["W_oaT"][pcs].rearrange("p c m -> p (c m)")), gd, 16))
    ph.wait("tensor", gd, doa[-1])
    for wi, (w0, wn) in enumerate(W512):
        ph.wait("tensor", dv)
        po = t["PSA"][:, 0:wn]
        pa_ = t["PSA"][0:64, 512:512 + wn]
        for c in range(8):
            ph.add("tensor", lambda h, c=c, p=po, w0=w0, wn=wn: h.matmul(
                p, cx.wr(8)[:, c * 128:(c + 1) * 128], cx.xng(w0, wn, c),
                start=(c == 0), stop=(c == 7)))
        for c in range(8):
            b = lambda h, c=c, p=pa_, w0=w0, wn=wn: h.matmul(
                p, cx.wr(9)[:, c * 128: c * 128 + 64], cx.xng(w0, wn, c),
                start=(c == 0), stop=(c == 7))
            (ph.addi("tensor", b, pe) if c == 7 else ph.add("tensor", b))
        ph.wait("vector", pe)
        ph.add("vector", lambda h, p=po, wn=wn: h.tensor_scalar(
            t["SCF"][:, 0:wn], p, t["VEC2"][:, 24:25], None, ALU.add))
        ph.addi("vector", lambda h, p=pa_, wn=wn: h.tensor_scalar(
            t["SCF"][0:64, 512:512 + wn], p, bass.AP(t["VEC2"], 25, [[32, 64], [1, 1]]),
            None, ALU.add), dv)
        ntile = (wn + 127) // 128
        for tt_ in range(ntile):
            tn = min(128, wn - tt_ * 128)
            gtile = (w0 + tt_ * 128) // 128
            ph.wait("tensor", dv)
            pot = t["PSB"][0:tn, 0:128]
            pat = t["PSC"][0:tn, 0:64]
            ph.add("tensor", lambda h, p=pot, tt_=tt_, tn=tn: h.transpose(
                p, t["SCF"][:, tt_ * 128: tt_ * 128 + tn], t["EYEF"][:]))
            ph.addi("tensor", lambda h, p=pat, tt_=tt_, tn=tn: h.transpose(
                p, t["SCF"][0:64, 512 + tt_ * 128: 512 + tt_ * 128 + tn],
                t["EYEF"][0:64, 0:64]), pe)
            ph.wait("vector", pe)
            OT = t["SCF"][0:tn, 1024:1152]
            AT = t["SCF"][0:tn, 1152:1216]
            AM = t["SCF"][0:tn, 1216:1232]
            AW = t["SCF"][0:tn, 1232:1296]
            IX = t["SCF"][0:tn, 1296:1360]
            IY = t["SCF"][0:tn, 1360:1424]
            WX = t["SCF"][0:tn, 1424:1488]
            SS = t["SCF"][0:tn, 1488:1744]
            UU = t["SCF"][0:tn, 1744:1808]
            ph.add("vector", lambda h, p=pot, OT=OT: h.tensor_copy(OT, p))
            ph.addi("vector", lambda h, p=pat, AT=AT: h.tensor_copy(AT, p), dv)
            ph.add("vector", lambda h, AT=AT, AM=AM, tn=tn: h.tensor_reduce(
                AM, AT.rearrange("p (a b) -> p a b", b=4), AX.X, ALU.max))
            amb = lambda AM, tn: bass.AP(AM.tensor if False else t["SCF"], 0, [[0, 0]])
            ph.addi("vector", lambda h, AT=AT, AM=AM, tn=tn: h.tensor_tensor(
                AT, AT, bass.AP(t["SCF"], 1216, [[2048, tn], [1, 16], [0, 4]]), ALU.subtract), dv)
            ph.wait("scalar", dv)
            ph.addi("scalar", lambda h, AT=AT: h.activation(AT, AT, AF.Exp), cx.sc)
            ph.wait("vector", cx.sc)
            ph.add("vector", lambda h, AT=AT, AM=AM: h.tensor_reduce(
                AM, AT.rearrange("p (a b) -> p a b", b=4), AX.X, ALU.add))
            ph.add("vector", lambda h, AM=AM: h.reciprocal(AM, AM))
            ph.add("vector", lambda h, AT=AT, AW=AW, tn=tn: h.tensor_tensor(
                AW, AT, bass.AP(t["SCF"], 1216, [[2048, tn], [1, 16], [0, 4]]), ALU.mult))
            ph.add("vector", lambda h, OT=OT, IX=IX, gtile=gtile, tn=tn: h.tensor_scalar(
                IX, bass.AP(t["SCF"], 1024, [[2048, tn], [2, 64]]),
                t["REFQ"][0:tn, gtile:gtile + 1], None, ALU.add))
            ph.add("vector", lambda h, IX=IX: h.tensor_scalar(IX, IX, -0.5, None, ALU.add))
            ph.add("vector", lambda h, OT=OT, IY=IY, tn=tn: h.tensor_scalar(
                IY, bass.AP(t["SCF"], 1025, [[2048, tn], [2, 64]]), -0.5, None, ALU.add))
            ph.add("vector", lambda h, IX=IX, WX=WX: h.tensor_scalar(WX, IX, -1.0, None, ALU.mult))
            ph.add("vector", lambda h, IX=IX, WX=WX: h.tensor_tensor(WX, WX, IX, ALU.max))
            ph.add("vector", lambda h, WX=WX: h.tensor_scalar(WX, WX, -1.0, 1.0, ALU.mult, ALU.add))
            ph.add("vector", lambda h, WX=WX: h.tensor_scalar(WX, WX, 0.0, None, ALU.max))
            ph.add("vector", lambda h, AW=AW, WX=WX: h.tensor_tensor(AW, AW, WX, ALU.mult))
            ph.addi("vector", lambda h, AW=AW, IY=IY, UU=UU: h.tensor_tensor(UU, AW, IY, ALU.mult), dv)
            TMP = t["PSA"][0:tn, 1024:2048]
            ph.wait("vector", cx.gp)
            gb = bass.AP(t["SCF"], 1232, [[2048, tn], [4, 16], [0, 16], [1, 4]])
            ub = bass.AP(t["SCF"], 1744, [[2048, tn], [4, 16], [0, 16], [1, 4]])
            ph.add("vector", lambda h, TMP=TMP, gb=gb, tn=tn: h.tensor_tensor(
                TMP, gb, cx.wr(13, 1024)[0:tn, :], ALU.mult))
            ph.add("vector", lambda h, TMP=TMP, ub=ub: h.tensor_tensor(TMP, ub, TMP, ALU.subtract))
            ph.add("vector", lambda h, TMP=TMP: h.tensor_scalar(TMP, TMP, 0.0, None, ALU.max))
            ph.add("vector", lambda h, TMP=TMP, SS=SS, tn=tn: h.tensor_reduce(
                SS, TMP.rearrange("p (a b) -> p a b", b=4), AX.X, ALU.add))
            sA0 = bass.AP(t["SCF"], 1488, [[2048, tn], [16, 16], [1, 14]])
            sA1 = bass.AP(t["SCF"], 1489, [[2048, tn], [16, 16], [1, 14]])
            sA2 = bass.AP(t["SCF"], 1490, [[2048, tn], [16, 16], [1, 14]])
            tS = t["SCF"][0:tn, 1808:2032].rearrange("p (a b) -> p a b", b=14)
            dA = bass.AP(t["SCB1"], 12576, [[16640, tn], [16, 16], [1, 14]])
            ph.add("vector", lambda h, tS=tS, sA0=sA0, sA2=sA2: h.tensor_tensor(
                tS, sA0, sA2, ALU.add))
            ph.addi("vector", lambda h, dA=dA, sA1=sA1, tS=tS: h.scalar_tensor_tensor(
                dA, sA1, -2.0, tS, ALU.mult, ALU.add), dv)
            tstart = w0 + tt_ * 128
            for hhalf in range(2):
                bank = "PST" if hhalf == 0 else "PST2"
                pq_ = t[bank][0:128, 0:tn]
                ph.wait("tensor", dv)
                ph.addi("tensor", lambda h, p=pq_, hhalf=hhalf, tn=tn: h.transpose(
                    p, t["SCB1"][0:tn, 12576 + hhalf * 128: 12576 + (hhalf + 1) * 128],
                    t["EYEB"][0:tn, 0:tn]), pe)
                ph.wait("vector", pe)
                ph.addi("vector", lambda h, p=pq_, hhalf=hhalf, tstart=tstart, tn=tn: h.tensor_copy(
                    t["SCB1"][:, 8448 + hhalf * 2064 + tstart: 8448 + hhalf * 2064 + tstart + tn], p), dv)

    dwo = []
    ph.wait("gpsimd", pe)
    for mp in range(8):
        dwo.append(ph.addi("gpsimd", lambda h, mp=mp: h.dma_start(
            cx.wr(mp), t["W_out1T"][mp].rearrange("p c m -> p (c m)")), gd, 16))
    ph.wait("tensor", sd, bdv_done)
    for w0, wn in W512:
        for p_ in range(8):
            bank = "PSB" if p_ % 2 == 0 else "PSC"
            pz = t[bank][:, 0:wn]
            ph.wait("tensor", dv)
            afm_col = 8448 + (p_ // 4) * 2064 + w0
            arow = 32 * (p_ % 4)
            b = lambda h, p=pz, p_=p_, afm_col=afm_col, arow=arow, wn=wn: h.matmul(
                p, t["BDV"][arow:arow + 32, p_ * 128:(p_ + 1) * 128],
                t["SCB1"][arow:arow + 32, afm_col:afm_col + wn], start=True, stop=True,
                tile_position=(arow, 0))
            ph.addi("tensor", b, pe)
            ph.wait("vector", pe)
            ph.addi("vector", lambda h, p=pz, p_=p_, wn=wn: h.tensor_copy(
                t["SCB1"][:, p_ * 512: p_ * 512 + wn], p), dv)
        for mo in range(8):
            bank = "PSB" if mo % 2 == 0 else "PSC"
            pz = t[bank][:, 0:wn]
            ph.wait("tensor", dv)
            ph.wait("tensor", gd, dwo[-1])
            for kc in range(8):
                b = lambda h, p=pz, kc=kc, mo=mo, wn=wn: h.matmul(
                    p, cx.wr(mo)[:, kc * 128:(kc + 1) * 128],
                    t["SCB1"][:, kc * 512: kc * 512 + wn], start=(kc == 0), stop=(kc == 7))
                (ph.addi("tensor", b, pe) if kc == 7 else ph.add("tensor", b))
            ph.wait("vector", pe)
            ph.addi("vector", lambda h, p=pz, mo=mo, w0=w0, wn=wn: h.tensor_scalar(
                cx.xng(w0, wn, mo), p, t["VEC2"][:, 8 + mo: 9 + mo], None, ALU.add), dv)

    dwp = []
    ph.wait("gpsimd", pe)
    for mp in range(8):
        dwp.append(ph.addi("gpsimd", lambda h, mp=mp: h.dma_start(
            cx.wr(mp), t["W_projT"][mp].rearrange("p c m -> p (c m)")), gd, 16))
    ph.wait("tensor", gd, dwp[-1])
    for w0, wn in W512:
        for mo in range(8):
            bank = "PSB" if mo % 2 == 0 else "PSC"
            pz = t[bank][:, 0:wn]
            ph.wait("tensor", dv)
            for kc in range(8):
                b = lambda h, p=pz, kc=kc, mo=mo, w0=w0, wn=wn: h.matmul(
                    p, cx.wr(mo)[:, kc * 128:(kc + 1) * 128],
                    cx.xng(w0, wn, kc), start=(kc == 0), stop=(kc == 7))
                (ph.addi("tensor", b, pe) if kc == 7 else ph.add("tensor", b))
            ph.wait("vector", pe)
            resw = t["RES"][:, mo * TT + w0: mo * TT + w0 + wn]
            ph.addi("vector", lambda h, p=pz, r=resw, mo=mo: h.scalar_tensor_tensor(
                r, p, t["VEC2"][:, 16 + mo: 17 + mo], r, ALU.add, ALU.add), dv)


def emit_p7(cx, ph):
    t = cx.t
    nc = cx.nc
    pe, dv, sc, sd, gd = cx.pe, cx.dv, cx.sc, cx.sd, cx.gd

    if "kwg" not in t:
        t["kwg"] = nc.dram_tensor("kwg", [8, 128, 4096], BF16, kind="Internal")
        t["kwu"] = nc.dram_tensor("kwu", [8, 128, 4096], BF16, kind="Internal")
        t["kwd"] = nc.dram_tensor("kwd", [32, 128, 1024], BF16, kind="Internal")

    emit_plain_ln(cx, ph)

    def token_absmax_rows(src_block_fn, nblocks, srow_part, irow_part, kmax):
        TR = t["SCB2"][:, 0:2064]
        TM = t["SCB1"][:, 0:2064]
        ph.add("vector", lambda h: h.memset(t["SCF"][:, 0:17], 0.0))
        ph.add("vector", lambda h: h.tensor_tensor(TR, src_block_fn(0), src_block_fn(1), ALU.max))
        ph.add("vector", lambda h: h.tensor_tensor(TM, src_block_fn(0), src_block_fn(1), ALU.min))
        for b2 in range(2, nblocks):
            ph.add("vector", lambda h, b2=b2: h.tensor_tensor(TR, TR, src_block_fn(b2), ALU.max))
            ph.add("vector", lambda h, b2=b2: h.tensor_tensor(TM, TM, src_block_fn(b2), ALU.min))
        ph.add("vector", lambda h: h.tensor_scalar(TM, TM, -1.0, None, ALU.mult))
        ph.addi("vector", lambda h: h.tensor_tensor(TR, TR, TM, ALU.max), dv)
        ph.wait("tensor", dv)
        for tt_ in range(17):
            tn = min(128, 2064 - tt_ * 128)
            bank = "PST" if tt_ % 2 == 0 else "PST2"
            pq_ = t[bank][0:tn, 0:128]
            if tt_ >= 2:
                ph.wait("tensor", dv)
            ph.addi("tensor", lambda h, p=pq_, tt_=tt_, tn=tn: h.transpose(
                p, TR[:, tt_ * 128: tt_ * 128 + tn], t["EYEB"][:]), pe)
            ph.wait("vector", pe)
            ph.addi("vector", lambda h, p=pq_, tt_=tt_, tn=tn: h.tensor_reduce(
                t["SCF"][0:tn, tt_:tt_ + 1], p, AX.X, ALU.max, apply_absolute_value=True), dv)
        ph.add("vector", lambda h: h.tensor_scalar(
            t["SCF"][:, 0:17], t["SCF"][:, 0:17], 1e-5, None, ALU.max))
        ph.add("vector", lambda h: h.tensor_scalar(
            t["SCF"][:, 17:34], t["SCF"][:, 0:17], 1.0 / float(kmax), None, ALU.mult))
        ph.add("vector", lambda h: h.reciprocal(t["SCF"][:, 34:51], t["SCF"][:, 0:17]))
        ph.addi("vector", lambda h: h.tensor_scalar(
            t["SCF"][:, 34:51], t["SCF"][:, 34:51], float(kmax), None, ALU.mult), dv)
        ph.wait("tensor", dv)
        psx = t["PSC"][0:34, 0:128]
        ph.addi("tensor", lambda h, p=psx: h.transpose(
            p, t["SCF"][:, 17:51], t["EYEF"][:]), pe)
        ph.wait("vector", pe)
        ph.addi("vector", lambda h, p=psx: h.tensor_copy(t["SCF"][0:34, 64:192], p), dv)
        ph.wait("sync", dv)
        for part, roff in ((irow_part, 0), (srow_part, 17)):
            src = bass.AP(t["SCF"], roff * 2048 + 64, [[2048, 16], [1, 128]])
            dstrow = bass.AP(t["ROWS"], part * 2064, [[2064, 1], [128, 16], [1, 128]])
            ph.addi("sync", lambda h, d=dstrow, s2=src: h.dma_start(d, s2), sd, 16)
            src2 = bass.AP(t["SCF"], (roff + 16) * 2048 + 64, [[2048, 1], [1, 16]])
            dst2 = bass.AP(t["ROWS"], part * 2064 + 2048, [[2064, 1], [1, 16]])
            ph.addi("sync", lambda h, d=dst2, s2=src2: h.dma_start(d, s2), sd, 16)
        ph.wait("vector", sd)
        ph.wait("tensor", sd)

    token_absmax_rows(lambda c: bass.AP(t["XNG"], c * TT, [[8 * TT, 128], [1, 2064]]),
                      8, 64, 96, 127.0)
    if os.environ.get("BASSMK_P7CUT") == "1":
        return

    for w0, wn in [(i * 256, 256) for i in range(8)] + [(2048, 16)]:
        ph.wait("tensor", dv)
        psb = t["PSB"][:, 0:wn]
        ph.addi("tensor", lambda h, p=psb, w0=w0, wn=wn: h.matmul(
            p, t["ONES1"][64:65, :], cx.row("ROWS", 64, w0, wn), start=True, stop=True), pe)
        ph.wait("vector", pe)
        for c in range(8):
            ph.add("vector", lambda h, c=c, p=psb, w0=w0, wn=wn: h.tensor_tensor(
                t["SCF"][:, c * 256: c * 256 + wn], cx.xng(w0, wn, c), p, ALU.mult))
        for c in range(8):
            ph.add("vector", lambda h, c=c, wn=wn: h.tensor_scalar(
                t["SCF"][:, c * 256: c * 256 + wn], t["SCF"][:, c * 256: c * 256 + wn],
                C23, C23, ALU.add, ALU.subtract))
        for c in range(8):
            b = lambda h, c=c, w0=w0, wn=wn: h.tensor_copy(
                cx.xng(w0, wn, c), t["SCF"][:, c * 256: c * 256 + wn])
            (ph.addi("vector", b, dv) if c == 7 else ph.add("vector", b))

    if os.environ.get("BASSMK_P7CUT") == "2":
        return
    ABS = t["ROW2"][:, 0:1]
    ACC = t["ROW2"][:, 1:2]
    MEAN = t["ROW2"][:, 2:3]
    RM = t["ROW2"][:, 3:4]
    for nm, srcnm, npieces, piece_cols, sw_col, inv_col, dstnm in [
            ("gate", "wq_gateT", 8, 4096, 26, 29, "kwg"),
            ("up", "wq_upT", 8, 4096, 27, 30, "kwu"),
            ("down", "wq_downT", 32, 1024, 28, 31, "kwd")]:
        nsub = piece_cols // 2048 if piece_cols >= 2048 else 1
        subc = min(2048, piece_cols)
        first = True
        for pc in range(npieces):
            for sb2 in range(nsub):
                ph.wait("sync", dv)
                if piece_cols == 4096:
                    src = t[srcnm][pc].rearrange("p c m -> p (c m)")
                    sap = bass.AP(src.tensor, src.offset + sb2 * 2048, [[4096, 128], [1, 2048]])
                else:
                    sap = t[srcnm][pc]
                ph.addi("sync", lambda h, s2=sap, subc=subc: h.dma_start(
                    t["SCF"][:, 0:subc], s2), sd, 16)
                ph.wait("vector", sd)
                ph.add("vector", lambda h, subc=subc: h.tensor_reduce(
                    ABS, t["SCF"][:, 0:subc], AX.X, ALU.add, apply_absolute_value=True))
                if first:
                    b = lambda h: h.tensor_copy(ACC, ABS)
                    first = False
                else:
                    b = lambda h: h.tensor_tensor(ACC, ACC, ABS, ALU.add)
                ph.addi("vector", b, dv)
        total = npieces * piece_cols
        ph.wait("tensor", dv)
        pm1 = t["PSB"][:, 0:1]
        ph.addi("tensor", lambda h, p=pm1: h.matmul(
            p, t["ONESF128"][:], ACC, start=True, stop=True), pe)
        ph.wait("vector", pe)
        ph.add("vector", lambda h, total=total: h.tensor_scalar(
            MEAN, pm1, 1024.0 / (128.0 * float(total)), None, ALU.mult))
        ph.add("vector", lambda h: h.tensor_scalar(MEAN, MEAN, 1e-5, None, ALU.max))
        ph.add("vector", lambda h: h.reciprocal(RM, MEAN))
        ph.add("vector", lambda h, sw_col=sw_col: h.tensor_copy(
            t["VEC2"][:, sw_col:sw_col + 1], RM))
        ph.addi("vector", lambda h, inv_col=inv_col: h.tensor_copy(
            t["VEC2"][:, inv_col:inv_col + 1], MEAN), dv)
        for pc in range(npieces):
            for sb2 in range(nsub):
                ph.wait("sync", dv)
                ph.wait("vector", sd)
                if piece_cols == 4096:
                    src = t[srcnm][pc].rearrange("p c m -> p (c m)")
                    sap = bass.AP(src.tensor, src.offset + sb2 * 2048, [[4096, 128], [1, 2048]])
                    dst = t[dstnm][pc].rearrange("p m -> p m") if False else None
                    dap = bass.AP(t[dstnm][pc].tensor, t[dstnm][pc].offset + sb2 * 2048,
                                  [[4096, 128], [1, 2048]])
                else:
                    sap = t[srcnm][pc]
                    dap = t[dstnm][pc]
                nn = subc
                ph.addi("sync", lambda h, s2=sap, nn=nn: h.dma_start(
                    t["SCF"][:, 0:nn], s2), sd, 16)
                ph.wait("vector", sd)
                ph.add("vector", lambda h, nn=nn, sw_col=sw_col: h.tensor_scalar(
                    t["SCF"][:, 0:nn], t["SCF"][:, 0:nn],
                    t["VEC2"][:, sw_col:sw_col + 1], None, ALU.mult))
                ph.add("vector", lambda h, nn=nn: h.tensor_scalar(
                    t["SCF"][:, 0:nn], t["SCF"][:, 0:nn], C23, C23, ALU.add, ALU.subtract))
                ph.addi("vector", lambda h, nn=nn: h.tensor_scalar(
                    t["SCB2"][:, 2048:2048 + nn], t["SCF"][:, 0:nn],
                    -1.0, 1.0, ALU.max, ALU.min), dv)
                ph.wait("sync", dv)
                ph.addi("sync", lambda h, d=dap, nn=nn: h.dma_start(
                    d, t["SCB2"][:, 2048:2048 + nn]), sd, 16)


    if os.environ.get("BASSMK_P7CUT") == "3":
        return
    QW = [(0, 512), (512, 512), (1024, 512), (1536, 512), (2048, 16)]
    if os.environ.get("BASSMK_NQ"):
        QW = QW[:int(os.environ["BASSMK_NQ"])]
    for w0, wn in QW:
        for mb in range(8):
            sl_g = 0
            sl_u = 4
            ph.wait("gpsimd", pe)
            dg = ph.addi("gpsimd", lambda h, mb=mb, sl_g=sl_g: h.dma_start(
                cx.wr(sl_g, 4096), t["kwg"][mb]), gd, 16)
            du = ph.addi("gpsimd", lambda h, mb=mb, sl_u=sl_u: h.dma_start(
                cx.wr(sl_u, 4096), t["kwu"][mb]), gd, 16)
            ph.wait("tensor", gd, du)
            ph.wait("tensor", dv)
            pib = t["PSB"][:, 0:wn]
            ph.addi("tensor", lambda h, p=pib, w0=w0, wn=wn: h.matmul(
                p, t["ONES1"][96:97, :], cx.row("ROWS", 96, w0, wn), start=True, stop=True,
                tile_position=(96, 0)), pe)
            sib = t["SCF"][:, 1024:1024 + wn]
            ph.wait("vector", pe)
            ph.addi("vector", lambda h, p=pib, s=sib: h.tensor_copy(s, p), dv)
            for ms in range(4):
                pg = t["PSA"][:, (ms % 2) * 1024: (ms % 2) * 1024 + wn]
                pu = t["PSA"][:, (ms % 2) * 1024 + 512: (ms % 2) * 1024 + 512 + wn]
                if ms >= 2:
                    ph.wait("tensor", dv)
                for c in range(8):
                    ph.add("tensor", lambda h, c=c, p=pg, sl_g=sl_g, ms=ms, w0=w0, wn=wn: h.matmul(
                        p, bass.AP(t["WR"], sl_g * 1024 + c * 512 + ms * 128, [[14336, 128], [1, 128]]),
                        cx.xng(w0, wn, c), start=(c == 0), stop=(c == 7)))
                for c in range(8):
                    b = lambda h, c=c, p=pu, sl_u=sl_u, ms=ms, w0=w0, wn=wn: h.matmul(
                        p, bass.AP(t["WR"], sl_u * 1024 + c * 512 + ms * 128, [[14336, 128], [1, 128]]),
                        cx.xng(w0, wn, c), start=(c == 0), stop=(c == 7))
                    (ph.addi("tensor", b, pe) if c == 7 else ph.add("tensor", b))
                ph.wait("vector", pe)
                ph.add("vector", lambda h, p=pg, pib=sib, wn=wn: h.tensor_tensor(
                    t["SCB2"][:, 2048:2048 + wn], p, pib, ALU.mult))
                ph.addi("vector", lambda h, wn=wn: h.tensor_scalar(
                    t["SCB2"][:, 2048:2048 + wn], t["SCB2"][:, 2048:2048 + wn],
                    t["VEC2"][:, 29:30], None, ALU.mult), dv)
                ph.wait("scalar", dv)
                ph.addi("scalar", lambda h, wn=wn: h.activation(
                    t["SCB2"][:, 2560:2560 + wn], t["SCB2"][:, 2048:2048 + wn], AF.Silu), cx.sc)
                ph.wait("vector", cx.sc)
                ph.add("vector", lambda h, p=pu, pib=sib, wn=wn: h.tensor_tensor(
                    t["SCB2"][:, 3072:3072 + wn], p, pib, ALU.mult))
                ph.add("vector", lambda h, wn=wn: h.tensor_scalar(
                    t["SCB2"][:, 3072:3072 + wn], t["SCB2"][:, 3072:3072 + wn],
                    t["VEC2"][:, 30:31], None, ALU.mult))
                b = lambda h, mb=mb, ms=ms, wn=wn: h.tensor_tensor(
                    t["SCB1"][:, mb * 2080 + ms * 512: mb * 2080 + ms * 512 + wn],
                    t["SCB2"][:, 2560:2560 + wn], t["SCB2"][:, 3072:3072 + wn], ALU.mult)
                ph.addi("vector", b, dv)
        TRH = t["SCB2"][:, 0:512]
        TMH = t["SCB2"][:, 512:1024]
        ph.add("vector", lambda h: h.memset(t["SCF"][:, 0:4], 0.0))
        first = True
        for mb in range(8):
            for ms in range(4):
                sap = t["SCB1"][:, mb * 2080 + ms * 512: mb * 2080 + ms * 512 + wn]
                if first:
                    ph.add("vector", lambda h, s2=sap, wn=wn: h.tensor_copy(
                        TRH[:, 0:wn], s2))
                    ph.add("vector", lambda h, s2=sap, wn=wn: h.tensor_copy(
                        TMH[:, 0:wn], s2))
                    first = False
                else:
                    ph.add("vector", lambda h, s2=sap, wn=wn: h.tensor_tensor(
                        TRH[:, 0:wn], TRH[:, 0:wn], s2, ALU.max))
                    ph.add("vector", lambda h, s2=sap, wn=wn: h.tensor_tensor(
                        TMH[:, 0:wn], TMH[:, 0:wn], s2, ALU.min))
        ph.add("vector", lambda h, wn=wn: h.tensor_scalar(
            TMH[:, 0:wn], TMH[:, 0:wn], -1.0, None, ALU.mult))
        ph.add("vector", lambda h, wn=wn: h.tensor_tensor(
            TRH[:, 0:wn], TRH[:, 0:wn], TMH[:, 0:wn], ALU.max))
        ph.addi("vector", lambda h: h.engine_nop(), dv)
        ph.wait("tensor", dv)
        ntile = (wn + 127) // 128
        ph.add("vector", lambda h: h.memset(t["SCF"][:, 0:4], 0.0))
        for tt_ in range(ntile):
            tn = min(128, wn - tt_ * 128)
            bank = "PST" if tt_ % 2 == 0 else "PST2"
            pq_ = t[bank][0:tn, 0:128]
            if tt_ >= 2:
                ph.wait("tensor", dv)
            ph.addi("tensor", lambda h, p=pq_, tt_=tt_, tn=tn: h.transpose(
                p, TRH[:, tt_ * 128: tt_ * 128 + tn], t["EYEB"][:]), pe)
            ph.wait("vector", pe)
            ph.addi("vector", lambda h, p=pq_, tt_=tt_, tn=tn: h.tensor_reduce(
                t["SCF"][0:tn, tt_:tt_ + 1], p, AX.X, ALU.max, apply_absolute_value=True), dv)
        ph.add("vector", lambda h: h.tensor_scalar(
            t["SCF"][:, 0:4], t["SCF"][:, 0:4], 1e-5, None, ALU.max))
        ph.add("vector", lambda h: h.tensor_scalar(
            t["SCF"][:, 4:8], t["SCF"][:, 0:4], 1.0 / 127.0, None, ALU.mult))
        ph.add("vector", lambda h: h.reciprocal(t["SCF"][:, 8:12], t["SCF"][:, 0:4]))
        ph.addi("vector", lambda h: h.tensor_scalar(
            t["SCF"][:, 8:12], t["SCF"][:, 8:12], 127.0, None, ALU.mult), dv)
        ph.wait("tensor", dv)
        psx = t["PSC"][0:8, 0:128]
        ph.addi("tensor", lambda h, p=psx: h.transpose(
            p, t["SCF"][:, 4:12], t["EYEF"][:]), pe)
        ph.wait("vector", pe)
        ph.addi("vector", lambda h, p=psx: h.tensor_copy(t["SCF"][0:8, 64:192], p), dv)
        ph.wait("sync", dv)
        nfull = wn // 128
        for part, roff in ((0, 0), (32, 4)):
            if nfull > 0:
                src = bass.AP(t["SCF"], roff * 2048 + 64, [[2048, nfull], [1, 128]])
                dstrow = bass.AP(t["ROWS"], part * 2064 + w0, [[2064, 1], [128, nfull], [1, 128]])
                ph.addi("sync", lambda h, d=dstrow, s2=src: h.dma_start(d, s2), sd, 16)
            if wn % 128:
                src2 = bass.AP(t["SCF"], (roff + nfull) * 2048 + 64, [[2048, 1], [1, wn % 128]])
                dst2 = bass.AP(t["ROWS"], part * 2064 + w0 + nfull * 128, [[2064, 1], [1, wn % 128]])
                ph.addi("sync", lambda h, d=dst2, s2=src2: h.dma_start(d, s2), sd, 16)
        ph.wait("vector", sd)
        ph.wait("tensor", sd)
        ph.wait("tensor", dv)
        psh = t["PSB"][:, 0:wn]
        ph.addi("tensor", lambda h, p=psh, w0=w0, wn=wn: h.matmul(
            p, t["ONES1"][32:33, :], cx.row("ROWS", 32, w0, wn), start=True, stop=True), pe)
        ph.wait("vector", pe)
        for mb in range(8):
            for ms in range(4):
                sap = t["SCB1"][:, mb * 2080 + ms * 512: mb * 2080 + ms * 512 + wn]
                ph.add("vector", lambda h, s2=sap, p=psh, wn=wn: h.tensor_tensor(
                    t["SCF"][:, 0:wn], s2, p, ALU.mult))
                ph.add("vector", lambda h, wn=wn: h.tensor_scalar(
                    t["SCF"][:, 0:wn], t["SCF"][:, 0:wn], C23, C23, ALU.add, ALU.subtract))
                b = lambda h, s2=sap, wn=wn: h.tensor_copy(s2, t["SCF"][:, 0:wn])
                ph.addi("vector", b, dv)
        for g2 in range(4):
            ph.wait("gpsimd", pe)
            dk = []
            for kk in range(8):
                kc = g2 * 8 + kk
                dk.append(ph.addi("gpsimd", lambda h, kc=kc, kk=kk: h.dma_start(
                    cx.wr(kk), t["kwd"][kc]), gd, 16))
            ph.wait("tensor", gd, dk[-1])
            ph.wait("tensor", dv)
            pih = t["PSB"][:, 0:wn]
            ph.addi("tensor", lambda h, p=pih, w0=w0, wn=wn: h.matmul(
                p, t["ONES1"][0:1, :], cx.row("ROWS", 0, w0, wn), start=True, stop=True), pe)
            sih = t["SCF"][:, 1024:1024 + wn]
            ph.wait("vector", pe)
            ph.addi("vector", lambda h, p=pih, s=sih: h.tensor_copy(s, p), dv)
            for mo in range(8):
                pz = t["PSC"][:, 0:wn]
                ph.wait("tensor", dv)
                for kk in range(8):
                    kc = g2 * 8 + kk
                    mbb, mss = kc // 4, kc % 4
                    b = lambda h, p=pz, kk=kk, mo=mo, mbb=mbb, mss=mss, w0=w0, wn=wn: h.matmul(
                        p, bass.AP(t["WR"], kk * 1024 + mo * 128, [[14336, 128], [1, 128]]),
                        t["SCB1"][:, mbb * 2080 + mss * 512: mbb * 2080 + mss * 512 + wn],
                        start=(kk == 0), stop=(kk == 7))
                    (ph.addi("tensor", b, pe) if kk == 7 else ph.add("tensor", b))
                ph.wait("vector", pe)
                ph.add("vector", lambda h, p=pz, pih=sih, wn=wn: h.tensor_tensor(
                    t["SCB2"][:, 2048:2048 + wn], p, pih, ALU.mult))
                ph.add("vector", lambda h, wn=wn: h.tensor_scalar(
                    t["SCB2"][:, 2048:2048 + wn], t["SCB2"][:, 2048:2048 + wn],
                    t["VEC2"][:, 31:32], None, ALU.mult))
                resw = t["RES"][:, mo * TT + w0: mo * TT + w0 + wn]
                ph.addi("vector", lambda h, r=resw, wn=wn: h.tensor_tensor(
                    r, r, t["SCB2"][:, 2048:2048 + wn], ALU.add), dv)


def emit_pf(cx, ph):
    t = cx.t
    src = bass.AP(t["RES"], 0, [[8 * TT, 128], [TT, 8], [1, TC]])
    ph.addi("sync", lambda h: h.dma_start(t["out"][:], src), cx.sd, 16)
    ph.wait("sync", cx.sd)



def prep_shared(inputs):
    g = lambda k: np.asarray(inputs[k], dtype=np.float32)
    d = {}
    d["eyef"] = np.eye(128, dtype=np.float32)
    d["eyeb"] = np.eye(128, dtype=np.float32)
    d["onesS"] = np.full((128, 128), 1.0 / 1024.0, np.float32)
    d["onesSb"] = np.full((128, 128), 1.0 / 1024.0, np.float32)
    d["ones1"] = np.ones((128, 128), np.float32)
    d["w_inT"] = lhsT_host(np.ascontiguousarray(g("L0_attn_w_in").T))
    d["w_gateT"] = lhsT_host(np.ascontiguousarray(g("L0_attn_w_gate").T))
    wot = np.ascontiguousarray(g("L0_attn_w_out").T)
    d["w_outT"] = np.ascontiguousarray(np.concatenate([wot[0:64], wot[64:128]], axis=1))
    coup = g("L0_attn_coupling")
    bd = np.zeros((64, 64), np.float32)
    for h in range(4):
        bd[h * 16:(h + 1) * 16, h * 16:(h + 1) * 16] = coup[h].T
    d["bdcoupT"] = np.concatenate([bd, bd], 0)
    d["log_decay"] = g("L0_attn_log_decay").reshape(64, 1)
    d["frequency"] = g("L0_attn_frequency").reshape(64, 1)
    w1t = lhsT_host(np.ascontiguousarray(g("L0_attn_w_mlp1").T))
    d["w_mlp1T"] = np.ascontiguousarray(
        w1t.reshape(128, 8, 32, 128).transpose(2, 0, 1, 3))
    w2t = lhsT_host(np.ascontiguousarray(g("L0_attn_w_mlp2").T))
    d["w_mlp2T"] = np.ascontiguousarray(w2t.transpose(1, 0, 2))
    for nm, key in [("ln1_g", "L0_attn_ln1_g"), ("ln1_b", "L0_attn_ln1_b"),
                    ("ln2_g", "L0_attn_ln2_g"), ("ln2_b", "L0_attn_ln2_b"),
                    ("b_gate", "L0_attn_b_gate"), ("b_mlp2", "L0_attn_b_mlp2")]:
        d[nm] = vec_host(g(key))
    d["b_mlp1"] = vec_host(g("L0_attn_b_mlp1"))
    d["WgT"] = lhsT_host(np.ascontiguousarray(g("L0_ffn_Wg").T))
    ws1 = lhsT_host(g("L0_ffn_Ws1"))
    d["Ws1T"] = np.ascontiguousarray(ws1.reshape(128, 8, 16, 128).transpose(2, 0, 1, 3))
    ws2 = lhsT_host(g("L0_ffn_Ws2"))
    d["Ws2T"] = np.ascontiguousarray(ws2.transpose(1, 0, 2))
    w1 = g("L0_ffn_W1")
    w1l = np.stack([lhsT_host(w1[e]) for e in range(16)])
    d["W1e"] = np.ascontiguousarray(
        w1l.reshape(16, 128, 8, 8, 128).transpose(0, 3, 1, 2, 4))
    w2 = g("L0_ffn_W2")
    w2l = np.stack([lhsT_host(w2[e]) for e in range(16)])
    d["W2e"] = np.ascontiguousarray(w2l.transpose(0, 2, 1, 3))
    for nm, key in [("W_valueT", "L1_attn_W_value"), ("W_out1T", "L1_attn_W_out"),
                    ("W_projT", "L1_attn_W_proj")]:
        wl = lhsT_host(g(key))
        d[nm] = np.ascontiguousarray(wl.reshape(128, 8, 8, 128).transpose(2, 0, 1, 3))
    woa = np.concatenate([g("L1_attn_W_off"), g("L1_attn_W_attn"),
                          np.zeros((1024, 64), np.float32)], 1)
    woal = lhsT_host(woa)
    d["W_oaT"] = np.ascontiguousarray(woal.reshape(128, 8, 2, 128).transpose(2, 0, 1, 3))
    d["b_value"] = vec_host(g("L1_attn_b_value"))
    d["b_out1"] = vec_host(g("L1_attn_b_out"))
    d["b_proj"] = vec_host(g("L1_attn_b_proj"))
    d["b_off"] = g("L1_attn_b_off").reshape(128, 1)
    d["b_attn"] = g("L1_attn_b_attn").reshape(64, 1)
    wq1 = lhsT_host(np.ascontiguousarray(g("L1_ffn_w_gate").T))
    d["wq_gateT"] = np.ascontiguousarray(wq1.reshape(128, 8, 8, 512).transpose(2, 0, 1, 3))
    wq2 = lhsT_host(np.ascontiguousarray(g("L1_ffn_w_up").T))
    d["wq_upT"] = np.ascontiguousarray(wq2.reshape(128, 8, 8, 512).transpose(2, 0, 1, 3))
    wq3 = lhsT_host(np.ascontiguousarray(g("L1_ffn_w_down").T))
    d["wq_downT"] = np.ascontiguousarray(wq3.transpose(1, 0, 2))
    return d


def prep_core(inputs, core):
    x = np.asarray(inputs["L0_attn_x"], dtype=np.float32)
    b, hlf = core // 2, core % 2
    d = {}
    main = x[b, hlf * TC:(hlf + 1) * TC]
    ghost = x[b, 0:GH]
    d["xmain"] = fm_host(np.concatenate([main, ghost], 0))
    d["xpre"] = fm_host(x[b, 0:TC])
    fl = np.zeros((128, 2), np.float32)
    fl[:, 0] = float(hlf)
    fl[:, 1] = 1.0 - float(hlf)
    d["flags"] = fl
    rq = np.zeros((128, 17), np.float32)
    for tile in range(17):
        if tile < 16:
            idx = hlf * TC + tile * 128 + np.arange(128)
        else:
            idx = np.arange(128) % GH
        rq[:, tile] = idx / float(T - 1)
    d["refq"] = rq
    return d


def gather_out(results):
    out = np.zeros((B, T, D), np.float32)
    for core in range(8):
        b, hlf = core // 2, core % 2
        r = results[core]["out"]
        out[b, hlf * TC:(hlf + 1) * TC] = r.transpose(1, 0, 2).reshape(D, TC).T
    return out


_in_names = None


def kernel(**inputs):
    global _in_names
    nc = build_program()
    shared = prep_shared(inputs)
    in_maps = []
    for core in range(8):
        m = dict(shared)
        m.update(prep_core(inputs, core))
        in_maps.append(m)
    res = run_bass_kernel_spmd(nc, in_maps, core_ids=list(range(8)))
    return gather_out(res.results)


# version 02
# speedup vs baseline: 1.0372x; 1.0372x over previous
import math
import os
import numpy as np
import concourse.bass as bass
import concourse.mybir as mybir
from concourse.bass_utils import run_bass_kernel_spmd

F32 = mybir.dt.float32
BF16 = mybir.dt.bfloat16
AF = mybir.ActivationFunctionType
ALU = mybir.AluOpType
AX = mybir.AxisListType

B, T, D = 4, 4096, 1024
TC = 2048
GH = 16
TT = TC + GH
EPS = 1e-5
C23 = 3.0 * 2.0 ** 22
PI = math.pi

W512 = [(0, 512), (512, 512), (1024, 512), (1536, 512), (2048, 16)]
W256 = [(i * 256, 256) for i in range(8)] + [(2048, 16)]

STAGE_N = int(os.environ.get("BASSMK_STAGES", "99"))
DBG_TAP = os.environ.get("BASSMK_TAP", "")

ENGINES = ("sync", "tensor", "vector", "scalar", "gpsimd")
PHASE_BOUNDS = []


def lhsT_host(wT):
    d_in, M = wT.shape
    return np.ascontiguousarray(wT.reshape(d_in // 128, 128, M).transpose(1, 0, 2))


def vec_host(v):
    n = v.shape[0]
    return np.ascontiguousarray(v.reshape(n // 128, 128).T)


def fm_host(x2d):
    tn, d = x2d.shape
    return np.ascontiguousarray(x2d.T.reshape(d // 128, 128, tn).transpose(1, 0, 2))


class Sem:
    def __init__(self, nc, name):
        self.h = nc.alloc_semaphore(name)
        self.n = 0


def _ap_range(a):
    if not hasattr(a, "ap") or not hasattr(a, "offset"):
        return None
    lo = hi = a.offset
    for st, cnt in a.ap:
        d = st * (cnt - 1)
        if d >= 0:
            hi += d
        else:
            lo += d
    return (a.name, lo, hi)


class _Rec:
    __slots__ = ("ins", "writes", "inc")

    def __init__(self, ins, writes):
        self.ins = ins
        self.writes = writes
        self.inc = None


class _TIWrap:
    def __init__(self, guard, rec):
        self._g = guard
        self._r = rec

    def then_inc(self, semh, k):
        ins = self._r.ins.then_inc(semh, k)
        key = id(semh)
        self._g.semcnt[key] = self._g.semcnt.get(key, 0) + k
        self._r.inc = (semh, self._g.semcnt[key])
        return ins


class HazardGuard:

    HZK = 4

    def __init__(self, cx, eng):
        self.cx = cx
        self.eng = eng
        self.hist = []
        self.semcnt = {}

    def run_op(self, h, name, args, kwargs):
        writes = []
        reads = []
        for i, a in enumerate(args):
            r = _ap_range(a)
            if r is None:
                continue
            (writes if i == 0 else reads).append(r)
        for k, a in kwargs.items():
            r = _ap_range(a)
            if r is None:
                continue
            (writes if k in ("out", "accum_out") else reads).append(r)
        needs = []
        for rec in self.hist[-self.HZK:]:
            hit = False
            for (rn, rlo, rhi) in reads:
                for (wn, wlo, whi) in rec.writes:
                    if rn == wn and rlo <= whi and wlo <= rhi:
                        hit = True
                        break
                if hit:
                    break
            if hit:
                needs.append(rec)
        for rec in needs:
            if rec.inc is None:
                hz = self.hz
                hz.n += 1
                rec.ins.then_inc(hz.h, 1)
                key = id(hz.h)
                self.semcnt[key] = self.semcnt.get(key, 0) + 1
                rec.inc = (hz.h, self.semcnt[key])
                self.cx.n_hz_guards += 1
        waits = {}
        for rec in needs:
            semh, v = rec.inc
            key = id(semh)
            waits[key] = (semh, max(v, waits[key][1]) if key in waits else v)
        for semh, v in waits.values():
            h.wait_ge(semh, v)
        ins = getattr(h, name)(*args, **kwargs)
        rec = _Rec(ins, writes)
        self.hist.append(rec)
        if len(self.hist) > self.HZK:
            self.hist = self.hist[-self.HZK:]
        return _TIWrap(self, rec)


class _EngProxy:
    _PASS = ("wait_ge",)

    def __init__(self, h, guard):
        self._h = h
        self._g = guard

    def wait_ge(self, *a, **k):
        return self._h.wait_ge(*a, **k)

    def engine_nop(self, *a, **k):
        ins = self._h.engine_nop(*a, **k)
        rec = _Rec(ins, [])
        self._g.hist.append(rec)
        return _TIWrap(self._g, rec)

    def __getattr__(self, name):
        real = getattr(self._h, name)
        if not callable(real):
            return real
        g = self._g

        def call(*args, **kwargs):
            return g.run_op(self._h, name, args, kwargs)
        return call


class Phase:
    def __init__(self, cx, name):
        self.cx = cx
        self.name = name
        self.qs = {e: [] for e in ENGINES}

    def add(self, eng, build):
        self.qs[eng].append(build)

    def addi(self, eng, build, sem, k=1):
        sem.n += k
        self.qs[eng].append(lambda h, b=build, s=sem.h, kk=k: b(h).then_inc(s, kk))
        return sem.n

    def wait(self, eng, sem, val=None):
        v = sem.n if val is None else val
        self.qs[eng].append(lambda h, s=sem.h, vv=v: h.wait_ge(s, vv))

    def run(self):
        cx = self.cx
        PHASE_BOUNDS.append((self.name, cx.nc.n_instructions()))
        with cx.nc.Block(self.name) as blk:
            for e in ENGINES:
                fns = self.qs[e]
                if fns:
                    if e in ("vector", "scalar"):
                        gd_ = cx.guards[e]
                        getattr(blk, e)(lambda h, fns=fns, gd_=gd_: [
                            f(_EngProxy(h, gd_)) for f in fns])
                    else:
                        getattr(blk, e)(lambda h, fns=fns: [f(h) for f in fns])


class QFeeder:

    def __init__(self):
        self.units = []
        self.idx = 0

    def feed(self, ph, n=1):
        while n > 0 and self.idx < len(self.units):
            self.units[self.idx](ph)
            self.idx += 1
            n -= 1

    def drain(self, ph):
        self.feed(ph, len(self.units) - self.idx)


def build_q_units(cx):
    t = cx.t
    pe, dv, sd, gd = cx.pe, cx.dv, cx.sd, cx.gd
    qf = QFeeder()
    st = {"cons": {}, "store_gd": {}, "first": True, "gj": 0}
    SUB = 512
    ABS = t["ROW2"][:, 0:1]
    ACC = t["ROW2"][:, 1:2]
    rows_pitch = 2064

    def qsf(h2):
        return t["QSF"][:, h2 * SUB:(h2 + 1) * SUB]

    def qsb(h2):
        return t["QSB"][:, h2 * SUB:(h2 + 1) * SUB]

    for nm, srcnm, npieces, piece_cols, sw_col, inv_col, dstnm in [
            ("gate", "wq_gateT", 8, 4096, 26, 29, "kwg"),
            ("up", "wq_upT", 8, 4096, 27, 30, "kwu"),
            ("down", "wq_downT", 32, 1024, 28, 31, "kwd")]:
        subs = [(pc, co) for pc in range(npieces) for co in range(0, piece_cols, SUB)]
        total = npieces * piece_cols

        def sweep_unit(ph, j, pc, co, srcnm=srcnm, nm=nm):
            gj = st["gj"]; st["gj"] += 1
            h2 = gj % 2
            if gj - 2 in st["cons"]:
                ph.wait("sync", dv, st["cons"][gj - 2])
            if st["first"]:
                ph.wait("sync", pe)
                st["first"] = False
            src0 = t[srcnm][pc]
            sap = bass.AP(src0.tensor, src0.offset + co, [[4096 if srcnm != "wq_downT" else 1024, 128], [1, SUB]])
            dma = ph.addi("sync", lambda h, s2=sap, h2=h2: h.dma_start(qsf(h2), s2), sd, 16)
            ph.wait("vector", sd, dma)
            ph.add("vector", lambda h, h2=h2: h.tensor_reduce(
                ABS, qsf(h2), AX.X, ALU.add, apply_absolute_value=True))
            if j == 0:
                b = lambda h: h.tensor_copy(ACC, ABS)
            else:
                b = lambda h: h.tensor_tensor(ACC, ACC, ABS, ALU.add)
            st["cons"][gj] = ph.addi("vector", b, dv)

        def mean_unit(ph, nm=nm, total=total, sw_col=sw_col, inv_col=inv_col):
            ph.wait("gpsimd", dv)
            ph.wait("gpsimd", pe)
            dst = bass.AP(t["ROWS"], 0, [[rows_pitch, 1], [1, 128]])
            src = bass.AP(t["ROW2"], 1, [[4, 128], [1, 1]])
            g1 = ph.addi("gpsimd", lambda h, d=dst, s=src: h.dma_start(d, s), gd, 16)
            ph.wait("vector", gd, g1)
            ph.add("vector", lambda h: h.tensor_reduce(
                t["ROWS"][0:1, 130:131], t["ROWS"][0:1, 0:128], AX.X, ALU.add))
            ph.add("vector", lambda h, total=total: h.tensor_scalar(
                t["ROWS"][0:1, 131:132], t["ROWS"][0:1, 130:131], 1.0 / float(total * 128), None, ALU.mult))
            ph.add("vector", lambda h: h.tensor_scalar(
                t["ROWS"][0:1, 131:132], t["ROWS"][0:1, 131:132], 1e-5, None, ALU.max))
            m_dv = ph.addi("vector", lambda h: h.reciprocal(
                t["ROWS"][0:1, 132:133], t["ROWS"][0:1, 131:132]), dv)
            ph.wait("gpsimd", dv, m_dv)
            for c2, rc in ((sw_col, 132), (inv_col, 131)):
                dst2 = bass.AP(t["VEC2"], c2, [[32, 128], [1, 1]])
                src2 = bass.AP(t["ROWS"], rc, [[rows_pitch, 1], [0, 128], [1, 1]])
                st["store_gd"][("mean", nm)] = ph.addi(
                    "gpsimd", lambda h, d=dst2, s=src2: h.dma_start(d, s), gd, 16)

        def quant_unit(ph, j, pc, co, srcnm=srcnm, nm=nm, sw_col=sw_col, dstnm=dstnm,
                       piece_cols=piece_cols):
            gj = st["gj"]; st["gj"] += 1
            h2 = gj % 2
            if gj - 2 in st["cons"]:
                ph.wait("sync", dv, st["cons"][gj - 2])
            src0 = t[srcnm][pc]
            pitch = 4096 if srcnm != "wq_downT" else 1024
            sap = bass.AP(src0.tensor, src0.offset + co, [[pitch, 128], [1, SUB]])
            dma = ph.addi("sync", lambda h, s2=sap, h2=h2: h.dma_start(qsf(h2), s2), sd, 16)
            ph.wait("vector", sd, dma)
            ph.wait("vector", gd, max(st["store_gd"].get(("qsb", h2), 0),
                                      st["store_gd"][("mean", nm)]))
            ph.add("vector", lambda h, h2=h2, sw_col=sw_col: h.tensor_scalar(
                qsf(h2), qsf(h2), t["VEC2"][:, sw_col:sw_col + 1], None, ALU.mult))
            ph.add("vector", lambda h, h2=h2: h.tensor_scalar(
                qsf(h2), qsf(h2), C23, C23, ALU.add, ALU.subtract))
            st["cons"][gj] = ph.addi("vector", lambda h, h2=h2: h.tensor_scalar(
                qsb(h2), qsf(h2), -1.0, 1.0, ALU.max, ALU.min), dv)
            d0 = t[dstnm][pc]
            dap = bass.AP(d0.tensor, d0.offset + co, [[pitch, 128], [1, SUB]])
            ph.wait("gpsimd", dv, st["cons"][gj])
            st["store_gd"][("qsb", h2)] = ph.addi(
                "gpsimd", lambda h, d=dap, h2=h2: h.dma_start(d, qsb(h2)), gd, 16)

        for j, (pc, co) in enumerate(subs):
            qf.units.append(lambda ph, j=j, pc=pc, co=co, f=sweep_unit: f(ph, j, pc, co))
        qf.units.append(lambda ph, f=mean_unit: f(ph))
        for j, (pc, co) in enumerate(subs):
            qf.units.append(lambda ph, j=j, pc=pc, co=co, f=quant_unit: f(ph, j, pc, co))
    return qf


class Ctx:
    def __init__(self, nc):
        self.nc = nc
        self.t = {}
        self.sd = Sem(nc, "sd")
        self.gd = Sem(nc, "gd")
        self.pe = Sem(nc, "pe")
        self.dv = Sem(nc, "dv")
        self.sc = Sem(nc, "sc")
        self.gp = Sem(nc, "gp")
        self.n_hz_guards = 0
        self.guards = {"vector": HazardGuard(self, "vector"),
                       "scalar": HazardGuard(self, "scalar")}
        self.guards["vector"].hz = Sem(nc, "hzv")
        self.guards["scalar"].hz = Sem(nc, "hzs")
        self.dbg = []

    def row(self, tensor, part, c0, n):
        th = self.t[tensor]
        pitch = th.shape[1]
        return bass.AP(th, part * pitch + c0, [[pitch, 1], [1, n]])

    def wr(self, slot, cols=1024):
        return self.t["WR"][:, slot * 1024: slot * 1024 + cols]

    def res(self, w0, wn, c):
        return self.t["RES"][:, c * TT + w0: c * TT + w0 + wn]

    def xng(self, w0, wn, c):
        return self.t["XNG"][:, c * TT + w0: c * TT + w0 + wn]

    def beta_pre(self, w0, wn):
        return bass.AP(self.t["WR"], 10 * 1024 + w0, [[14336, 128], [1, wn]])

    def beta_main(self, w0, wn):
        return bass.AP(self.t["WR"], 10 * 1024 + 2048 + w0, [[14336, 128], [1, wn]])

    def beta_chunk(self, n):
        return bass.AP(self.t["WR"], 10 * 1024 + 64 * n, [[14336, 128], [1, 64]])


def build_program():
    nc = bass.Bass("TRN2", target_bir_lowering=False, detect_race_conditions=False)
    cx = Ctx(nc)
    t = cx.t

    def din(name, shape, dtype=F32):
        t[name] = nc.dram_tensor(name, list(shape), dtype, kind="ExternalInput")

    def sb(name, shape, dtype):
        t[name] = nc.alloc_sbuf_tensor(name, list(shape), dtype)

    din("xmain", (128, 8, TT))
    din("xpre", (128, 8, TC))
    din("flags", (128, 2))
    din("eyef", (128, 128)); din("eyeb", (128, 128))
    din("onesS", (128, 128)); din("onesSb", (128, 128)); din("ones1", (128, 128))
    din("refq", (128, 17))
    din("w_inT", (128, 8, 128))
    din("w_gateT", (128, 8, 1024))
    din("w_outT", (64, 2048))
    din("bdcoupT", (128, 64))
    din("log_decay", (64, 1)); din("frequency", (64, 1))
    din("w_mlp1T", (32, 128, 8, 128))
    din("w_mlp2T", (32, 128, 1024))
    din("ln1_g", (128, 8)); din("ln1_b", (128, 8))
    din("ln2_g", (128, 8)); din("ln2_b", (128, 8))
    din("b_gate", (128, 8)); din("b_mlp1", (128, 32)); din("b_mlp2", (128, 8))
    din("WgT", (128, 8, 16))
    din("Ws1T", (16, 128, 8, 128))
    din("Ws2T", (8, 128, 1024))
    din("W1e", (16, 8, 128, 8, 128))
    din("W2e", (16, 4, 128, 1024))
    din("W_valueT", (8, 128, 8, 128))
    din("W_oaT", (2, 128, 8, 128))
    din("b_value", (128, 8)); din("b_off", (128, 1)); din("b_attn", (64, 1))
    din("W_out1T", (8, 128, 8, 128)); din("b_out1", (128, 8))
    din("W_projT", (8, 128, 8, 128)); din("b_proj", (128, 8))
    din("wq_gateT", (8, 128, 8, 512))
    din("wq_upT", (8, 128, 8, 512))
    din("wq_downT", (32, 128, 1024))
    t["out"] = nc.dram_tensor("out", [128, 8, TC], F32, kind="ExternalOutput")

    sb("RES", (128, 8 * TT), F32)
    sb("XNG", (128, 8 * TT), BF16)
    sb("SCB1", (128, 16640), BF16)
    sb("SCB2", (128, 4096), BF16)
    sb("SCF", (128, 2048), F32)
    sb("WR", (128, 14336), BF16)
    _crci_base = nc.sbuf_base
    sb("CRCI", (128, 4112), BF16)
    _after_crci = nc.sbuf_base
    nc.sbuf_base = _crci_base
    sb("QSF", (128, 1024), F32)
    sb("QSB", (128, 1024), BF16)
    assert nc.sbuf_base <= _after_crci, (nc.sbuf_base, _after_crci)
    nc.sbuf_base = _after_crci
    sb("CC", (128, 512), BF16)
    sb("ROWS", (128, 2064), F32)
    sb("ROW2", (128, 4), F32)
    sb("SCFX", (128, 1024), F32)
    sb("VEC", (128, 48), F32)
    sb("VECB", (128, 32), F32)
    sb("VEC2", (128, 32), F32)
    sb("EYEF", (128, 128), F32)
    sb("EYEB", (128, 128), BF16)
    sb("ONESF128", (128, 128), F32)
    sb("ONESB128", (128, 128), BF16)
    sb("ONES1", (128, 128), F32)
    sb("FLG", (128, 2), F32)
    sb("REFQ", (128, 17), F32)
    sb("EKER", (64, 400), F32)
    sb("PKF", (64, 576), BF16)
    sb("BDC", (128, 64), BF16)
    sb("V14", (16, 1024), BF16)
    sb("BDV", (128, 1024), BF16)
    sb("WOUT", (64, 2048), BF16)

    t["PSA"] = nc.alloc_psum_tensor("PSA", [128, 2048], F32)
    t["PSB"] = nc.alloc_psum_tensor("PSB", [128, 512], F32)
    t["PSC"] = nc.alloc_psum_tensor("PSC", [128, 512], F32)
    t["PST"] = nc.alloc_psum_tensor("PST", [128, 512], BF16)
    t["PST2"] = nc.alloc_psum_tensor("PST2", [128, 512], BF16)
    assert nc.psum_base <= 8, nc.psum_base

    t["kwg"] = nc.dram_tensor("kwg", [8, 128, 4096], BF16, kind="Internal")
    t["kwu"] = nc.dram_tensor("kwu", [8, 128, 4096], BF16, kind="Internal")
    t["kwd"] = nc.dram_tensor("kwd", [32, 128, 1024], BF16, kind="Internal")
    cx.qf = build_q_units(cx)

    stages = [(0, emit_p0), (1, emit_p1), (2, emit_p2), (3, emit_p3),
              (4, emit_p4a), (4, emit_p4b), (5, emit_p5), (6, emit_p6), (7, emit_p7)]
    for sn, fn in stages:
        if sn <= STAGE_N:
            p = Phase(cx, fn.__name__)
            fn(cx, p)
            p.run()

    if DBG_TAP:
        emit_tap(cx)
    p = Phase(cx, "pf")
    emit_pf(cx, p)
    p.run()
    nc.finalize()
    return nc



def emit_tap(cx):
    nc, t = cx.nc, cx.t
    spec = DBG_TAP.split(",")
    name = spec[0]
    th = t[name]
    shape = list(th.shape)
    c0, c1 = (int(spec[1]), int(spec[2])) if len(spec) >= 3 else (0, shape[1])
    p0, p1 = (int(spec[3]), int(spec[4])) if len(spec) >= 5 else (0, shape[0])
    dt = th.dtype
    t["dbg"] = nc.dram_tensor("dbg", [p1 - p0, c1 - c0], dt, kind="ExternalOutput")
    p = Phase(cx, "ptap")
    p.addi("sync", lambda h: h.dma_start(t["dbg"][:], th[p0:p1, c0:c1]), cx.sd, 16)
    p.wait("sync", cx.sd)
    p.run()



def ln_rep(cx, ph, srcs, wn, bf16_src=False, rho=False):
    t = cx.t
    pe, dv, sc = cx.pe, cx.dv, cx.sc
    ph.wait("scalar", pe)
    ph.wait("scalar", dv)
    for c in range(8):
        b = lambda h, c=c: h.activation(
            t["SCB1"][:, c * wn: (c + 1) * wn], srcs[c], AF.Square)
        (ph.addi("scalar", b, sc) if c == 7 else ph.add("scalar", b))
    ph.wait("tensor", sc)
    ph.wait("tensor", dv)
    pm = t["PSB"][:, 0:wn]
    pq = t["PSC"][:, 0:wn]
    lhs_m = t["ONESB128"][:] if bf16_src else t["ONESF128"][:]
    for c in range(8):
        ph.add("tensor", lambda h, c=c, lhs_m=lhs_m: h.matmul(
            pm, lhs_m, srcs[c], start=(c == 0), stop=(c == 7)))
    for c in range(8):
        b = lambda h, c=c: h.matmul(
            pq, t["ONESB128"][:], t["SCB1"][:, c * wn: (c + 1) * wn],
            start=(c == 0), stop=(c == 7))
        (ph.addi("tensor", b, pe) if c == 7 else ph.add("tensor", b))
    ph.wait("vector", pe)
    mb = t["SCB2"][:, 2048:2048 + wn]
    var = t["SCFX"][:, 0:wn]
    rstd = t["SCB2"][:, 2560:2560 + wn]
    ph.add("vector", lambda h: h.tensor_copy(mb, pm))
    ph.add("vector", lambda h: h.tensor_tensor(var, mb, mb, ALU.mult))
    ph.add("vector", lambda h: h.tensor_tensor(var, pq, var, ALU.subtract))
    ph.addi("vector", lambda h: h.tensor_scalar(var, var, EPS, None, ALU.add), dv)
    ph.wait("scalar", dv)
    ph.add("scalar", lambda h: h.activation(t["SCFX"][:, 512:512 + wn], var, AF.Ln))
    ph.addi("scalar", lambda h: h.activation(
        rstd, t["SCFX"][:, 512:512 + wn], AF.Exp, scale=-0.5), sc)
    out = dict(m_rep=pm, mb=mb, rstd=rstd)
    if rho:
        vq = t["SCFX"][:, 512:512 + wn]
        rr = t["SCB2"][:, 3072:3072 + wn]
        rhot = t["SCB2"][:, 3584:3584 + wn]
        ph.wait("vector", sc)
        ph.add("vector", lambda h: h.tensor_tensor(vq, rstd, rstd, ALU.mult))
        ph.add("vector", lambda h: h.tensor_tensor(vq, var, vq, ALU.mult))
        ph.addi("vector", lambda h: h.tensor_scalar(vq, vq, EPS, None, ALU.add), dv)
        ph.wait("scalar", dv)
        ph.add("scalar", lambda h: h.activation(vq, vq, AF.Ln))
        ph.addi("scalar", lambda h: h.activation(rr, vq, AF.Exp, scale=-0.5), sc)
        ph.wait("vector", sc)
        ph.addi("vector", lambda h: h.tensor_tensor(rhot, rstd, rr, ALU.mult), dv)
        out["rho"] = rhot
    return out


def emit_plain_ln(cx, ph, store_rows=None):
    t = cx.t
    pe, dv, sc, sd, gd = cx.pe, cx.dv, cx.sc, cx.sd, cx.gd
    for w0, wn in W512:
        r = ln_rep(cx, ph, [cx.res(w0, wn, c) for c in range(8)], wn,
                   rho=bool(store_rows and "rho" in store_rows))
        ph.wait("vector", sc)
        for c in range(8):
            ph.add("vector", lambda h, c=c, r=r, w0=w0, wn=wn: h.tensor_tensor(
                t["SCB1"][:, 8448 + c * wn: 8448 + (c + 1) * wn],
                cx.res(w0, wn, c), r["m_rep"], ALU.subtract))
        for c in range(8):
            b = lambda h, c=c, r=r, w0=w0, wn=wn: h.tensor_tensor(
                cx.xng(w0, wn, c), t["SCB1"][:, 8448 + c * wn: 8448 + (c + 1) * wn],
                r["rstd"], ALU.mult)
            (ph.addi("vector", b, dv) if c == 7 else ph.add("vector", b))
        if store_rows:
            ph.wait("gpsimd", dv)
            ph.wait("gpsimd", sc)
            for q, part in store_rows.items():
                src = {"m": r["mb"], "rstd": r["rstd"], "rho": r.get("rho")}[q]
                ap = bass.AP(src.tensor, src.offset, [[src.ap[0][0], 1], [1, wn]])
                dst = cx.row("ROWS", part, w0, wn)
                ph.addi("gpsimd", lambda h, d=dst, s2=ap: h.dma_start(d, s2), gd, 16)
            ph.wait("scalar", gd)
            ph.wait("vector", gd)



def emit_p0(cx, ph):
    t = cx.t
    sd, gd, dv, sc, gp = cx.sd, cx.gd, cx.dv, cx.sc, cx.gp

    def s_dma(dst, src):
        ph.addi("sync", lambda h, d=dst, s=src: h.dma_start(d, s), sd, 16)

    def g_dma(dst, src):
        ph.addi("gpsimd", lambda h, d=dst, s=src: h.dma_start(d, s), gd, 16)

    s_dma(t["EKER"][:, 256:257], t["log_decay"][:])
    s_dma(t["EKER"][:, 257:258], t["frequency"][:])
    cx.n_eker_dma = sd.n
    s_dma(t["RES"][:].rearrange("p (c n) -> p c n", c=8), t["xmain"][:])
    s_dma(t["FLG"][:], t["flags"][:])
    s_dma(t["REFQ"][:], t["refq"][:])
    s_dma(t["EYEF"][:], t["eyef"][:])
    s_dma(t["ONESF128"][:], t["onesS"][:])
    s_dma(t["ONES1"][:], t["ones1"][:])
    s_dma(t["VEC"][:, 0:8], t["ln1_g"][:])
    s_dma(t["VEC"][:, 8:16], t["ln1_b"][:])
    s_dma(t["VEC"][:, 16:24], t["ln2_g"][:])
    s_dma(t["VEC"][:, 24:32], t["ln2_b"][:])
    s_dma(t["VEC"][:, 32:40], t["b_gate"][:])
    s_dma(t["VEC"][:, 40:48], t["b_mlp2"][:])
    s_dma(t["VECB"][:, 0:32], t["b_mlp1"][:])
    s_dma(t["VEC2"][:, 0:8], t["b_value"][:])
    s_dma(t["VEC2"][:, 8:16], t["b_out1"][:])
    s_dma(t["VEC2"][:, 16:24], t["b_proj"][:])
    s_dma(t["VEC2"][:, 24:25], t["b_off"][:])
    s_dma(bass.AP(t["VEC2"], 25, [[32, 64], [1, 1]]), t["b_attn"][:])

    g_dma(t["EYEB"][:], t["eyeb"][:])
    g_dma(t["ONESB128"][:], t["onesSb"][:])
    g_dma(t["BDC"][:], t["bdcoupT"][:])
    g_dma(cx.wr(0, 1024), t["w_inT"][:].rearrange("p c m -> p (c m)"))
    for c in range(8):
        g_dma(cx.wr(1 + c, 1024), t["w_gateT"][:, c, :])
    g_dma(t["WOUT"][:], t["w_outT"][:])
    ph.addi("gpsimd", lambda h: h.memset(t["PKF"][:], 0.0), gp)

    ek = t["EKER"]
    ph.wait("scalar", sd, cx.n_eker_dma * 1)
    ph.addi("scalar", lambda h: h.activation(ek[:, 258:259], ek[:, 256:257], AF.Sigmoid), sc)
    ph.wait("vector", sd, cx.n_eker_dma * 1)
    ph.add("vector", lambda h: h.tensor_scalar(ek[:, 259:260], ek[:, 257:258], 1.0 / (2 * PI), None, ALU.mult))
    ph.add("vector", lambda h: h.tensor_scalar(ek[:, 260:261], ek[:, 259:260], C23, C23, ALU.add, ALU.subtract))
    ph.add("vector", lambda h: h.scalar_tensor_tensor(ek[:, 261:262], ek[:, 260:261], -1.0, ek[:, 259:260], ALU.mult, ALU.add))
    ph.add("vector", lambda h: h.tensor_scalar(ek[:, 261:262], ek[:, 261:262], 2 * PI, None, ALU.mult))
    ph.add("vector", lambda h: h.tensor_scalar(ek[:, 262:263], ek[:, 259:260], 1.0, 0.25, ALU.mult, ALU.add))
    ph.add("vector", lambda h: h.tensor_scalar(ek[:, 263:264], ek[:, 262:263], C23, C23, ALU.add, ALU.subtract))
    ph.add("vector", lambda h: h.scalar_tensor_tensor(ek[:, 264:265], ek[:, 263:264], -1.0, ek[:, 262:263], ALU.mult, ALU.add))
    ph.addi("vector", lambda h: h.tensor_scalar(ek[:, 264:265], ek[:, 264:265], 2 * PI, None, ALU.mult), dv)
    ph.wait("scalar", dv)
    ph.add("scalar", lambda h: h.activation(ek[:, 265:266], ek[:, 261:262], AF.Sin))
    ph.addi("scalar", lambda h: h.activation(ek[:, 266:267], ek[:, 264:265], AF.Sin), sc)
    ph.wait("vector", sc)
    ph.add("vector", lambda h: h.tensor_tensor(ek[:, 267:268], ek[:, 258:259], ek[:, 266:267], ALU.mult))
    ph.add("vector", lambda h: h.tensor_tensor(ek[:, 268:269], ek[:, 258:259], ek[:, 265:266], ALU.mult))
    ph.add("vector", lambda h: h.memset(ek[:, 0:1], 1.0))
    ph.add("vector", lambda h: h.memset(ek[:, 64:65], 0.0))
    ph.add("vector", lambda h: h.tensor_copy(ek[:, 1:2], ek[:, 267:268]))
    ph.add("vector", lambda h: h.tensor_copy(ek[:, 65:66], ek[:, 268:269]))
    q = 1
    while q < 64:
        n = min(q, 64 - q)
        ph.add("vector", lambda h, q=q: h.tensor_tensor(
            ek[:, 396:397], ek[:, q - 1:q], ek[:, 267:268], ALU.mult))
        ph.add("vector", lambda h, q=q: h.tensor_tensor(
            ek[:, 397:398], ek[:, 64 + q - 1:64 + q], ek[:, 268:269], ALU.mult))
        ph.add("vector", lambda h: h.tensor_tensor(
            ek[:, 398:399], ek[:, 396:397], ek[:, 397:398], ALU.subtract))
        ph.add("vector", lambda h, q=q: h.tensor_tensor(
            ek[:, 396:397], ek[:, q - 1:q], ek[:, 268:269], ALU.mult))
        ph.add("vector", lambda h, q=q: h.tensor_tensor(
            ek[:, 397:398], ek[:, 64 + q - 1:64 + q], ek[:, 267:268], ALU.mult))
        ph.add("vector", lambda h: h.tensor_tensor(
            ek[:, 399:400], ek[:, 396:397], ek[:, 397:398], ALU.add))
        ph.add("vector", lambda h, n=n: h.tensor_scalar(
            ek[:, 270:270 + n], ek[:, 0:n], ek[:, 398:399], None, ALU.mult))
        ph.add("vector", lambda h, n=n: h.tensor_scalar(
            ek[:, 334:334 + n], ek[:, 64:64 + n], ek[:, 399:400], None, ALU.mult))
        ph.add("vector", lambda h, n=n: h.tensor_scalar(
            ek[:, 334:334 + n], ek[:, 334:334 + n], -1.0, None, ALU.mult))
        ph.add("vector", lambda h, n=n: h.tensor_scalar(
            ek[:, 302:302 + n], ek[:, 0:n], ek[:, 399:400], None, ALU.mult))
        ph.add("vector", lambda h, n=n: h.tensor_scalar(
            ek[:, 366:366 + n], ek[:, 64:64 + n], ek[:, 398:399], None, ALU.mult))
        ph.add("vector", lambda h, q=q, n=n: h.tensor_tensor(
            ek[:, q:q + n], ek[:, 270:270 + n], ek[:, 334:334 + n], ALU.add))
        ph.add("vector", lambda h, q=q, n=n: h.tensor_tensor(
            ek[:, 64 + q:64 + q + n], ek[:, 302:302 + n], ek[:, 366:366 + n], ALU.add))
        q *= 2
    ph.wait("vector", gp)
    rev_r = bass.AP(ek.handle if hasattr(ek, "handle") else t["EKER"], 63, [[400, 64], [-1, 64]])
    rev_i = bass.AP(t["EKER"], 64 + 63, [[400, 64], [-1, 64]])
    ph.add("vector", lambda h: h.tensor_copy(t["PKF"][:, 0:64], rev_r))
    ph.add("vector", lambda h: h.tensor_copy(t["PKF"][:, 192:192 + 64], rev_i))
    ph.addi("vector", lambda h: h.tensor_scalar(
        t["PKF"][:, 384:384 + 64], rev_i, -1.0, None, ALU.mult), dv)
    cx.n_pkf_done = dv.n



def emit_p1(cx, ph):
    t = cx.t
    ph.wait("scalar", cx.sd)
    ph.wait("tensor", cx.sd)
    ph.wait("tensor", cx.gd)
    for w0, wn in W512:
        r = ln_rep(cx, ph, [cx.res(w0, wn, c) for c in range(8)], wn, rho=True)
        ph.wait("gpsimd", cx.dv)
        ph.wait("gpsimd", cx.sc)
        for q, part in (("m", 0), ("rstd", 32), ("rho", 64)):
            src = {"m": r["mb"], "rstd": r["rstd"], "rho": r["rho"]}[q]
            ap = bass.AP(src.tensor, src.offset, [[src.ap[0][0], 1], [1, wn]])
            dst = cx.row("ROWS", part, w0, wn)
            ph.addi("gpsimd", lambda h, d=dst, s2=ap: h.dma_start(d, s2), cx.gd, 16)
        ph.wait("scalar", cx.gd)
        ph.wait("vector", cx.gd)



def emit_p2(cx, ph):
    t = cx.t
    pe, dv, sc, sd, gd = cx.pe, cx.dv, cx.sc, cx.sd, cx.gd
    ph.wait("tensor", gd)

    def beta_mm(dst_ap, wn):
        pb = t["PSC"][:, 0:wn]
        ph.wait("tensor", dv)
        for c in range(8):
            b = lambda h, c=c, pb=pb, wn=wn: h.matmul(
                pb, cx.wr(0)[:, c * 128:(c + 1) * 128],
                t["SCB2"][:, c * wn:(c + 1) * wn], start=(c == 0), stop=(c == 7))
            (ph.addi("tensor", b, pe) if c == 7 else ph.add("tensor", b))
        ph.wait("vector", pe)
        ph.addi("vector", lambda h, pb=pb, d=dst_ap: h.tensor_copy(d, pb), dv)

    def gate_mm(w0, wn):
        for half in range(2):
            if half == 0:
                ph.wait("tensor", dv)
            ph.wait("tensor", sc)
            for mm in range(4):
                m = half * 4 + mm
                for c in range(8):
                    b = lambda h, m=m, mm=mm, c=c, wn=wn: h.matmul(
                        t["PSA"][:, mm * 512: mm * 512 + wn],
                        cx.wr(1 + c)[:, m * 128:(m + 1) * 128],
                        t["SCB2"][:, c * wn:(c + 1) * wn], start=(c == 0), stop=(c == 7))
                    (ph.addi("tensor", b, pe) if (mm == 3 and c == 7) else ph.add("tensor", b))
            ph.wait("scalar", pe)
            for mm in range(4):
                m = half * 4 + mm
                b = lambda h, m=m, mm=mm, w0=w0, wn=wn: h.activation(
                    cx.xng(w0, wn, m), t["PSA"][:, mm * 512: mm * 512 + wn],
                    AF.Sigmoid, bias=t["VEC"][:, 32 + m: 33 + m])
                (ph.addi("scalar", b, sc) if mm == 3 else ph.add("scalar", b))

    for w0, wn in W512:
        ph.wait("tensor", dv)
        pm = t["PSB"][:, 0:wn]
        pb = t["PSC"][:, 0:wn]
        ph.add("tensor", lambda h, pm=pm, w0=w0, wn=wn: h.matmul(
            pm, t["ONES1"][0:1, :], cx.row("ROWS", 0, w0, wn), start=True, stop=True))
        ph.addi("tensor", lambda h, pb=pb, w0=w0, wn=wn: h.matmul(
            pb, t["ONES1"][64:65, :], cx.row("ROWS", 64, w0, wn), start=True, stop=True,
            tile_position=(64, 0)), pe)
        ph.wait("vector", pe)
        for c in range(8):
            ph.add("vector", lambda h, c=c, pm=pm, w0=w0, wn=wn: h.tensor_tensor(
                t["SCB2"][:, c * wn: c * wn + wn], cx.res(w0, wn, c), pm, ALU.subtract))
        for c in range(8):
            ph.add("vector", lambda h, c=c, pb=pb, wn=wn: h.tensor_tensor(
                t["SCB2"][:, c * wn: c * wn + wn], t["SCB2"][:, c * wn: c * wn + wn], pb, ALU.mult))
        for c in range(8):
            b = lambda h, c=c, wn=wn: h.tensor_scalar(
                t["SCB2"][:, c * wn: (c + 1) * wn], t["SCB2"][:, c * wn: (c + 1) * wn],
                t["VEC"][:, 0 + c: 1 + c], t["VEC"][:, 8 + c: 9 + c], ALU.mult, ALU.add)
            (ph.addi("vector", b, dv) if c == 7 else ph.add("vector", b))
        if w0 < TC:
            beta_mm(cx.beta_main(w0, wn), wn)
        gate_mm(w0, wn)

    for wi in range(8):
        w0 = wi * 256
        ph.wait("sync", sc)
        ph.wait("sync", dv)
        ph.addi("sync", lambda h, w0=w0: h.dma_start(
            t["SCF"][:, 0:2048].rearrange("p (c n) -> p c n", c=8),
            t["xpre"][:, :, w0:w0 + 256]), sd, 16)
        ph.wait("scalar", sd)
        ph.wait("vector", sd)
        ph.wait("tensor", sd)
        srcs = [t["SCF"][:, c * 256:(c + 1) * 256] for c in range(8)]
        r = ln_rep(cx, ph, srcs, 256, rho=True)
        ph.wait("vector", sc)
        for c in range(8):
            ph.add("vector", lambda h, c=c, r=r: h.tensor_tensor(
                t["SCB2"][:, c * 256: (c + 1) * 256], srcs[c], r["m_rep"], ALU.subtract))
        for c in range(8):
            ph.add("vector", lambda h, c=c, r=r: h.tensor_tensor(
                t["SCB2"][:, c * 256: (c + 1) * 256], t["SCB2"][:, c * 256: (c + 1) * 256],
                r["rho"], ALU.mult))
        for c in range(8):
            b = lambda h, c=c: h.tensor_scalar(
                t["SCB2"][:, c * 256: (c + 1) * 256], t["SCB2"][:, c * 256: (c + 1) * 256],
                t["VEC"][:, 0 + c: 1 + c], t["VEC"][:, 8 + c: 9 + c], ALU.mult, ALU.add)
            (ph.addi("vector", b, dv) if c == 7 else ph.add("vector", b))
        beta_mm(cx.beta_pre(w0, 256), 256)



def emit_p3(cx, ph):
    t = cx.t
    pe, dv, sc, sd, gd, gp = cx.pe, cx.dv, cx.sc, cx.sd, cx.gd, cx.gp

    evac_n = []
    for n in range(64):
        bank = "PST" if n % 2 == 0 else "PST2"
        pst = t[bank][0:64, 0:128]
        if n >= 2:
            ph.wait("tensor", dv, evac_n[n - 2])
        ph.addi("tensor", lambda h, n=n, pst=pst: h.transpose(
            pst, cx.beta_chunk(n), t["EYEB"][:]), pe)
        ph.wait("vector", pe)
        evac_n.append(ph.addi("vector", lambda h, n=n, pst=pst: h.tensor_copy(
            t["SCB1"][0:64, n * 128:(n + 1) * 128], pst), dv))
    bett_done = cx.dv.n

    ph.wait("gpsimd", cx.dv, cx.n_pkf_done)

    def toep_dma(grp):
        base = 8192 + (grp % 2) * 3072
        for st in range(3):
            for kk in range(16):
                k = grp * 16 + kk
                src = bass.AP(t["PKF"], k * 576 + st * 192 + 63, [[576, 1], [-1, 64], [1, 64]])
                dst = bass.AP(t["SCB1"], base + st * 1024 + kk * 64, [[16640, 64], [1, 64]])
                ph.addi("gpsimd", lambda h, s2=src, d=dst: h.dma_start(d, s2), gd, 16)
        return cx.gd.n

    grp_dma_done = {0: toep_dma(0), 1: toep_dma(1)}

    intra_r_ap = lambda k: bass.AP(t["WR"], k * 64, [[14336, 64], [1, 64]])
    intra_i_ap = lambda k: bass.AP(t["WR"], 4096 + k * 64, [[14336, 64], [1, 64]])
    ph.wait("tensor", dv, bett_done)
    evac_k = []
    for grp in range(4):
        base = 8192 + (grp % 2) * 3072
        ph.wait("tensor", gd, grp_dma_done[grp])
        for kk in range(16):
            k = grp * 16 + kk
            lr = t["SCB1"][0:64, base + kk * 64: base + kk * 64 + 64]
            li = t["SCB1"][0:64, base + 1024 + kk * 64: base + 1024 + kk * 64 + 64]
            nli = t["SCB1"][0:64, base + 2048 + kk * 64: base + 2048 + kk * 64 + 64]
            br = bass.AP(t["SCB1"], k, [[16640, 64], [128, 64]])
            bi = bass.AP(t["SCB1"], 64 + k, [[16640, 64], [128, 64]])
            bb = 2 * (k % 2)
            ppr = t["PSA"][0:64, bb * 512: bb * 512 + 64]
            ppi = t["PSA"][0:64, (bb + 1) * 512: (bb + 1) * 512 + 64]
            if k >= 2:
                ph.wait("tensor", dv, evac_k[k - 2])
            ph.add("tensor", lambda h, a=ppr, l=lr, r=br: h.matmul(a, l, r, start=True, stop=False))
            ph.add("tensor", lambda h, a=ppr, l=nli, r=bi: h.matmul(a, l, r, start=False, stop=True))
            ph.add("tensor", lambda h, a=ppi, l=lr, r=bi: h.matmul(a, l, r, start=True, stop=False))
            ph.addi("tensor", lambda h, a=ppi, l=li, r=br: h.matmul(a, l, r, start=False, stop=True), pe)
            ph.wait("vector", pe)
            ph.add("vector", lambda h, k=k, a=ppr: h.tensor_copy(intra_r_ap(k), a))
            evac_k.append(ph.addi("vector", lambda h, k=k, a=ppi: h.tensor_copy(intra_i_ap(k), a), dv))
        if grp + 2 <= 3:
            ph.wait("gpsimd", pe, cx.pe.n)
            grp_dma_done[grp + 2] = toep_dma(grp + 2)

    ph.wait("tensor", dv)
    copy_done = []
    for n in range(64):
        bank = "PST" if n % 2 == 0 else "PST2"
        pt_ = t[bank][:, 0:64]
        in_ri = bass.AP(t["WR"], n, [[14336, 64], [64, 128]])
        if n >= 2:
            ph.wait("tensor", dv, copy_done[n - 2])
        ph.addi("tensor", lambda h, o=pt_, i=in_ri: h.transpose(
            o, i, t["EYEB"][0:64, 0:64]), pe)
        ph.wait("vector", pe)
        crn = t["CRCI"][0:64, n * 64:(n + 1) * 64]
        cin = t["CRCI"][64:128, n * 64:(n + 1) * 64]
        ph.add("vector", lambda h, o=crn, p=pt_: h.tensor_copy(o, p[0:64, :]))
        copy_done.append(ph.addi("vector", lambda h, o=cin, p=pt_: h.tensor_copy(
            o, p[64:128, :]), dv))
        if n == 0:
            continue
        sr = t["CRCI"][0:64, n * 64 - 1: n * 64]
        si = t["CRCI"][64:128, n * 64 - 1: n * 64]
        c0r = t["CRCI"][0:64, n * 64: n * 64 + 1]
        c0i = t["CRCI"][64:128, n * 64: n * 64 + 1]
        if n == 32:
            ph.add("vector", lambda h, a=sr, o=c0r: h.scalar_tensor_tensor(
                o, a, t["FLG"][0:64, 0:1], o, ALU.mult, ALU.add))
            ph.add("vector", lambda h, a=si, o=c0i: h.scalar_tensor_tensor(
                o, a, t["FLG"][64:128, 0:1], o, ALU.mult, ALU.add))
        else:
            ph.add("vector", lambda h, a=sr, o=c0r: h.tensor_tensor(o, o, a, ALU.add))
            ph.add("vector", lambda h, a=si, o=c0i: h.tensor_tensor(o, o, a, ALU.add))

    ph.add("vector", lambda h: h.tensor_scalar(
        t["CRCI"][:, 4096:4112], t["CRCI"][:, 0:16], t["FLG"][:, 0:1], None, ALU.mult))
    ph.addi("vector", lambda h: h.scalar_tensor_tensor(
        t["CRCI"][:, 4096:4112], t["CRCI"][:, 2048:2064], t["FLG"][:, 1:2],
        t["CRCI"][:, 4096:4112], ALU.mult, ALU.add), dv)


def emit_p4a(cx, ph):
    t = cx.t
    pe, dv, sc, sd, gd = cx.pe, cx.dv, cx.sc, cx.sd, cx.gd
    ph.wait("tensor", gd)

    for w0, wn in W256:
        ccol = (2048 + w0) if w0 < TC else 4096
        ph.wait("tensor", dv)
        pr = t["PSB"][0:64, 0:wn]
        pi = t["PSC"][0:64, 0:wn]
        ph.add("tensor", lambda h, p=pr, c=ccol, wn=wn: h.matmul(
            p, t["BDC"][0:64, :], t["CRCI"][0:64, c:c + wn], start=True, stop=True))
        ph.addi("tensor", lambda h, p=pi, c=ccol, wn=wn: h.matmul(
            p, t["BDC"][64:128, :], t["CRCI"][64:128, c:c + wn], start=True, stop=True), pe)
        ph.wait("vector", pe)
        ph.add("vector", lambda h, p=pr, wn=wn: h.tensor_copy(t["CC"][0:64, 0:wn], p))
        ph.addi("vector", lambda h, p=pi, wn=wn: h.tensor_copy(
            t["CC"][0:64, 256:256 + wn], p), dv)
        ph.wait("tensor", dv)
        pm = t["PSB"][:, 0:wn]
        ps_ = t["PSC"][:, 0:wn]
        ph.add("tensor", lambda h, p=pm, w0=w0, wn=wn: h.matmul(
            p, t["ONES1"][0:1, :], cx.row("ROWS", 0, w0, wn), start=True, stop=True))
        ph.addi("tensor", lambda h, p=ps_, w0=w0, wn=wn: h.matmul(
            p, t["ONES1"][32:33, :], cx.row("ROWS", 32, w0, wn), start=True, stop=True), pe)
        ph.wait("vector", pe)
        for c in range(8):
            ph.add("vector", lambda h, c=c, p=pm, w0=w0, wn=wn: h.tensor_tensor(
                t["SCF"][:, c * 256: c * 256 + wn], cx.res(w0, wn, c), p, ALU.subtract))
        for c in range(8):
            b = lambda h, c=c, p=ps_, wn=wn: h.tensor_tensor(
                t["SCF"][:, c * 256: c * 256 + wn], t["SCF"][:, c * 256: c * 256 + wn], p, ALU.mult)
            (ph.addi("vector", b, dv) if c == 7 else ph.add("vector", b))
        for half in range(2):
            ph.wait("tensor", dv)
            for mm in range(4):
                m = half * 4 + mm
                pa = t["PSA"][:, mm * 512: mm * 512 + wn]
                ph.add("tensor", lambda h, p=pa, m=m, wn=wn: h.matmul(
                    p, t["WOUT"][:, m * 128: (m + 1) * 128], t["CC"][0:64, 0:wn],
                    start=True, stop=False))
                b = lambda h, p=pa, m=m, wn=wn: h.matmul(
                    p, t["WOUT"][:, 1024 + m * 128: 1024 + (m + 1) * 128],
                    t["CC"][0:64, 256:256 + wn], start=False, stop=True)
                (ph.addi("tensor", b, pe) if mm == 3 else ph.add("tensor", b))
            ph.wait("vector", pe)
            for mm in range(4):
                m = half * 4 + mm
                pa = t["PSA"][:, mm * 512: mm * 512 + wn]
                ph.add("vector", lambda h, p=pa, m=m, w0=w0, wn=wn: h.tensor_tensor(
                    t["CC"][:, 0:wn] if False else t["SCB2"][:, 2048:2048 + wn],
                    cx.xng(w0, wn, m), p, ALU.mult))
                b = lambda h, m=m, wn=wn: h.tensor_tensor(
                    t["SCF"][:, m * 256: m * 256 + wn], t["SCF"][:, m * 256: m * 256 + wn],
                    t["SCB2"][:, 2048:2048 + wn], ALU.add)
                (ph.addi("vector", b, dv) if mm == 3 else ph.add("vector", b))
        resw = bass.AP(t["RES"], w0, [[8 * TT, 128], [TT, 8], [1, wn]])
        scfw = t["SCF"][:, 0:2048].rearrange("p (c n) -> p c n", c=8)[:, :, 0:wn]
        ph.add("vector", lambda h, r=resw, s2=scfw: h.tensor_tensor(r, r, s2, ALU.add))
        for c in range(8):
            b = lambda h, c=c, wn=wn: h.tensor_copy(
                t["SCB2"][:, c * wn: (c + 1) * wn], t["SCF"][:, c * 256: c * 256 + wn])
            (ph.addi("vector", b, dv) if c == 7 else ph.add("vector", b))
        x1b = [t["SCB2"][:, c * wn: (c + 1) * wn] for c in range(8)]
        r2 = ln_rep(cx, ph, x1b, wn, bf16_src=True)
        ph.wait("vector", cx.sc)
        for c in range(8):
            ph.add("vector", lambda h, c=c, r2=r2, x1b=x1b, wn=wn: h.tensor_tensor(
                t["SCB1"][:, 8448 + c * wn: 8448 + (c + 1) * wn], x1b[c],
                r2["m_rep"], ALU.subtract))
        for c in range(8):
            ph.add("vector", lambda h, c=c, r2=r2, wn=wn: h.tensor_tensor(
                t["SCB1"][:, 8448 + c * wn: 8448 + (c + 1) * wn],
                t["SCB1"][:, 8448 + c * wn: 8448 + (c + 1) * wn], r2["rstd"], ALU.mult))
        for c in range(8):
            b = lambda h, c=c, w0=w0, wn=wn: h.tensor_scalar(
                cx.xng(w0, wn, c), t["SCB1"][:, 8448 + c * wn: 8448 + (c + 1) * wn],
                t["VEC"][:, 16 + c: 17 + c], t["VEC"][:, 24 + c: 25 + c], ALU.mult, ALU.add)
            (ph.addi("vector", b, dv) if c == 7 else ph.add("vector", b))


W512M = [(0, 512), (512, 512), (1024, 512), (1536, 512), (2048, 16)]


def emit_p4b(cx, ph):
    t = cx.t
    pe, dv, sc, sd, gd = cx.pe, cx.dv, cx.sc, cx.sd, cx.gd

    def gdma(slot, src):
        return ph.addi("gpsimd", lambda h, d=cx.wr(slot), s2=src: h.dma_start(d, s2), gd, 16)

    for g in range(4):
        ph.wait("gpsimd", pe)
        dma_done = []
        for mi in range(8):
            m = g * 8 + mi
            dma_done.append(gdma(mi, t["w_mlp1T"][m].rearrange("p c m -> p (c m)")))
        hid_done = []
        gidx = 0
        for mi in range(8):
            m = g * 8 + mi
            cx.qf.feed(ph, 2)
            ph.wait("tensor", gd, dma_done[mi])
            for wi, (w0, wn) in enumerate(W512M):
                pa = t["PSA"][:, (gidx % 4) * 512: (gidx % 4) * 512 + wn]
                gidx += 1
                if len(hid_done) >= 4:
                    ph.wait("tensor", sc, hid_done[-4])
                for c in range(8):
                    b = lambda h, p=pa, mi=mi, c=c, w0=w0, wn=wn: h.matmul(
                        p, cx.wr(mi)[:, c * 128: (c + 1) * 128],
                        cx.xng(w0, wn, c), start=(c == 0), stop=(c == 7))
                    (ph.addi("tensor", b, pe) if c == 7 else ph.add("tensor", b))
                ph.wait("scalar", pe)
                hid_done.append(ph.addi("scalar", lambda h, p=pa, mi=mi, m=m, w0=w0, wn=wn: h.activation(
                    t["SCB1"][:, mi * 2080 + w0: mi * 2080 + w0 + wn], p, AF.Silu,
                    bias=t["VECB"][:, m: m + 1]), sc))
        dma2 = []
        ph.wait("gpsimd", sc, hid_done[-1])
        for mi in range(8):
            kc = g * 8 + mi
            dma2.append(gdma(mi, t["w_mlp2T"][kc]))
        ph.wait("tensor", gd, dma2[-1])
        ph.wait("tensor", sc, hid_done[-1])
        for w0, wn in W512M:
            cx.qf.feed(ph, 2)
            for mo in range(8):
                bank = "PSB" if mo % 2 == 0 else "PSC"
                pz = t[bank][:, 0:wn]
                ph.wait("tensor", dv)
                for mi in range(8):
                    b = lambda h, p=pz, mi=mi, mo=mo, w0=w0, wn=wn: h.matmul(
                        p, cx.wr(mi)[:, mo * 128: (mo + 1) * 128],
                        t["SCB1"][:, mi * 2080 + w0: mi * 2080 + w0 + wn],
                        start=(mi == 0), stop=(mi == 7))
                    (ph.addi("tensor", b, pe) if mi == 7 else ph.add("tensor", b))
                ph.wait("vector", pe)
                resw = t["RES"][:, mo * TT + w0: mo * TT + w0 + wn]
                if g == 0:
                    b = lambda h, p=pz, r=resw, mo=mo: h.scalar_tensor_tensor(
                        r, p, t["VEC"][:, 40 + mo: 41 + mo], r, ALU.add, ALU.add)
                else:
                    b = lambda h, p=pz, r=resw: h.tensor_tensor(r, r, p, ALU.add)
                ph.addi("vector", b, dv)


def emit_p5(cx, ph):
    t = cx.t
    pe, dv, sc, sd, gd = cx.pe, cx.dv, cx.sc, cx.sd, cx.gd

    emit_plain_ln(cx, ph)

    ph.wait("gpsimd", pe)
    wg_done = ph.addi("gpsimd", lambda h: h.dma_start(
        cx.wr(13, 128), t["WgT"][:].rearrange("p c m -> p (c m)")), gd, 16)
    ph.wait("tensor", gd, wg_done)
    for wi, (w0, wn) in enumerate(W512):
        ph.wait("tensor", dv)
        psc = t["PSC"][0:16, 0:wn]
        for c in range(8):
            b = lambda h, c=c, p=psc, w0=w0, wn=wn: h.matmul(
                p, cx.wr(13, 128)[:, c * 16:(c + 1) * 16],
                cx.xng(w0, wn, c), start=(c == 0), stop=(c == 7))
            (ph.addi("tensor", b, pe) if c == 7 else ph.add("tensor", b))
        ph.wait("vector", pe)
        ph.addi("vector", lambda h, p=psc, wn=wn: h.tensor_copy(t["SCF"][0:16, 0:wn], p), dv)
        ntile = (wn + 127) // 128
        for tt_ in range(ntile):
            tn = min(128, wn - tt_ * 128)
            ph.wait("tensor", dv)
            psb = t["PSB"][0:tn, 0:16]
            ph.addi("tensor", lambda h, p=psb, tt_=tt_, tn=tn: h.transpose(
                p, t["SCF"][0:16, tt_ * 128: tt_ * 128 + tn], t["EYEF"][0:16, 0:16]), pe)
            ph.wait("vector", pe)
            TOK = t["SCF"][0:tn, 512:528]
            CW = t["SCF"][0:tn, 528:544]
            R1 = t["SCF"][0:tn, 544:545]
            R2 = t["SCF"][0:tn, 545:546]
            ph.add("vector", lambda h, p=psb, TOK=TOK: h.tensor_copy(TOK, p))
            ph.add("vector", lambda h, TOK=TOK, R1=R1: h.reduce_max(R1, TOK, AX.X))
            ph.addi("vector", lambda h, TOK=TOK, CW=CW, R1=R1: h.tensor_scalar(
                CW, TOK, R1, None, ALU.subtract), dv)
            ph.wait("scalar", dv)
            ph.addi("scalar", lambda h, CW=CW: h.activation(CW, CW, AF.Exp), cx.sc)
            ph.wait("vector", cx.sc)
            ph.add("vector", lambda h, CW=CW, R2=R2: h.tensor_reduce(R2, CW, AX.X, ALU.add))
            ph.add("vector", lambda h, R1=R1, R2=R2: h.reciprocal(R1, R2))
            ph.add("vector", lambda h, TOK=TOK, CW=CW, R1=R1: h.tensor_scalar(TOK, CW, R1, None, ALU.mult))
            ph.add("vector", lambda h, TOK=TOK, R1=R1: h.reduce_max(R1, TOK, AX.X))
            ph.add("vector", lambda h, TOK=TOK, CW=CW, R1=R1: h.tensor_scalar(CW, TOK, R1, None, ALU.subtract))
            ph.add("vector", lambda h, CW=CW: h.tensor_scalar(CW, CW, 1e38, None, ALU.mult))
            ph.add("vector", lambda h, CW=CW: h.tensor_scalar(CW, CW, -1.0, 1.0, ALU.max, ALU.add))
            ph.add("vector", lambda h, TOK=TOK, CW=CW: h.scalar_tensor_tensor(CW, CW, -1e9, TOK, ALU.mult, ALU.add))
            ph.add("vector", lambda h, CW=CW, R2=R2: h.reduce_max(R2, CW, AX.X))
            ph.add("vector", lambda h, TOK=TOK, CW=CW, R2=R2: h.tensor_scalar(CW, TOK, R2, None, ALU.subtract))
            ph.add("vector", lambda h, CW=CW: h.tensor_scalar(CW, CW, 1e38, None, ALU.mult))
            ph.add("vector", lambda h, CW=CW: h.tensor_scalar(CW, CW, -1.0, 1.0, ALU.max, ALU.add))
            ph.add("vector", lambda h, CW=CW: h.tensor_scalar(CW, CW, 1.0, None, ALU.min))
            ph.addi("vector", lambda h, TOK=TOK, CW=CW: h.tensor_tensor(CW, CW, TOK, ALU.mult), dv)
            pst = t["PSB"][0:16, 256:256 + tn]
            ph.wait("tensor", dv)
            ph.addi("tensor", lambda h, p=pst, tn=tn: h.transpose(
                p, bass.AP(t["SCF"], 528, [[2048, tn], [1, 16]]), t["EYEF"][0:tn, 0:tn]), pe)
            ph.wait("vector", pe)
            col = w0 + tt_ * 128
            ph.addi("vector", lambda h, p=pst, col=col, tn=tn: h.tensor_copy(
                t["SCB2"][0:16, col:col + tn], p), dv)

    ys_dma = {}
    for j in range(8):
        su, sg = (j % 2) * 2, (j % 2) * 2 + 1
        if j >= 2:
            ph.wait("gpsimd", pe)
        ys_dma[j] = (ph.addi("gpsimd", lambda h, j=j, su=su: h.dma_start(
            cx.wr(su), t["Ws1T"][j].rearrange("p c m -> p (c m)")), gd, 16),
            ph.addi("gpsimd", lambda h, j=j, sg=sg: h.dma_start(
                cx.wr(sg), t["Ws1T"][j + 8].rearrange("p c m -> p (c m)")), gd, 16))
        su_, sg_ = su, sg
        ph.wait("tensor", gd, ys_dma[j][1])
        for w0, wn in W512:
            cx.qf.feed(ph, 2)
            ph.wait("tensor", cx.sc)
            ph.wait("tensor", dv)
            pu = t["PSA"][:, 0:wn]
            pg = t["PSA"][:, 512:512 + wn]
            for c in range(8):
                ph.add("tensor", lambda h, c=c, p=pu, su=su_, w0=w0, wn=wn: h.matmul(
                    p, cx.wr(su)[:, c * 128:(c + 1) * 128], cx.xng(w0, wn, c),
                    start=(c == 0), stop=(c == 7)))
            for c in range(8):
                b = lambda h, c=c, p=pg, sg=sg_, w0=w0, wn=wn: h.matmul(
                    p, cx.wr(sg)[:, c * 128:(c + 1) * 128], cx.xng(w0, wn, c),
                    start=(c == 0), stop=(c == 7))
                (ph.addi("tensor", b, pe) if c == 7 else ph.add("tensor", b))
            ph.wait("scalar", pe)
            ph.addi("scalar", lambda h, p=pg, wn=wn: h.activation(
                t["SCB2"][:, 2176:2176 + wn], p, AF.Silu), cx.sc)
            ph.wait("vector", cx.sc)
            b = lambda h, p=pu, j=j, w0=w0, wn=wn: h.tensor_tensor(
                t["SCB1"][:, j * 2080 + w0: j * 2080 + w0 + wn], p,
                t["SCB2"][:, 2176:2176 + wn], ALU.mult)
            ph.addi("vector", b, dv)
    ws2_dma = []
    ph.wait("gpsimd", dv)
    for kc in range(8):
        ws2_dma.append(ph.addi("gpsimd", lambda h, kc=kc: h.dma_start(
            cx.wr(4 + kc), t["Ws2T"][kc]), gd, 16))
    ph.wait("tensor", gd, ws2_dma[-1])
    ph.wait("tensor", dv)
    for w0, wn in W512:
        for mo in range(8):
            bank = "PSB" if mo % 2 == 0 else "PSC"
            pz = t[bank][:, 0:wn]
            ph.wait("tensor", dv)
            for kc in range(8):
                b = lambda h, p=pz, kc=kc, mo=mo, w0=w0, wn=wn: h.matmul(
                    p, cx.wr(4 + kc)[:, mo * 128:(mo + 1) * 128],
                    t["SCB1"][:, kc * 2080 + w0: kc * 2080 + w0 + wn],
                    start=(kc == 0), stop=(kc == 7))
                (ph.addi("tensor", b, pe) if kc == 7 else ph.add("tensor", b))
            ph.wait("vector", pe)
            resw = t["RES"][:, mo * TT + w0: mo * TT + w0 + wn]
            ph.addi("vector", lambda h, p=pz, r=resw: h.tensor_tensor(r, r, p, ALU.add), dv)

    for e in range(16):
        ph.wait("gpsimd", pe)
        d1 = []
        for mp in range(8):
            d1.append(ph.addi("gpsimd", lambda h, e=e, mp=mp: h.dma_start(
                cx.wr(mp), t["W1e"][e, mp].rearrange("p c m -> p (c m)")), gd, 16))
        d2 = []
        for kp in range(4):
            d2.append(ph.addi("gpsimd", lambda h, e=e, kp=kp: h.dma_start(
                cx.wr(8 + kp), t["W2e"][e, kp]), gd, 16))
        for w0, wn in W512:
            cx.qf.feed(ph, 2)
            cwb = t["SCB2"][:, 2688:2688 + wn]
            ph.wait("gpsimd", dv)
            cwb_done = ph.addi("gpsimd", lambda h, e=e, w0=w0, wn=wn, d=cwb: h.dma_start(
                d, bass.AP(t["SCB2"], 2048 * 0 + e * 4096 + w0, [[4096, 1], [0, 128], [1, wn]])), gd, 16)
            for j in range(4):
                ph.wait("tensor", gd, d1[j + 4])
                ph.wait("tensor", cx.sc)
                ph.wait("tensor", dv)
                pu = t["PSA"][:, (j % 2) * 1024: (j % 2) * 1024 + wn]
                pg = t["PSA"][:, (j % 2) * 1024 + 512: (j % 2) * 1024 + 512 + wn]
                for c in range(8):
                    ph.add("tensor", lambda h, c=c, p=pu, j=j, w0=w0, wn=wn: h.matmul(
                        p, cx.wr(j)[:, c * 128:(c + 1) * 128], cx.xng(w0, wn, c),
                        start=(c == 0), stop=(c == 7)))
                for c in range(8):
                    b = lambda h, c=c, p=pg, j=j, w0=w0, wn=wn: h.matmul(
                        p, cx.wr(j + 4)[:, c * 128:(c + 1) * 128], cx.xng(w0, wn, c),
                        start=(c == 0), stop=(c == 7))
                    (ph.addi("tensor", b, pe) if c == 7 else ph.add("tensor", b))
                ph.wait("scalar", pe)
                ph.wait("scalar", gd, cwb_done)
                ph.addi("scalar", lambda h, p=pg, wn=wn: h.activation(
                    t["SCB2"][:, 3584:3584 + wn], p, AF.Silu), cx.sc)
                ph.wait("vector", cx.sc)
                ph.add("vector", lambda h, p=pu, j=j, wn=wn: h.tensor_tensor(
                    t["SCB1"][:, j * 512: j * 512 + wn], p,
                    t["SCB2"][:, 3584:3584 + wn], ALU.mult))
                ph.addi("vector", lambda h, j=j, wn=wn, d=cwb: h.tensor_tensor(
                    t["SCB1"][:, j * 512: j * 512 + wn],
                    t["SCB1"][:, j * 512: j * 512 + wn], d, ALU.mult), dv)
            ph.wait("tensor", gd, d2[-1])
            ph.wait("tensor", dv)
            for mo in range(8):
                bank = "PSB" if mo % 2 == 0 else "PSC"
                pz = t[bank][:, 0:wn]
                ph.wait("tensor", dv)
                for kc in range(4):
                    b = lambda h, p=pz, kc=kc, mo=mo, wn=wn: h.matmul(
                        p, cx.wr(8 + kc)[:, mo * 128:(mo + 1) * 128],
                        t["SCB1"][:, kc * 512: kc * 512 + wn],
                        start=(kc == 0), stop=(kc == 3))
                    (ph.addi("tensor", b, pe) if kc == 3 else ph.add("tensor", b))
                ph.wait("vector", pe)
                resw = t["RES"][:, mo * TT + w0: mo * TT + w0 + wn]
                ph.addi("vector", lambda h, p=pz, r=resw: h.tensor_tensor(r, r, p, ALU.add), dv)


def emit_p6(cx, ph):
    t = cx.t
    pe, dv, sc, sd, gd = cx.pe, cx.dv, cx.sc, cx.sd, cx.gd
    cgrid = cx.wr(13, 1024)
    ph.addi("gpsimd", lambda h: h.iota(cgrid, [[0, 16], [1, 16], [0, 4]],
                                       base=-1, channel_multiplier=0,
                                       allow_small_or_imprecise_dtypes=True), cx.gp)

    emit_plain_ln(cx, ph)

    ph.wait("gpsimd", pe)
    dval = []
    for mp in range(8):
        dval.append(ph.addi("gpsimd", lambda h, mp=mp: h.dma_start(
            cx.wr(mp), t["W_valueT"][mp].rearrange("p c m -> p (c m)")), gd, 16))
    VT = t["SCB2"][:, 3900:4012]
    for mp in range(8):
        ph.wait("tensor", gd, dval[mp])
        ph.wait("tensor", dv)
        pv = t["PSB"][:, 0:14]
        for c in range(8):
            b = lambda h, c=c, mp=mp, p=pv: h.matmul(
                p, cx.wr(mp)[:, c * 128:(c + 1) * 128],
                t["XNG"][:, c * TT + 2048: c * TT + 2062], start=(c == 0), stop=(c == 7))
            (ph.addi("tensor", b, pe) if c == 7 else ph.add("tensor", b))
        ph.wait("vector", pe)
        ph.addi("vector", lambda h, p=pv, mp=mp: h.tensor_scalar(
            VT[:, mp * 14:(mp + 1) * 14], p, t["VEC2"][:, mp: mp + 1], None, ALU.add), dv)
    for c in range(8):
        pst = t["PST"][0:14, 0:128]
        ph.wait("tensor", dv)
        ph.addi("tensor", lambda h, c=c, p=pst: h.transpose(
            p, VT[:, c * 14:(c + 1) * 14], t["EYEB"][:]), pe)
        ph.wait("vector", pe)
        ph.addi("vector", lambda h, c=c, p=pst: h.tensor_copy(
            t["V14"][0:14, c * 128:(c + 1) * 128], p), dv)
    ph.addi("gpsimd", lambda h: h.memset(t["BDV"][:], 0.0), cx.gp)
    ph.wait("sync", dv)
    ph.wait("sync", cx.gp)
    for a in range(4):
        src0 = bass.AP(t["V14"], 0, [[1024, 14], [128, 8], [1, 64]])
        dst0 = bass.AP(t["BDV"], a * 32 * 1024, [[1024, 14], [128, 8], [1, 64]])
        src1 = bass.AP(t["V14"], 64, [[1024, 14], [128, 8], [1, 64]])
        dst1 = bass.AP(t["BDV"], (a * 32 + 16) * 1024 + 64, [[1024, 14], [128, 8], [1, 64]])
        ph.addi("sync", lambda h, d=dst0, s2=src0: h.dma_start(d, s2), sd, 16)
        ph.addi("sync", lambda h, d=dst1, s2=src1: h.dma_start(d, s2), sd, 16)
    bdv_done = cx.sd.n

    doa = []
    for pcs in range(2):
        doa.append(ph.addi("gpsimd", lambda h, pcs=pcs: h.dma_start(
            cx.wr(8 + pcs), t["W_oaT"][pcs].rearrange("p c m -> p (c m)")), gd, 16))
    ph.wait("tensor", gd, doa[-1])
    for wi, (w0, wn) in enumerate(W512):
        ph.wait("tensor", dv)
        po = t["PSA"][:, 0:wn]
        pa_ = t["PSA"][0:64, 512:512 + wn]
        for c in range(8):
            ph.add("tensor", lambda h, c=c, p=po, w0=w0, wn=wn: h.matmul(
                p, cx.wr(8)[:, c * 128:(c + 1) * 128], cx.xng(w0, wn, c),
                start=(c == 0), stop=(c == 7)))
        for c in range(8):
            b = lambda h, c=c, p=pa_, w0=w0, wn=wn: h.matmul(
                p, cx.wr(9)[:, c * 128: c * 128 + 64], cx.xng(w0, wn, c),
                start=(c == 0), stop=(c == 7))
            (ph.addi("tensor", b, pe) if c == 7 else ph.add("tensor", b))
        ph.wait("vector", pe)
        ph.add("vector", lambda h, p=po, wn=wn: h.tensor_scalar(
            t["SCF"][:, 0:wn], p, t["VEC2"][:, 24:25], None, ALU.add))
        ph.addi("vector", lambda h, p=pa_, wn=wn: h.tensor_scalar(
            t["SCF"][0:64, 512:512 + wn], p, bass.AP(t["VEC2"], 25, [[32, 64], [1, 1]]),
            None, ALU.add), dv)
        ntile = (wn + 127) // 128
        for tt_ in range(ntile):
            tn = min(128, wn - tt_ * 128)
            gtile = (w0 + tt_ * 128) // 128
            cx.qf.feed(ph, 3)
            ph.wait("tensor", dv)
            pot = t["PSB"][0:tn, 0:128]
            pat = t["PSC"][0:tn, 0:64]
            ph.add("tensor", lambda h, p=pot, tt_=tt_, tn=tn: h.transpose(
                p, t["SCF"][:, tt_ * 128: tt_ * 128 + tn], t["EYEF"][:]))
            ph.addi("tensor", lambda h, p=pat, tt_=tt_, tn=tn: h.transpose(
                p, t["SCF"][0:64, 512 + tt_ * 128: 512 + tt_ * 128 + tn],
                t["EYEF"][0:64, 0:64]), pe)
            ph.wait("vector", pe)
            OT = t["SCF"][0:tn, 1024:1152]
            AT = t["SCF"][0:tn, 1152:1216]
            AM = t["SCF"][0:tn, 1216:1232]
            AW = t["SCF"][0:tn, 1232:1296]
            IX = t["SCF"][0:tn, 1296:1360]
            IY = t["SCF"][0:tn, 1360:1424]
            WX = t["SCF"][0:tn, 1424:1488]
            SS = t["SCF"][0:tn, 1488:1744]
            UU = t["SCF"][0:tn, 1744:1808]
            ph.add("vector", lambda h, p=pot, OT=OT: h.tensor_copy(OT, p))
            ph.addi("vector", lambda h, p=pat, AT=AT: h.tensor_copy(AT, p), dv)
            ph.add("vector", lambda h, AT=AT, AM=AM, tn=tn: h.tensor_reduce(
                AM, AT.rearrange("p (a b) -> p a b", b=4), AX.X, ALU.max))
            amb = lambda AM, tn: bass.AP(AM.tensor if False else t["SCF"], 0, [[0, 0]])
            ph.addi("vector", lambda h, AT=AT, AM=AM, tn=tn: h.tensor_tensor(
                AT, AT, bass.AP(t["SCF"], 1216, [[2048, tn], [1, 16], [0, 4]]), ALU.subtract), dv)
            ph.wait("scalar", dv)
            ph.addi("scalar", lambda h, AT=AT: h.activation(AT, AT, AF.Exp), cx.sc)
            ph.wait("vector", cx.sc)
            ph.add("vector", lambda h, AT=AT, AM=AM: h.tensor_reduce(
                AM, AT.rearrange("p (a b) -> p a b", b=4), AX.X, ALU.add))
            ph.add("vector", lambda h, AM=AM: h.reciprocal(AM, AM))
            ph.add("vector", lambda h, AT=AT, AW=AW, tn=tn: h.tensor_tensor(
                AW, AT, bass.AP(t["SCF"], 1216, [[2048, tn], [1, 16], [0, 4]]), ALU.mult))
            ph.add("vector", lambda h, OT=OT, IX=IX, gtile=gtile, tn=tn: h.tensor_scalar(
                IX, bass.AP(t["SCF"], 1024, [[2048, tn], [2, 64]]),
                t["REFQ"][0:tn, gtile:gtile + 1], None, ALU.add))
            ph.add("vector", lambda h, IX=IX: h.tensor_scalar(IX, IX, -0.5, None, ALU.add))
            ph.add("vector", lambda h, OT=OT, IY=IY, tn=tn: h.tensor_scalar(
                IY, bass.AP(t["SCF"], 1025, [[2048, tn], [2, 64]]), -0.5, None, ALU.add))
            ph.add("vector", lambda h, IX=IX, WX=WX: h.tensor_scalar(WX, IX, -1.0, None, ALU.mult))
            ph.add("vector", lambda h, IX=IX, WX=WX: h.tensor_tensor(WX, WX, IX, ALU.max))
            ph.add("vector", lambda h, WX=WX: h.tensor_scalar(WX, WX, -1.0, 1.0, ALU.mult, ALU.add))
            ph.add("vector", lambda h, WX=WX: h.tensor_scalar(WX, WX, 0.0, None, ALU.max))
            ph.add("vector", lambda h, AW=AW, WX=WX: h.tensor_tensor(AW, AW, WX, ALU.mult))
            ph.addi("vector", lambda h, AW=AW, IY=IY, UU=UU: h.tensor_tensor(UU, AW, IY, ALU.mult), dv)
            TMP = t["PSA"][0:tn, 1024:2048]
            ph.wait("vector", cx.gp)
            gb = bass.AP(t["SCF"], 1232, [[2048, tn], [4, 16], [0, 16], [1, 4]])
            ub = bass.AP(t["SCF"], 1744, [[2048, tn], [4, 16], [0, 16], [1, 4]])
            ph.add("vector", lambda h, TMP=TMP, gb=gb, tn=tn: h.tensor_tensor(
                TMP, gb, cx.wr(13, 1024)[0:tn, :], ALU.mult))
            ph.add("vector", lambda h, TMP=TMP, ub=ub: h.tensor_tensor(TMP, ub, TMP, ALU.subtract))
            ph.add("vector", lambda h, TMP=TMP: h.tensor_scalar(TMP, TMP, 0.0, None, ALU.max))
            ph.add("vector", lambda h, TMP=TMP, SS=SS, tn=tn: h.tensor_reduce(
                SS, TMP.rearrange("p (a b) -> p a b", b=4), AX.X, ALU.add))
            sA0 = bass.AP(t["SCF"], 1488, [[2048, tn], [16, 16], [1, 14]])
            sA1 = bass.AP(t["SCF"], 1489, [[2048, tn], [16, 16], [1, 14]])
            sA2 = bass.AP(t["SCF"], 1490, [[2048, tn], [16, 16], [1, 14]])
            tS = t["SCF"][0:tn, 1808:2032].rearrange("p (a b) -> p a b", b=14)
            dA = bass.AP(t["SCB1"], 12576, [[16640, tn], [16, 16], [1, 14]])
            ph.add("vector", lambda h, tS=tS, sA0=sA0, sA2=sA2: h.tensor_tensor(
                tS, sA0, sA2, ALU.add))
            ph.addi("vector", lambda h, dA=dA, sA1=sA1, tS=tS: h.scalar_tensor_tensor(
                dA, sA1, -2.0, tS, ALU.mult, ALU.add), dv)
            tstart = w0 + tt_ * 128
            for hhalf in range(2):
                bank = "PST" if hhalf == 0 else "PST2"
                pq_ = t[bank][0:128, 0:tn]
                ph.wait("tensor", dv)
                ph.addi("tensor", lambda h, p=pq_, hhalf=hhalf, tn=tn: h.transpose(
                    p, t["SCB1"][0:tn, 12576 + hhalf * 128: 12576 + (hhalf + 1) * 128],
                    t["EYEB"][0:tn, 0:tn]), pe)
                ph.wait("vector", pe)
                ph.addi("vector", lambda h, p=pq_, hhalf=hhalf, tstart=tstart, tn=tn: h.tensor_copy(
                    t["SCB1"][:, 8448 + hhalf * 2064 + tstart: 8448 + hhalf * 2064 + tstart + tn], p), dv)

    dwo = []
    ph.wait("gpsimd", pe)
    for mp in range(8):
        dwo.append(ph.addi("gpsimd", lambda h, mp=mp: h.dma_start(
            cx.wr(mp), t["W_out1T"][mp].rearrange("p c m -> p (c m)")), gd, 16))
    ph.wait("tensor", sd, bdv_done)
    for w0, wn in W512:
        for p_ in range(8):
            bank = "PSB" if p_ % 2 == 0 else "PSC"
            pz = t[bank][:, 0:wn]
            ph.wait("tensor", dv)
            afm_col = 8448 + (p_ // 4) * 2064 + w0
            arow = 32 * (p_ % 4)
            b = lambda h, p=pz, p_=p_, afm_col=afm_col, arow=arow, wn=wn: h.matmul(
                p, t["BDV"][arow:arow + 32, p_ * 128:(p_ + 1) * 128],
                t["SCB1"][arow:arow + 32, afm_col:afm_col + wn], start=True, stop=True,
                tile_position=(arow, 0))
            ph.addi("tensor", b, pe)
            ph.wait("vector", pe)
            ph.addi("vector", lambda h, p=pz, p_=p_, wn=wn: h.tensor_copy(
                t["SCB1"][:, p_ * 512: p_ * 512 + wn], p), dv)
        for mo in range(8):
            bank = "PSB" if mo % 2 == 0 else "PSC"
            pz = t[bank][:, 0:wn]
            ph.wait("tensor", dv)
            ph.wait("tensor", gd, dwo[-1])
            for kc in range(8):
                b = lambda h, p=pz, kc=kc, mo=mo, wn=wn: h.matmul(
                    p, cx.wr(mo)[:, kc * 128:(kc + 1) * 128],
                    t["SCB1"][:, kc * 512: kc * 512 + wn], start=(kc == 0), stop=(kc == 7))
                (ph.addi("tensor", b, pe) if kc == 7 else ph.add("tensor", b))
            ph.wait("vector", pe)
            ph.addi("vector", lambda h, p=pz, mo=mo, w0=w0, wn=wn: h.tensor_scalar(
                cx.xng(w0, wn, mo), p, t["VEC2"][:, 8 + mo: 9 + mo], None, ALU.add), dv)

    dwp = []
    ph.wait("gpsimd", pe)
    for mp in range(8):
        dwp.append(ph.addi("gpsimd", lambda h, mp=mp: h.dma_start(
            cx.wr(mp), t["W_projT"][mp].rearrange("p c m -> p (c m)")), gd, 16))
    ph.wait("tensor", gd, dwp[-1])
    for w0, wn in W512:
        for mo in range(8):
            bank = "PSB" if mo % 2 == 0 else "PSC"
            pz = t[bank][:, 0:wn]
            ph.wait("tensor", dv)
            for kc in range(8):
                b = lambda h, p=pz, kc=kc, mo=mo, w0=w0, wn=wn: h.matmul(
                    p, cx.wr(mo)[:, kc * 128:(kc + 1) * 128],
                    cx.xng(w0, wn, kc), start=(kc == 0), stop=(kc == 7))
                (ph.addi("tensor", b, pe) if kc == 7 else ph.add("tensor", b))
            ph.wait("vector", pe)
            resw = t["RES"][:, mo * TT + w0: mo * TT + w0 + wn]
            ph.addi("vector", lambda h, p=pz, r=resw, mo=mo: h.scalar_tensor_tensor(
                r, p, t["VEC2"][:, 16 + mo: 17 + mo], r, ALU.add, ALU.add), dv)


def emit_p7(cx, ph):
    t = cx.t
    nc = cx.nc
    pe, dv, sc, sd, gd = cx.pe, cx.dv, cx.sc, cx.sd, cx.gd

    cx.qf.drain(ph)
    ph.wait("gpsimd", gd)

    emit_plain_ln(cx, ph)

    def token_absmax_rows(src_block_fn, nblocks, srow_part, irow_part, kmax):
        TR = t["SCB2"][:, 0:2064]
        TM = t["SCB1"][:, 0:2064]
        ph.add("vector", lambda h: h.memset(t["SCF"][:, 0:17], 0.0))
        ph.add("vector", lambda h: h.tensor_tensor(TR, src_block_fn(0), src_block_fn(1), ALU.max))
        ph.add("vector", lambda h: h.tensor_tensor(TM, src_block_fn(0), src_block_fn(1), ALU.min))
        for b2 in range(2, nblocks):
            ph.add("vector", lambda h, b2=b2: h.tensor_tensor(TR, TR, src_block_fn(b2), ALU.max))
            ph.add("vector", lambda h, b2=b2: h.tensor_tensor(TM, TM, src_block_fn(b2), ALU.min))
        ph.add("vector", lambda h: h.tensor_scalar(TM, TM, -1.0, None, ALU.mult))
        ph.addi("vector", lambda h: h.tensor_tensor(TR, TR, TM, ALU.max), dv)
        ph.wait("tensor", dv)
        for tt_ in range(17):
            tn = min(128, 2064 - tt_ * 128)
            bank = "PST" if tt_ % 2 == 0 else "PST2"
            pq_ = t[bank][0:tn, 0:128]
            if tt_ >= 2:
                ph.wait("tensor", dv)
            ph.addi("tensor", lambda h, p=pq_, tt_=tt_, tn=tn: h.transpose(
                p, TR[:, tt_ * 128: tt_ * 128 + tn], t["EYEB"][:]), pe)
            ph.wait("vector", pe)
            ph.addi("vector", lambda h, p=pq_, tt_=tt_, tn=tn: h.tensor_reduce(
                t["SCF"][0:tn, tt_:tt_ + 1], p, AX.X, ALU.max, apply_absolute_value=True), dv)
        ph.add("vector", lambda h: h.tensor_scalar(
            t["SCF"][:, 0:17], t["SCF"][:, 0:17], 1e-5, None, ALU.max))
        ph.add("vector", lambda h: h.tensor_scalar(
            t["SCF"][:, 17:34], t["SCF"][:, 0:17], 1.0 / float(kmax), None, ALU.mult))
        ph.add("vector", lambda h: h.reciprocal(t["SCF"][:, 34:51], t["SCF"][:, 0:17]))
        ph.addi("vector", lambda h: h.tensor_scalar(
            t["SCF"][:, 34:51], t["SCF"][:, 34:51], float(kmax), None, ALU.mult), dv)
        ph.wait("tensor", dv)
        psx = t["PSC"][0:34, 0:128]
        ph.addi("tensor", lambda h, p=psx: h.transpose(
            p, t["SCF"][:, 17:51], t["EYEF"][:]), pe)
        ph.wait("vector", pe)
        ph.addi("vector", lambda h, p=psx: h.tensor_copy(t["SCF"][0:34, 64:192], p), dv)
        ph.wait("sync", dv)
        for part, roff in ((irow_part, 0), (srow_part, 17)):
            src = bass.AP(t["SCF"], roff * 2048 + 64, [[2048, 16], [1, 128]])
            dstrow = bass.AP(t["ROWS"], part * 2064, [[2064, 1], [128, 16], [1, 128]])
            ph.addi("sync", lambda h, d=dstrow, s2=src: h.dma_start(d, s2), sd, 16)
            src2 = bass.AP(t["SCF"], (roff + 16) * 2048 + 64, [[2048, 1], [1, 16]])
            dst2 = bass.AP(t["ROWS"], part * 2064 + 2048, [[2064, 1], [1, 16]])
            ph.addi("sync", lambda h, d=dst2, s2=src2: h.dma_start(d, s2), sd, 16)
        ph.wait("vector", sd)
        ph.wait("tensor", sd)

    token_absmax_rows(lambda c: bass.AP(t["XNG"], c * TT, [[8 * TT, 128], [1, 2064]]),
                      8, 64, 96, 127.0)
    if os.environ.get("BASSMK_P7CUT") == "1":
        return

    for w0, wn in [(i * 256, 256) for i in range(8)] + [(2048, 16)]:
        ph.wait("tensor", dv)
        psb = t["PSB"][:, 0:wn]
        ph.addi("tensor", lambda h, p=psb, w0=w0, wn=wn: h.matmul(
            p, t["ONES1"][64:65, :], cx.row("ROWS", 64, w0, wn), start=True, stop=True), pe)
        ph.wait("vector", pe)
        for c in range(8):
            ph.add("vector", lambda h, c=c, p=psb, w0=w0, wn=wn: h.tensor_tensor(
                t["SCF"][:, c * 256: c * 256 + wn], cx.xng(w0, wn, c), p, ALU.mult))
        for c in range(8):
            ph.add("vector", lambda h, c=c, wn=wn: h.tensor_scalar(
                t["SCF"][:, c * 256: c * 256 + wn], t["SCF"][:, c * 256: c * 256 + wn],
                C23, C23, ALU.add, ALU.subtract))
        for c in range(8):
            b = lambda h, c=c, w0=w0, wn=wn: h.tensor_copy(
                cx.xng(w0, wn, c), t["SCF"][:, c * 256: c * 256 + wn])
            (ph.addi("vector", b, dv) if c == 7 else ph.add("vector", b))

    if os.environ.get("BASSMK_P7CUT") == "2":
        return

    if os.environ.get("BASSMK_P7CUT") == "3":
        return
    QW = [(0, 512), (512, 512), (1024, 512), (1536, 512), (2048, 16)]
    if os.environ.get("BASSMK_NQ"):
        QW = QW[:int(os.environ["BASSMK_NQ"])]
    for w0, wn in QW:
        for mb in range(8):
            sl_g = 0
            sl_u = 4
            ph.wait("gpsimd", pe)
            dg = ph.addi("gpsimd", lambda h, mb=mb, sl_g=sl_g: h.dma_start(
                cx.wr(sl_g, 4096), t["kwg"][mb]), gd, 16)
            du = ph.addi("gpsimd", lambda h, mb=mb, sl_u=sl_u: h.dma_start(
                cx.wr(sl_u, 4096), t["kwu"][mb]), gd, 16)
            ph.wait("tensor", gd, du)
            ph.wait("tensor", dv)
            pib = t["PSB"][:, 0:wn]
            ph.addi("tensor", lambda h, p=pib, w0=w0, wn=wn: h.matmul(
                p, t["ONES1"][96:97, :], cx.row("ROWS", 96, w0, wn), start=True, stop=True,
                tile_position=(96, 0)), pe)
            sib = t["SCF"][:, 1024:1024 + wn]
            ph.wait("vector", pe)
            ph.addi("vector", lambda h, p=pib, s=sib: h.tensor_copy(s, p), dv)
            for ms in range(4):
                pg = t["PSA"][:, (ms % 2) * 1024: (ms % 2) * 1024 + wn]
                pu = t["PSA"][:, (ms % 2) * 1024 + 512: (ms % 2) * 1024 + 512 + wn]
                if ms >= 2:
                    ph.wait("tensor", dv)
                for c in range(8):
                    ph.add("tensor", lambda h, c=c, p=pg, sl_g=sl_g, ms=ms, w0=w0, wn=wn: h.matmul(
                        p, bass.AP(t["WR"], sl_g * 1024 + c * 512 + ms * 128, [[14336, 128], [1, 128]]),
                        cx.xng(w0, wn, c), start=(c == 0), stop=(c == 7)))
                for c in range(8):
                    b = lambda h, c=c, p=pu, sl_u=sl_u, ms=ms, w0=w0, wn=wn: h.matmul(
                        p, bass.AP(t["WR"], sl_u * 1024 + c * 512 + ms * 128, [[14336, 128], [1, 128]]),
                        cx.xng(w0, wn, c), start=(c == 0), stop=(c == 7))
                    (ph.addi("tensor", b, pe) if c == 7 else ph.add("tensor", b))
                ph.wait("vector", pe)
                ph.add("vector", lambda h, p=pg, pib=sib, wn=wn: h.tensor_tensor(
                    t["SCB2"][:, 2048:2048 + wn], p, pib, ALU.mult))
                ph.addi("vector", lambda h, wn=wn: h.tensor_scalar(
                    t["SCB2"][:, 2048:2048 + wn], t["SCB2"][:, 2048:2048 + wn],
                    t["VEC2"][:, 29:30], None, ALU.mult), dv)
                ph.wait("scalar", dv)
                ph.addi("scalar", lambda h, wn=wn: h.activation(
                    t["SCB2"][:, 2560:2560 + wn], t["SCB2"][:, 2048:2048 + wn], AF.Silu), cx.sc)
                ph.wait("vector", cx.sc)
                ph.add("vector", lambda h, p=pu, pib=sib, wn=wn: h.tensor_tensor(
                    t["SCB2"][:, 3072:3072 + wn], p, pib, ALU.mult))
                ph.add("vector", lambda h, wn=wn: h.tensor_scalar(
                    t["SCB2"][:, 3072:3072 + wn], t["SCB2"][:, 3072:3072 + wn],
                    t["VEC2"][:, 30:31], None, ALU.mult))
                b = lambda h, mb=mb, ms=ms, wn=wn: h.tensor_tensor(
                    t["SCB1"][:, mb * 2080 + ms * 512: mb * 2080 + ms * 512 + wn],
                    t["SCB2"][:, 2560:2560 + wn], t["SCB2"][:, 3072:3072 + wn], ALU.mult)
                ph.addi("vector", b, dv)
        TRH = t["SCB2"][:, 0:512]
        TMH = t["SCB2"][:, 512:1024]
        ph.add("vector", lambda h: h.memset(t["SCF"][:, 0:4], 0.0))
        first = True
        for mb in range(8):
            for ms in range(4):
                sap = t["SCB1"][:, mb * 2080 + ms * 512: mb * 2080 + ms * 512 + wn]
                if first:
                    ph.add("vector", lambda h, s2=sap, wn=wn: h.tensor_copy(
                        TRH[:, 0:wn], s2))
                    ph.add("vector", lambda h, s2=sap, wn=wn: h.tensor_copy(
                        TMH[:, 0:wn], s2))
                    first = False
                else:
                    ph.add("vector", lambda h, s2=sap, wn=wn: h.tensor_tensor(
                        TRH[:, 0:wn], TRH[:, 0:wn], s2, ALU.max))
                    ph.add("vector", lambda h, s2=sap, wn=wn: h.tensor_tensor(
                        TMH[:, 0:wn], TMH[:, 0:wn], s2, ALU.min))
        ph.add("vector", lambda h, wn=wn: h.tensor_scalar(
            TMH[:, 0:wn], TMH[:, 0:wn], -1.0, None, ALU.mult))
        ph.add("vector", lambda h, wn=wn: h.tensor_tensor(
            TRH[:, 0:wn], TRH[:, 0:wn], TMH[:, 0:wn], ALU.max))
        ph.addi("vector", lambda h: h.engine_nop(), dv)
        ph.wait("tensor", dv)
        ntile = (wn + 127) // 128
        ph.add("vector", lambda h: h.memset(t["SCF"][:, 0:4], 0.0))
        for tt_ in range(ntile):
            tn = min(128, wn - tt_ * 128)
            bank = "PST" if tt_ % 2 == 0 else "PST2"
            pq_ = t[bank][0:tn, 0:128]
            if tt_ >= 2:
                ph.wait("tensor", dv)
            ph.addi("tensor", lambda h, p=pq_, tt_=tt_, tn=tn: h.transpose(
                p, TRH[:, tt_ * 128: tt_ * 128 + tn], t["EYEB"][:]), pe)
            ph.wait("vector", pe)
            ph.addi("vector", lambda h, p=pq_, tt_=tt_, tn=tn: h.tensor_reduce(
                t["SCF"][0:tn, tt_:tt_ + 1], p, AX.X, ALU.max, apply_absolute_value=True), dv)
        ph.add("vector", lambda h: h.tensor_scalar(
            t["SCF"][:, 0:4], t["SCF"][:, 0:4], 1e-5, None, ALU.max))
        ph.add("vector", lambda h: h.tensor_scalar(
            t["SCF"][:, 4:8], t["SCF"][:, 0:4], 1.0 / 127.0, None, ALU.mult))
        ph.add("vector", lambda h: h.reciprocal(t["SCF"][:, 8:12], t["SCF"][:, 0:4]))
        ph.addi("vector", lambda h: h.tensor_scalar(
            t["SCF"][:, 8:12], t["SCF"][:, 8:12], 127.0, None, ALU.mult), dv)
        ph.wait("tensor", dv)
        psx = t["PSC"][0:8, 0:128]
        ph.addi("tensor", lambda h, p=psx: h.transpose(
            p, t["SCF"][:, 4:12], t["EYEF"][:]), pe)
        ph.wait("vector", pe)
        ph.addi("vector", lambda h, p=psx: h.tensor_copy(t["SCF"][0:8, 64:192], p), dv)
        ph.wait("sync", dv)
        nfull = wn // 128
        for part, roff in ((0, 0), (32, 4)):
            if nfull > 0:
                src = bass.AP(t["SCF"], roff * 2048 + 64, [[2048, nfull], [1, 128]])
                dstrow = bass.AP(t["ROWS"], part * 2064 + w0, [[2064, 1], [128, nfull], [1, 128]])
                ph.addi("sync", lambda h, d=dstrow, s2=src: h.dma_start(d, s2), sd, 16)
            if wn % 128:
                src2 = bass.AP(t["SCF"], (roff + nfull) * 2048 + 64, [[2048, 1], [1, wn % 128]])
                dst2 = bass.AP(t["ROWS"], part * 2064 + w0 + nfull * 128, [[2064, 1], [1, wn % 128]])
                ph.addi("sync", lambda h, d=dst2, s2=src2: h.dma_start(d, s2), sd, 16)
        ph.wait("vector", sd)
        ph.wait("tensor", sd)
        ph.wait("tensor", dv)
        psh = t["PSB"][:, 0:wn]
        ph.addi("tensor", lambda h, p=psh, w0=w0, wn=wn: h.matmul(
            p, t["ONES1"][32:33, :], cx.row("ROWS", 32, w0, wn), start=True, stop=True), pe)
        ph.wait("vector", pe)
        for mb in range(8):
            for ms in range(4):
                sap = t["SCB1"][:, mb * 2080 + ms * 512: mb * 2080 + ms * 512 + wn]
                ph.add("vector", lambda h, s2=sap, p=psh, wn=wn: h.tensor_tensor(
                    t["SCF"][:, 0:wn], s2, p, ALU.mult))
                ph.add("vector", lambda h, wn=wn: h.tensor_scalar(
                    t["SCF"][:, 0:wn], t["SCF"][:, 0:wn], C23, C23, ALU.add, ALU.subtract))
                b = lambda h, s2=sap, wn=wn: h.tensor_copy(s2, t["SCF"][:, 0:wn])
                ph.addi("vector", b, dv)
        for g2 in range(4):
            ph.wait("gpsimd", pe)
            dk = []
            for kk in range(8):
                kc = g2 * 8 + kk
                dk.append(ph.addi("gpsimd", lambda h, kc=kc, kk=kk: h.dma_start(
                    cx.wr(kk), t["kwd"][kc]), gd, 16))
            ph.wait("tensor", gd, dk[-1])
            ph.wait("tensor", dv)
            pih = t["PSB"][:, 0:wn]
            ph.addi("tensor", lambda h, p=pih, w0=w0, wn=wn: h.matmul(
                p, t["ONES1"][0:1, :], cx.row("ROWS", 0, w0, wn), start=True, stop=True), pe)
            sih = t["SCF"][:, 1024:1024 + wn]
            ph.wait("vector", pe)
            ph.addi("vector", lambda h, p=pih, s=sih: h.tensor_copy(s, p), dv)
            for mo in range(8):
                pz = t["PSC"][:, 0:wn]
                ph.wait("tensor", dv)
                for kk in range(8):
                    kc = g2 * 8 + kk
                    mbb, mss = kc // 4, kc % 4
                    b = lambda h, p=pz, kk=kk, mo=mo, mbb=mbb, mss=mss, w0=w0, wn=wn: h.matmul(
                        p, bass.AP(t["WR"], kk * 1024 + mo * 128, [[14336, 128], [1, 128]]),
                        t["SCB1"][:, mbb * 2080 + mss * 512: mbb * 2080 + mss * 512 + wn],
                        start=(kk == 0), stop=(kk == 7))
                    (ph.addi("tensor", b, pe) if kk == 7 else ph.add("tensor", b))
                ph.wait("vector", pe)
                ph.add("vector", lambda h, p=pz, pih=sih, wn=wn: h.tensor_tensor(
                    t["SCB2"][:, 2048:2048 + wn], p, pih, ALU.mult))
                ph.add("vector", lambda h, wn=wn: h.tensor_scalar(
                    t["SCB2"][:, 2048:2048 + wn], t["SCB2"][:, 2048:2048 + wn],
                    t["VEC2"][:, 31:32], None, ALU.mult))
                resw = t["RES"][:, mo * TT + w0: mo * TT + w0 + wn]
                ph.addi("vector", lambda h, r=resw, wn=wn: h.tensor_tensor(
                    r, r, t["SCB2"][:, 2048:2048 + wn], ALU.add), dv)


def emit_pf(cx, ph):
    t = cx.t
    src = bass.AP(t["RES"], 0, [[8 * TT, 128], [TT, 8], [1, TC]])
    ph.addi("sync", lambda h: h.dma_start(t["out"][:], src), cx.sd, 16)
    ph.wait("sync", cx.sd)



def prep_shared(inputs):
    g = lambda k: np.asarray(inputs[k], dtype=np.float32)
    d = {}
    d["eyef"] = np.eye(128, dtype=np.float32)
    d["eyeb"] = np.eye(128, dtype=np.float32)
    d["onesS"] = np.full((128, 128), 1.0 / 1024.0, np.float32)
    d["onesSb"] = np.full((128, 128), 1.0 / 1024.0, np.float32)
    d["ones1"] = np.ones((128, 128), np.float32)
    d["w_inT"] = lhsT_host(np.ascontiguousarray(g("L0_attn_w_in").T))
    d["w_gateT"] = lhsT_host(np.ascontiguousarray(g("L0_attn_w_gate").T))
    wot = np.ascontiguousarray(g("L0_attn_w_out").T)
    d["w_outT"] = np.ascontiguousarray(np.concatenate([wot[0:64], wot[64:128]], axis=1))
    coup = g("L0_attn_coupling")
    bd = np.zeros((64, 64), np.float32)
    for h in range(4):
        bd[h * 16:(h + 1) * 16, h * 16:(h + 1) * 16] = coup[h].T
    d["bdcoupT"] = np.concatenate([bd, bd], 0)
    d["log_decay"] = g("L0_attn_log_decay").reshape(64, 1)
    d["frequency"] = g("L0_attn_frequency").reshape(64, 1)
    w1t = lhsT_host(np.ascontiguousarray(g("L0_attn_w_mlp1").T))
    d["w_mlp1T"] = np.ascontiguousarray(
        w1t.reshape(128, 8, 32, 128).transpose(2, 0, 1, 3))
    w2t = lhsT_host(np.ascontiguousarray(g("L0_attn_w_mlp2").T))
    d["w_mlp2T"] = np.ascontiguousarray(w2t.transpose(1, 0, 2))
    for nm, key in [("ln1_g", "L0_attn_ln1_g"), ("ln1_b", "L0_attn_ln1_b"),
                    ("ln2_g", "L0_attn_ln2_g"), ("ln2_b", "L0_attn_ln2_b"),
                    ("b_gate", "L0_attn_b_gate"), ("b_mlp2", "L0_attn_b_mlp2")]:
        d[nm] = vec_host(g(key))
    d["b_mlp1"] = vec_host(g("L0_attn_b_mlp1"))
    d["WgT"] = lhsT_host(np.ascontiguousarray(g("L0_ffn_Wg").T))
    ws1 = lhsT_host(g("L0_ffn_Ws1"))
    d["Ws1T"] = np.ascontiguousarray(ws1.reshape(128, 8, 16, 128).transpose(2, 0, 1, 3))
    ws2 = lhsT_host(g("L0_ffn_Ws2"))
    d["Ws2T"] = np.ascontiguousarray(ws2.transpose(1, 0, 2))
    w1 = g("L0_ffn_W1")
    w1l = np.stack([lhsT_host(w1[e]) for e in range(16)])
    d["W1e"] = np.ascontiguousarray(
        w1l.reshape(16, 128, 8, 8, 128).transpose(0, 3, 1, 2, 4))
    w2 = g("L0_ffn_W2")
    w2l = np.stack([lhsT_host(w2[e]) for e in range(16)])
    d["W2e"] = np.ascontiguousarray(w2l.transpose(0, 2, 1, 3))
    for nm, key in [("W_valueT", "L1_attn_W_value"), ("W_out1T", "L1_attn_W_out"),
                    ("W_projT", "L1_attn_W_proj")]:
        wl = lhsT_host(g(key))
        d[nm] = np.ascontiguousarray(wl.reshape(128, 8, 8, 128).transpose(2, 0, 1, 3))
    woa = np.concatenate([g("L1_attn_W_off"), g("L1_attn_W_attn"),
                          np.zeros((1024, 64), np.float32)], 1)
    woal = lhsT_host(woa)
    d["W_oaT"] = np.ascontiguousarray(woal.reshape(128, 8, 2, 128).transpose(2, 0, 1, 3))
    d["b_value"] = vec_host(g("L1_attn_b_value"))
    d["b_out1"] = vec_host(g("L1_attn_b_out"))
    d["b_proj"] = vec_host(g("L1_attn_b_proj"))
    d["b_off"] = g("L1_attn_b_off").reshape(128, 1)
    d["b_attn"] = g("L1_attn_b_attn").reshape(64, 1)
    wq1 = lhsT_host(np.ascontiguousarray(g("L1_ffn_w_gate").T))
    d["wq_gateT"] = np.ascontiguousarray(wq1.reshape(128, 8, 8, 512).transpose(2, 0, 1, 3))
    wq2 = lhsT_host(np.ascontiguousarray(g("L1_ffn_w_up").T))
    d["wq_upT"] = np.ascontiguousarray(wq2.reshape(128, 8, 8, 512).transpose(2, 0, 1, 3))
    wq3 = lhsT_host(np.ascontiguousarray(g("L1_ffn_w_down").T))
    d["wq_downT"] = np.ascontiguousarray(wq3.transpose(1, 0, 2))
    return d


def prep_core(inputs, core):
    x = np.asarray(inputs["L0_attn_x"], dtype=np.float32)
    b, hlf = core // 2, core % 2
    d = {}
    main = x[b, hlf * TC:(hlf + 1) * TC]
    ghost = x[b, 0:GH]
    d["xmain"] = fm_host(np.concatenate([main, ghost], 0))
    d["xpre"] = fm_host(x[b, 0:TC])
    fl = np.zeros((128, 2), np.float32)
    fl[:, 0] = float(hlf)
    fl[:, 1] = 1.0 - float(hlf)
    d["flags"] = fl
    rq = np.zeros((128, 17), np.float32)
    for tile in range(17):
        if tile < 16:
            idx = hlf * TC + tile * 128 + np.arange(128)
        else:
            idx = np.arange(128) % GH
        rq[:, tile] = idx / float(T - 1)
    d["refq"] = rq
    return d


def gather_out(results):
    out = np.zeros((B, T, D), np.float32)
    for core in range(8):
        b, hlf = core // 2, core % 2
        r = results[core]["out"]
        out[b, hlf * TC:(hlf + 1) * TC] = r.transpose(1, 0, 2).reshape(D, TC).T
    return out


_in_names = None


def kernel(**inputs):
    global _in_names
    nc = build_program()
    shared = prep_shared(inputs)
    in_maps = []
    for core in range(8):
        m = dict(shared)
        m.update(prep_core(inputs, core))
        in_maps.append(m)
    res = run_bass_kernel_spmd(nc, in_maps, core_ids=list(range(8)))
    return gather_out(res.results)


# version 03
# speedup vs baseline: 1.1799x; 1.1376x over previous
import math
import os
import numpy as np
import concourse.bass as bass
import concourse.mybir as mybir
from concourse.bass_utils import run_bass_kernel_spmd

F32 = mybir.dt.float32
BF16 = mybir.dt.bfloat16
AF = mybir.ActivationFunctionType
ALU = mybir.AluOpType
AX = mybir.AxisListType

B, T, D = 4, 4096, 1024
TC = 2048
GH = 16
TT = TC + GH
EPS = 1e-5
C23 = 3.0 * 2.0 ** 22
PI = math.pi

W512 = [(0, 512), (512, 512), (1024, 512), (1536, 512), (2048, 16)]
W256 = [(i * 256, 256) for i in range(8)] + [(2048, 16)]

STAGE_N = int(os.environ.get("BASSMK_STAGES", "99"))
DBG_TAP = os.environ.get("BASSMK_TAP", "")

ENGINES = ("sync", "tensor", "vector", "scalar", "gpsimd")
PHASE_BOUNDS = []


def lhsT_host(wT):
    d_in, M = wT.shape
    return np.ascontiguousarray(wT.reshape(d_in // 128, 128, M).transpose(1, 0, 2))


def vec_host(v):
    n = v.shape[0]
    return np.ascontiguousarray(v.reshape(n // 128, 128).T)


def fm_host(x2d):
    tn, d = x2d.shape
    return np.ascontiguousarray(x2d.T.reshape(d // 128, 128, tn).transpose(1, 0, 2))


class Sem:
    def __init__(self, nc, name):
        self.h = nc.alloc_semaphore(name)
        self.n = 0


def _ap_range(a):
    if not hasattr(a, "ap") or not hasattr(a, "offset"):
        return None
    lo = hi = a.offset
    for st, cnt in a.ap:
        d = st * (cnt - 1)
        if d >= 0:
            hi += d
        else:
            lo += d
    return (a.name, lo, hi)


class _Rec:
    __slots__ = ("ins", "writes", "inc")

    def __init__(self, ins, writes):
        self.ins = ins
        self.writes = writes
        self.inc = None


class _TIWrap:
    def __init__(self, guard, rec):
        self._g = guard
        self._r = rec

    def then_inc(self, semh, k):
        ins = self._r.ins.then_inc(semh, k)
        key = id(semh)
        self._g.semcnt[key] = self._g.semcnt.get(key, 0) + k
        self._r.inc = (semh, self._g.semcnt[key])
        return ins


class HazardGuard:

    HZK = 4

    def __init__(self, cx, eng):
        self.cx = cx
        self.eng = eng
        self.hist = []
        self.semcnt = {}

    def run_op(self, h, name, args, kwargs):
        writes = []
        reads = []
        for i, a in enumerate(args):
            r = _ap_range(a)
            if r is None:
                continue
            (writes if i == 0 else reads).append(r)
        for k, a in kwargs.items():
            r = _ap_range(a)
            if r is None:
                continue
            (writes if k in ("out", "accum_out") else reads).append(r)
        needs = []
        for rec in self.hist[-self.HZK:]:
            hit = False
            for (rn, rlo, rhi) in reads:
                for (wn, wlo, whi) in rec.writes:
                    if rn == wn and rlo <= whi and wlo <= rhi:
                        hit = True
                        break
                if hit:
                    break
            if hit:
                needs.append(rec)
        for rec in needs:
            if rec.inc is None:
                hz = self.hz
                hz.n += 1
                rec.ins.then_inc(hz.h, 1)
                key = id(hz.h)
                self.semcnt[key] = self.semcnt.get(key, 0) + 1
                rec.inc = (hz.h, self.semcnt[key])
                self.cx.n_hz_guards += 1
        waits = {}
        for rec in needs:
            semh, v = rec.inc
            key = id(semh)
            waits[key] = (semh, max(v, waits[key][1]) if key in waits else v)
        for semh, v in waits.values():
            h.wait_ge(semh, v)
        ins = getattr(h, name)(*args, **kwargs)
        rec = _Rec(ins, writes)
        self.hist.append(rec)
        if len(self.hist) > self.HZK:
            self.hist = self.hist[-self.HZK:]
        return _TIWrap(self, rec)


class _EngProxy:
    _PASS = ("wait_ge",)

    def __init__(self, h, guard):
        self._h = h
        self._g = guard

    def wait_ge(self, *a, **k):
        return self._h.wait_ge(*a, **k)

    def engine_nop(self, *a, **k):
        ins = self._h.engine_nop(*a, **k)
        rec = _Rec(ins, [])
        self._g.hist.append(rec)
        return _TIWrap(self._g, rec)

    def __getattr__(self, name):
        real = getattr(self._h, name)
        if not callable(real):
            return real
        g = self._g

        def call(*args, **kwargs):
            return g.run_op(self._h, name, args, kwargs)
        return call


class Phase:
    def __init__(self, cx, name):
        self.cx = cx
        self.name = name
        self.qs = {e: [] for e in ENGINES}

    def add(self, eng, build):
        self.qs[eng].append(build)

    def addi(self, eng, build, sem, k=1):
        sem.n += k
        self.qs[eng].append(lambda h, b=build, s=sem.h, kk=k: b(h).then_inc(s, kk))
        return sem.n

    def wait(self, eng, sem, val=None):
        v = sem.n if val is None else val
        self.qs[eng].append(lambda h, s=sem.h, vv=v: h.wait_ge(s, vv))

    def run(self):
        cx = self.cx
        PHASE_BOUNDS.append((self.name, cx.nc.n_instructions()))
        with cx.nc.Block(self.name) as blk:
            for e in ENGINES:
                fns = self.qs[e]
                if fns:
                    if e in ("vector", "scalar"):
                        gd_ = cx.guards[e]
                        getattr(blk, e)(lambda h, fns=fns, gd_=gd_: [
                            f(_EngProxy(h, gd_)) for f in fns])
                    else:
                        getattr(blk, e)(lambda h, fns=fns: [f(h) for f in fns])


class QFeeder:

    def __init__(self):
        self.units = []
        self.idx = 0

    def feed(self, ph, n=1):
        while n > 0 and self.idx < len(self.units):
            self.units[self.idx](ph)
            self.idx += 1
            n -= 1

    def drain(self, ph):
        self.feed(ph, len(self.units) - self.idx)


def build_q_units(cx):
    t = cx.t
    pe, dv, sd, gd = cx.pe, cx.dv, cx.sd, cx.gd
    qf = QFeeder()
    st = {"cons": {}, "store_gd": {}, "first": True, "gj": 0}
    SUB = 512
    ABS = t["ROW2"][:, 0:1]
    ACC = t["ROW2"][:, 1:2]
    rows_pitch = 2064

    def qsf(h2):
        return t["QSF"][:, h2 * SUB:(h2 + 1) * SUB]

    def qsb(h2):
        return t["QSB"][:, h2 * SUB:(h2 + 1) * SUB]

    for nm, srcnm, npieces, piece_cols, sw_col, inv_col, dstnm in [
            ("gate", "wq_gateT", 8, 4096, 26, 29, "kwg"),
            ("up", "wq_upT", 8, 4096, 27, 30, "kwu"),
            ("down", "wq_downT", 32, 1024, 28, 31, "kwd")]:
        subs = [(pc, co) for pc in range(npieces) for co in range(0, piece_cols, SUB)]
        total = npieces * piece_cols

        def sweep_unit(ph, j, pc, co, srcnm=srcnm, nm=nm):
            gj = st["gj"]; st["gj"] += 1
            h2 = gj % 2
            if gj - 2 in st["cons"]:
                ph.wait("sync", dv, st["cons"][gj - 2])
            if st["first"]:
                ph.wait("sync", pe)
                st["first"] = False
            src0 = t[srcnm][pc]
            sap = bass.AP(src0.tensor, src0.offset + co, [[4096 if srcnm != "wq_downT" else 1024, 128], [1, SUB]])
            dma = ph.addi("sync", lambda h, s2=sap, h2=h2: h.dma_start(qsf(h2), s2), sd, 16)
            ph.wait("vector", sd, dma)
            ph.add("vector", lambda h, h2=h2: h.tensor_reduce(
                ABS, qsf(h2), AX.X, ALU.add, apply_absolute_value=True))
            if j == 0:
                b = lambda h: h.tensor_copy(ACC, ABS)
            else:
                b = lambda h: h.tensor_tensor(ACC, ACC, ABS, ALU.add)
            st["cons"][gj] = ph.addi("vector", b, dv)

        def mean_unit(ph, nm=nm, total=total, sw_col=sw_col, inv_col=inv_col):
            ph.wait("gpsimd", dv)
            ph.wait("gpsimd", pe)
            dst = bass.AP(t["ROWS"], 0, [[rows_pitch, 1], [1, 128]])
            src = bass.AP(t["ROW2"], 1, [[4, 128], [1, 1]])
            g1 = ph.addi("gpsimd", lambda h, d=dst, s=src: h.dma_start(d, s), gd, 16)
            ph.wait("vector", gd, g1)
            ph.add("vector", lambda h: h.tensor_reduce(
                t["ROWS"][0:1, 130:131], t["ROWS"][0:1, 0:128], AX.X, ALU.add))
            ph.add("vector", lambda h, total=total: h.tensor_scalar(
                t["ROWS"][0:1, 131:132], t["ROWS"][0:1, 130:131], 1.0 / float(total * 128), None, ALU.mult))
            ph.add("vector", lambda h: h.tensor_scalar(
                t["ROWS"][0:1, 131:132], t["ROWS"][0:1, 131:132], 1e-5, None, ALU.max))
            m_dv = ph.addi("vector", lambda h: h.reciprocal(
                t["ROWS"][0:1, 132:133], t["ROWS"][0:1, 131:132]), dv)
            ph.wait("gpsimd", dv, m_dv)
            for c2, rc in ((sw_col, 132), (inv_col, 131)):
                dst2 = bass.AP(t["VEC2"], c2, [[32, 128], [1, 1]])
                src2 = bass.AP(t["ROWS"], rc, [[rows_pitch, 1], [0, 128], [1, 1]])
                st["store_gd"][("mean", nm)] = ph.addi(
                    "gpsimd", lambda h, d=dst2, s=src2: h.dma_start(d, s), gd, 16)

        def quant_unit(ph, j, pc, co, srcnm=srcnm, nm=nm, sw_col=sw_col, dstnm=dstnm,
                       piece_cols=piece_cols):
            gj = st["gj"]; st["gj"] += 1
            h2 = gj % 2
            if gj - 2 in st["cons"]:
                ph.wait("sync", dv, st["cons"][gj - 2])
            src0 = t[srcnm][pc]
            pitch = 4096 if srcnm != "wq_downT" else 1024
            sap = bass.AP(src0.tensor, src0.offset + co, [[pitch, 128], [1, SUB]])
            dma = ph.addi("sync", lambda h, s2=sap, h2=h2: h.dma_start(qsf(h2), s2), sd, 16)
            ph.wait("vector", sd, dma)
            ph.wait("vector", gd, max(st["store_gd"].get(("qsb", h2), 0),
                                      st["store_gd"][("mean", nm)]))
            ph.add("vector", lambda h, h2=h2, sw_col=sw_col: h.tensor_scalar(
                qsf(h2), qsf(h2), t["VEC2"][:, sw_col:sw_col + 1], None, ALU.mult))
            ph.add("vector", lambda h, h2=h2: h.tensor_scalar(
                qsf(h2), qsf(h2), C23, C23, ALU.add, ALU.subtract))
            st["cons"][gj] = ph.addi("vector", lambda h, h2=h2: h.tensor_scalar(
                qsb(h2), qsf(h2), -1.0, 1.0, ALU.max, ALU.min), dv)
            d0 = t[dstnm][pc]
            dap = bass.AP(d0.tensor, d0.offset + co, [[pitch, 128], [1, SUB]])
            ph.wait("gpsimd", dv, st["cons"][gj])
            st["store_gd"][("qsb", h2)] = ph.addi(
                "gpsimd", lambda h, d=dap, h2=h2: h.dma_start(d, qsb(h2)), gd, 16)

        for j, (pc, co) in enumerate(subs):
            qf.units.append(lambda ph, j=j, pc=pc, co=co, f=sweep_unit: f(ph, j, pc, co))
        qf.units.append(lambda ph, f=mean_unit: f(ph))
        for j, (pc, co) in enumerate(subs):
            qf.units.append(lambda ph, j=j, pc=pc, co=co, f=quant_unit: f(ph, j, pc, co))
    return qf


class Ctx:
    def __init__(self, nc):
        self.nc = nc
        self.t = {}
        self.sd = Sem(nc, "sd")
        self.gd = Sem(nc, "gd")
        self.pe = Sem(nc, "pe")
        self.dv = Sem(nc, "dv")
        self.sc = Sem(nc, "sc")
        self.gp = Sem(nc, "gp")
        self.n_hz_guards = 0
        self.guards = {"vector": HazardGuard(self, "vector"),
                       "scalar": HazardGuard(self, "scalar")}
        self.guards["vector"].hz = Sem(nc, "hzv")
        self.guards["scalar"].hz = Sem(nc, "hzs")
        self.dbg = []

    def row(self, tensor, part, c0, n):
        th = self.t[tensor]
        pitch = th.shape[1]
        return bass.AP(th, part * pitch + c0, [[pitch, 1], [1, n]])

    def wr(self, slot, cols=1024):
        return self.t["WR"][:, slot * 1024: slot * 1024 + cols]

    def res(self, w0, wn, c):
        return self.t["RES"][:, c * TT + w0: c * TT + w0 + wn]

    def xng(self, w0, wn, c):
        return self.t["XNG"][:, c * TT + w0: c * TT + w0 + wn]

    def beta_pre(self, w0, wn):
        return bass.AP(self.t["WR"], 10 * 1024 + w0, [[14336, 128], [1, wn]])

    def beta_main(self, w0, wn):
        return bass.AP(self.t["WR"], 10 * 1024 + 2048 + w0, [[14336, 128], [1, wn]])

    def beta_chunk(self, n):
        return bass.AP(self.t["WR"], 10 * 1024 + 64 * n, [[14336, 128], [1, 64]])


def build_program():
    nc = bass.Bass("TRN2", target_bir_lowering=False, detect_race_conditions=False)
    cx = Ctx(nc)
    t = cx.t

    def din(name, shape, dtype=F32):
        t[name] = nc.dram_tensor(name, list(shape), dtype, kind="ExternalInput")

    def sb(name, shape, dtype):
        t[name] = nc.alloc_sbuf_tensor(name, list(shape), dtype)

    din("xmain", (128, 8, TT))
    din("xpre", (128, 8, TC))
    din("flags", (128, 2))
    din("eyef", (128, 128)); din("eyeb", (128, 128))
    din("onesS", (128, 128)); din("onesSb", (128, 128)); din("ones1", (128, 128))
    din("refq", (128, 17))
    din("w_inT", (128, 8, 128))
    din("w_gateT", (128, 8, 1024))
    din("w_outT", (64, 2048))
    din("bdcoupT", (128, 64))
    din("log_decay", (64, 1)); din("frequency", (64, 1))
    din("w_mlp1T", (32, 128, 8, 128))
    din("w_mlp2T", (32, 128, 1024))
    din("ln1_g", (128, 8)); din("ln1_b", (128, 8))
    din("ln2_g", (128, 8)); din("ln2_b", (128, 8))
    din("b_gate", (128, 8)); din("b_mlp1", (128, 32)); din("b_mlp2", (128, 8))
    din("WgT", (128, 8, 16))
    din("Ws1T", (16, 128, 8, 128))
    din("Ws2T", (8, 128, 1024))
    din("W1e", (16, 8, 128, 8, 128))
    din("W2e", (16, 4, 128, 1024))
    din("W_valueT", (8, 128, 8, 128))
    din("W_oaT", (2, 128, 8, 128))
    din("b_value", (128, 8)); din("b_off", (128, 1)); din("b_attn", (64, 1))
    din("W_out1T", (8, 128, 8, 128)); din("b_out1", (128, 8))
    din("W_projT", (8, 128, 8, 128)); din("b_proj", (128, 8))
    din("wq_gateT", (8, 128, 8, 512))
    din("wq_upT", (8, 128, 8, 512))
    din("wq_downT", (32, 128, 1024))
    t["out"] = nc.dram_tensor("out", [128, 8, TC], F32, kind="ExternalOutput")

    sb("RES", (128, 8 * TT), F32)
    sb("XNG", (128, 8 * TT), BF16)
    sb("SCB1", (128, 16640), BF16)
    sb("SCB2", (128, 4096), BF16)
    sb("SCF", (128, 2048), F32)
    sb("WR", (128, 14336), BF16)
    _crci_base = nc.sbuf_base
    sb("CRCI", (128, 4112), BF16)
    _after_crci = nc.sbuf_base
    nc.sbuf_base = _crci_base
    sb("QSF", (128, 1024), F32)
    sb("QSB", (128, 1024), BF16)
    assert nc.sbuf_base <= _after_crci, (nc.sbuf_base, _after_crci)
    nc.sbuf_base = _after_crci
    sb("CC", (128, 512), BF16)
    sb("ROWS", (128, 2064), F32)
    sb("ROW2", (128, 4), F32)
    sb("SCFX", (128, 1024), F32)
    sb("VEC", (128, 48), F32)
    sb("VECB", (128, 32), F32)
    sb("VEC2", (128, 32), F32)
    sb("EYEF", (128, 128), F32)
    sb("EYEB", (128, 128), BF16)
    sb("ONESF128", (128, 128), F32)
    sb("ONESB128", (128, 128), BF16)
    sb("ONES1", (128, 128), F32)
    sb("FLG", (128, 2), F32)
    sb("REFQ", (128, 17), F32)
    sb("EKER", (64, 400), F32)
    sb("PKF", (64, 576), BF16)
    sb("BDC", (128, 64), BF16)
    sb("V14", (16, 1024), BF16)
    sb("BDV", (128, 1024), BF16)
    sb("WOUT", (64, 2048), BF16)

    t["PSA"] = nc.alloc_psum_tensor("PSA", [128, 2048], F32)
    t["PSB"] = nc.alloc_psum_tensor("PSB", [128, 512], F32)
    t["PSC"] = nc.alloc_psum_tensor("PSC", [128, 512], F32)
    t["PST"] = nc.alloc_psum_tensor("PST", [128, 512], BF16)
    t["PST2"] = nc.alloc_psum_tensor("PST2", [128, 512], BF16)
    assert nc.psum_base <= 8, nc.psum_base

    t["kwg"] = nc.dram_tensor("kwg", [8, 128, 4096], BF16, kind="Internal")
    t["kwu"] = nc.dram_tensor("kwu", [8, 128, 4096], BF16, kind="Internal")
    t["kwd"] = nc.dram_tensor("kwd", [32, 128, 1024], BF16, kind="Internal")
    cx.qf = build_q_units(cx)

    stages = [(0, emit_p0), (1, emit_p1), (2, emit_p2), (3, emit_p3),
              (4, emit_p4a), (4, emit_p4b), (5, emit_p5), (6, emit_p6), (7, emit_p7)]
    for sn, fn in stages:
        if sn <= STAGE_N:
            p = Phase(cx, fn.__name__)
            fn(cx, p)
            p.run()

    if DBG_TAP:
        emit_tap(cx)
    p = Phase(cx, "pf")
    emit_pf(cx, p)
    p.run()
    nc.finalize()
    return nc



def emit_tap(cx):
    nc, t = cx.nc, cx.t
    spec = DBG_TAP.split(",")
    name = spec[0]
    th = t[name]
    shape = list(th.shape)
    c0, c1 = (int(spec[1]), int(spec[2])) if len(spec) >= 3 else (0, shape[1])
    p0, p1 = (int(spec[3]), int(spec[4])) if len(spec) >= 5 else (0, shape[0])
    dt = th.dtype
    t["dbg"] = nc.dram_tensor("dbg", [p1 - p0, c1 - c0], dt, kind="ExternalOutput")
    p = Phase(cx, "ptap")
    p.addi("sync", lambda h: h.dma_start(t["dbg"][:], th[p0:p1, c0:c1]), cx.sd, 16)
    p.wait("sync", cx.sd)
    p.run()



def ln_rep(cx, ph, srcs, wn, bf16_src=False, rho=False):
    t = cx.t
    pe, dv, sc = cx.pe, cx.dv, cx.sc
    ph.wait("scalar", pe)
    ph.wait("scalar", dv)
    for c in range(8):
        b = lambda h, c=c: h.activation(
            t["SCB1"][:, c * wn: (c + 1) * wn], srcs[c], AF.Square)
        (ph.addi("scalar", b, sc) if c == 7 else ph.add("scalar", b))
    ph.wait("tensor", sc)
    ph.wait("tensor", dv)
    pm = t["PSB"][:, 0:wn]
    pq = t["PSC"][:, 0:wn]
    lhs_m = t["ONESB128"][:] if bf16_src else t["ONESF128"][:]
    for c in range(8):
        ph.add("tensor", lambda h, c=c, lhs_m=lhs_m: h.matmul(
            pm, lhs_m, srcs[c], start=(c == 0), stop=(c == 7)))
    for c in range(8):
        b = lambda h, c=c: h.matmul(
            pq, t["ONESB128"][:], t["SCB1"][:, c * wn: (c + 1) * wn],
            start=(c == 0), stop=(c == 7))
        (ph.addi("tensor", b, pe) if c == 7 else ph.add("tensor", b))
    ph.wait("vector", pe)
    mb = t["SCB2"][:, 2048:2048 + wn]
    var = t["SCFX"][:, 0:wn]
    rstd = t["SCB2"][:, 2560:2560 + wn]
    ph.add("vector", lambda h: h.tensor_copy(mb, pm))
    ph.add("vector", lambda h: h.tensor_tensor(var, mb, mb, ALU.mult))
    ph.add("vector", lambda h: h.tensor_tensor(var, pq, var, ALU.subtract))
    ph.addi("vector", lambda h: h.tensor_scalar(var, var, EPS, None, ALU.add), dv)
    ph.wait("scalar", dv)
    ph.add("scalar", lambda h: h.activation(t["SCFX"][:, 512:512 + wn], var, AF.Ln))
    ph.addi("scalar", lambda h: h.activation(
        rstd, t["SCFX"][:, 512:512 + wn], AF.Exp, scale=-0.5), sc)
    out = dict(m_rep=pm, mb=mb, rstd=rstd)
    if rho:
        vq = t["SCFX"][:, 512:512 + wn]
        rr = t["SCB2"][:, 3072:3072 + wn]
        rhot = t["SCB2"][:, 3584:3584 + wn]
        ph.wait("vector", sc)
        ph.add("vector", lambda h: h.tensor_tensor(vq, rstd, rstd, ALU.mult))
        ph.add("vector", lambda h: h.tensor_tensor(vq, var, vq, ALU.mult))
        ph.addi("vector", lambda h: h.tensor_scalar(vq, vq, EPS, None, ALU.add), dv)
        ph.wait("scalar", dv)
        ph.add("scalar", lambda h: h.activation(vq, vq, AF.Ln))
        ph.addi("scalar", lambda h: h.activation(rr, vq, AF.Exp, scale=-0.5), sc)
        ph.wait("vector", sc)
        ph.addi("vector", lambda h: h.tensor_tensor(rhot, rstd, rr, ALU.mult), dv)
        out["rho"] = rhot
    return out


def emit_plain_ln(cx, ph, store_rows=None):
    t = cx.t
    pe, dv, sc, sd, gd = cx.pe, cx.dv, cx.sc, cx.sd, cx.gd
    for w0, wn in W512:
        r = ln_rep(cx, ph, [cx.res(w0, wn, c) for c in range(8)], wn,
                   rho=bool(store_rows and "rho" in store_rows))
        ph.wait("vector", sc)
        for c in range(8):
            ph.add("vector", lambda h, c=c, r=r, w0=w0, wn=wn: h.tensor_tensor(
                t["SCB1"][:, 8448 + c * wn: 8448 + (c + 1) * wn],
                cx.res(w0, wn, c), r["m_rep"], ALU.subtract))
        for c in range(8):
            b = lambda h, c=c, r=r, w0=w0, wn=wn: h.tensor_tensor(
                cx.xng(w0, wn, c), t["SCB1"][:, 8448 + c * wn: 8448 + (c + 1) * wn],
                r["rstd"], ALU.mult)
            (ph.addi("vector", b, dv) if c == 7 else ph.add("vector", b))
        if store_rows:
            ph.wait("gpsimd", dv)
            ph.wait("gpsimd", sc)
            for q, part in store_rows.items():
                src = {"m": r["mb"], "rstd": r["rstd"], "rho": r.get("rho")}[q]
                ap = bass.AP(src.tensor, src.offset, [[src.ap[0][0], 1], [1, wn]])
                dst = cx.row("ROWS", part, w0, wn)
                ph.addi("gpsimd", lambda h, d=dst, s2=ap: h.dma_start(d, s2), gd, 16)
            ph.wait("scalar", gd)
            ph.wait("vector", gd)



def emit_p0(cx, ph):
    t = cx.t
    sd, gd, dv, sc, gp = cx.sd, cx.gd, cx.dv, cx.sc, cx.gp

    def s_dma(dst, src):
        ph.addi("sync", lambda h, d=dst, s=src: h.dma_start(d, s), sd, 16)

    def g_dma(dst, src):
        ph.addi("gpsimd", lambda h, d=dst, s=src: h.dma_start(d, s), gd, 16)

    s_dma(t["EKER"][:, 256:257], t["log_decay"][:])
    s_dma(t["EKER"][:, 257:258], t["frequency"][:])
    cx.n_eker_dma = sd.n
    s_dma(t["RES"][:].rearrange("p (c n) -> p c n", c=8), t["xmain"][:])
    s_dma(t["FLG"][:], t["flags"][:])
    s_dma(t["REFQ"][:], t["refq"][:])
    s_dma(t["EYEF"][:], t["eyef"][:])
    s_dma(t["ONESF128"][:], t["onesS"][:])
    s_dma(t["ONES1"][:], t["ones1"][:])
    s_dma(t["VEC"][:, 0:8], t["ln1_g"][:])
    s_dma(t["VEC"][:, 8:16], t["ln1_b"][:])
    s_dma(t["VEC"][:, 16:24], t["ln2_g"][:])
    s_dma(t["VEC"][:, 24:32], t["ln2_b"][:])
    s_dma(t["VEC"][:, 32:40], t["b_gate"][:])
    s_dma(t["VEC"][:, 40:48], t["b_mlp2"][:])
    s_dma(t["VECB"][:, 0:32], t["b_mlp1"][:])
    s_dma(t["VEC2"][:, 0:8], t["b_value"][:])
    s_dma(t["VEC2"][:, 8:16], t["b_out1"][:])
    s_dma(t["VEC2"][:, 16:24], t["b_proj"][:])
    s_dma(t["VEC2"][:, 24:25], t["b_off"][:])
    s_dma(bass.AP(t["VEC2"], 25, [[32, 64], [1, 1]]), t["b_attn"][:])

    g_dma(t["EYEB"][:], t["eyeb"][:])
    g_dma(t["ONESB128"][:], t["onesSb"][:])
    g_dma(t["BDC"][:], t["bdcoupT"][:])
    g_dma(cx.wr(0, 1024), t["w_inT"][:].rearrange("p c m -> p (c m)"))
    for c in range(8):
        g_dma(cx.wr(1 + c, 1024), t["w_gateT"][:, c, :])
    g_dma(t["WOUT"][:], t["w_outT"][:])
    ph.addi("gpsimd", lambda h: h.memset(t["PKF"][:], 0.0), gp)

    ek = t["EKER"]
    ph.wait("scalar", sd, cx.n_eker_dma * 1)
    ph.addi("scalar", lambda h: h.activation(ek[:, 258:259], ek[:, 256:257], AF.Sigmoid), sc)
    ph.wait("vector", sd, cx.n_eker_dma * 1)
    ph.add("vector", lambda h: h.tensor_scalar(ek[:, 259:260], ek[:, 257:258], 1.0 / (2 * PI), None, ALU.mult))
    ph.add("vector", lambda h: h.tensor_scalar(ek[:, 260:261], ek[:, 259:260], C23, C23, ALU.add, ALU.subtract))
    ph.add("vector", lambda h: h.scalar_tensor_tensor(ek[:, 261:262], ek[:, 260:261], -1.0, ek[:, 259:260], ALU.mult, ALU.add))
    ph.add("vector", lambda h: h.tensor_scalar(ek[:, 261:262], ek[:, 261:262], 2 * PI, None, ALU.mult))
    ph.add("vector", lambda h: h.tensor_scalar(ek[:, 262:263], ek[:, 259:260], 1.0, 0.25, ALU.mult, ALU.add))
    ph.add("vector", lambda h: h.tensor_scalar(ek[:, 263:264], ek[:, 262:263], C23, C23, ALU.add, ALU.subtract))
    ph.add("vector", lambda h: h.scalar_tensor_tensor(ek[:, 264:265], ek[:, 263:264], -1.0, ek[:, 262:263], ALU.mult, ALU.add))
    ph.addi("vector", lambda h: h.tensor_scalar(ek[:, 264:265], ek[:, 264:265], 2 * PI, None, ALU.mult), dv)
    ph.wait("scalar", dv)
    ph.add("scalar", lambda h: h.activation(ek[:, 265:266], ek[:, 261:262], AF.Sin))
    ph.addi("scalar", lambda h: h.activation(ek[:, 266:267], ek[:, 264:265], AF.Sin), sc)
    ph.wait("vector", sc)
    ph.add("vector", lambda h: h.tensor_tensor(ek[:, 267:268], ek[:, 258:259], ek[:, 266:267], ALU.mult))
    ph.add("vector", lambda h: h.tensor_tensor(ek[:, 268:269], ek[:, 258:259], ek[:, 265:266], ALU.mult))
    ph.add("vector", lambda h: h.memset(ek[:, 0:1], 1.0))
    ph.add("vector", lambda h: h.memset(ek[:, 64:65], 0.0))
    ph.add("vector", lambda h: h.tensor_copy(ek[:, 1:2], ek[:, 267:268]))
    ph.add("vector", lambda h: h.tensor_copy(ek[:, 65:66], ek[:, 268:269]))
    q = 1
    while q < 64:
        n = min(q, 64 - q)
        ph.add("vector", lambda h, q=q: h.tensor_tensor(
            ek[:, 396:397], ek[:, q - 1:q], ek[:, 267:268], ALU.mult))
        ph.add("vector", lambda h, q=q: h.tensor_tensor(
            ek[:, 397:398], ek[:, 64 + q - 1:64 + q], ek[:, 268:269], ALU.mult))
        ph.add("vector", lambda h: h.tensor_tensor(
            ek[:, 398:399], ek[:, 396:397], ek[:, 397:398], ALU.subtract))
        ph.add("vector", lambda h, q=q: h.tensor_tensor(
            ek[:, 396:397], ek[:, q - 1:q], ek[:, 268:269], ALU.mult))
        ph.add("vector", lambda h, q=q: h.tensor_tensor(
            ek[:, 397:398], ek[:, 64 + q - 1:64 + q], ek[:, 267:268], ALU.mult))
        ph.add("vector", lambda h: h.tensor_tensor(
            ek[:, 399:400], ek[:, 396:397], ek[:, 397:398], ALU.add))
        ph.add("vector", lambda h, n=n: h.tensor_scalar(
            ek[:, 270:270 + n], ek[:, 0:n], ek[:, 398:399], None, ALU.mult))
        ph.add("vector", lambda h, n=n: h.tensor_scalar(
            ek[:, 334:334 + n], ek[:, 64:64 + n], ek[:, 399:400], None, ALU.mult))
        ph.add("vector", lambda h, n=n: h.tensor_scalar(
            ek[:, 334:334 + n], ek[:, 334:334 + n], -1.0, None, ALU.mult))
        ph.add("vector", lambda h, n=n: h.tensor_scalar(
            ek[:, 302:302 + n], ek[:, 0:n], ek[:, 399:400], None, ALU.mult))
        ph.add("vector", lambda h, n=n: h.tensor_scalar(
            ek[:, 366:366 + n], ek[:, 64:64 + n], ek[:, 398:399], None, ALU.mult))
        ph.add("vector", lambda h, q=q, n=n: h.tensor_tensor(
            ek[:, q:q + n], ek[:, 270:270 + n], ek[:, 334:334 + n], ALU.add))
        ph.add("vector", lambda h, q=q, n=n: h.tensor_tensor(
            ek[:, 64 + q:64 + q + n], ek[:, 302:302 + n], ek[:, 366:366 + n], ALU.add))
        q *= 2
    ph.wait("vector", gp)
    rev_r = bass.AP(ek.handle if hasattr(ek, "handle") else t["EKER"], 63, [[400, 64], [-1, 64]])
    rev_i = bass.AP(t["EKER"], 64 + 63, [[400, 64], [-1, 64]])
    ph.add("vector", lambda h: h.tensor_copy(t["PKF"][:, 0:64], rev_r))
    ph.add("vector", lambda h: h.tensor_copy(t["PKF"][:, 192:192 + 64], rev_i))
    ph.addi("vector", lambda h: h.tensor_scalar(
        t["PKF"][:, 384:384 + 64], rev_i, -1.0, None, ALU.mult), dv)
    cx.n_pkf_done = dv.n



def emit_p1(cx, ph):
    t = cx.t
    ph.wait("scalar", cx.sd)
    ph.wait("tensor", cx.sd)
    ph.wait("tensor", cx.gd)
    for w0, wn in W512:
        r = ln_rep(cx, ph, [cx.res(w0, wn, c) for c in range(8)], wn, rho=True)
        ph.wait("gpsimd", cx.dv)
        ph.wait("gpsimd", cx.sc)
        for q, part in (("m", 0), ("rstd", 32), ("rho", 64)):
            src = {"m": r["mb"], "rstd": r["rstd"], "rho": r["rho"]}[q]
            ap = bass.AP(src.tensor, src.offset, [[src.ap[0][0], 1], [1, wn]])
            dst = cx.row("ROWS", part, w0, wn)
            ph.addi("gpsimd", lambda h, d=dst, s2=ap: h.dma_start(d, s2), cx.gd, 16)
        ph.wait("scalar", cx.gd)
        ph.wait("vector", cx.gd)



def emit_p2(cx, ph):
    t = cx.t
    pe, dv, sc, sd, gd = cx.pe, cx.dv, cx.sc, cx.sd, cx.gd
    ph.wait("tensor", gd)

    def beta_mm(dst_ap, wn):
        pb = t["PSC"][:, 0:wn]
        ph.wait("tensor", dv)
        for c in range(8):
            b = lambda h, c=c, pb=pb, wn=wn: h.matmul(
                pb, cx.wr(0)[:, c * 128:(c + 1) * 128],
                t["SCB2"][:, c * wn:(c + 1) * wn], start=(c == 0), stop=(c == 7))
            (ph.addi("tensor", b, pe) if c == 7 else ph.add("tensor", b))
        ph.wait("vector", pe)
        ph.addi("vector", lambda h, pb=pb, d=dst_ap: h.tensor_copy(d, pb), dv)

    def gate_mm(w0, wn):
        for half in range(2):
            if half == 0:
                ph.wait("tensor", dv)
            ph.wait("tensor", sc)
            for mm in range(4):
                m = half * 4 + mm
                for c in range(8):
                    b = lambda h, m=m, mm=mm, c=c, wn=wn: h.matmul(
                        t["PSA"][:, mm * 512: mm * 512 + wn],
                        cx.wr(1 + c)[:, m * 128:(m + 1) * 128],
                        t["SCB2"][:, c * wn:(c + 1) * wn], start=(c == 0), stop=(c == 7))
                    (ph.addi("tensor", b, pe) if (mm == 3 and c == 7) else ph.add("tensor", b))
            ph.wait("scalar", pe)
            for mm in range(4):
                m = half * 4 + mm
                b = lambda h, m=m, mm=mm, w0=w0, wn=wn: h.activation(
                    cx.xng(w0, wn, m), t["PSA"][:, mm * 512: mm * 512 + wn],
                    AF.Sigmoid, bias=t["VEC"][:, 32 + m: 33 + m])
                (ph.addi("scalar", b, sc) if mm == 3 else ph.add("scalar", b))

    for w0, wn in W512:
        ph.wait("tensor", dv)
        pm = t["PSB"][:, 0:wn]
        pb = t["PSC"][:, 0:wn]
        ph.add("tensor", lambda h, pm=pm, w0=w0, wn=wn: h.matmul(
            pm, t["ONES1"][0:1, :], cx.row("ROWS", 0, w0, wn), start=True, stop=True))
        ph.addi("tensor", lambda h, pb=pb, w0=w0, wn=wn: h.matmul(
            pb, t["ONES1"][64:65, :], cx.row("ROWS", 64, w0, wn), start=True, stop=True,
            tile_position=(64, 0)), pe)
        ph.wait("vector", pe)
        for c in range(8):
            ph.add("vector", lambda h, c=c, pm=pm, w0=w0, wn=wn: h.tensor_tensor(
                t["SCB2"][:, c * wn: c * wn + wn], cx.res(w0, wn, c), pm, ALU.subtract))
        for c in range(8):
            ph.add("vector", lambda h, c=c, pb=pb, wn=wn: h.tensor_tensor(
                t["SCB2"][:, c * wn: c * wn + wn], t["SCB2"][:, c * wn: c * wn + wn], pb, ALU.mult))
        for c in range(8):
            b = lambda h, c=c, wn=wn: h.tensor_scalar(
                t["SCB2"][:, c * wn: (c + 1) * wn], t["SCB2"][:, c * wn: (c + 1) * wn],
                t["VEC"][:, 0 + c: 1 + c], t["VEC"][:, 8 + c: 9 + c], ALU.mult, ALU.add)
            (ph.addi("vector", b, dv) if c == 7 else ph.add("vector", b))
        if w0 < TC:
            beta_mm(cx.beta_main(w0, wn), wn)
        gate_mm(w0, wn)

    for wi in range(8):
        w0 = wi * 256
        ph.wait("sync", sc)
        ph.wait("sync", dv)
        ph.addi("sync", lambda h, w0=w0: h.dma_start(
            t["SCF"][:, 0:2048].rearrange("p (c n) -> p c n", c=8),
            t["xpre"][:, :, w0:w0 + 256]), sd, 16)
        ph.wait("scalar", sd)
        ph.wait("vector", sd)
        ph.wait("tensor", sd)
        srcs = [t["SCF"][:, c * 256:(c + 1) * 256] for c in range(8)]
        r = ln_rep(cx, ph, srcs, 256, rho=True)
        ph.wait("vector", sc)
        for c in range(8):
            ph.add("vector", lambda h, c=c, r=r: h.tensor_tensor(
                t["SCB2"][:, c * 256: (c + 1) * 256], srcs[c], r["m_rep"], ALU.subtract))
        for c in range(8):
            ph.add("vector", lambda h, c=c, r=r: h.tensor_tensor(
                t["SCB2"][:, c * 256: (c + 1) * 256], t["SCB2"][:, c * 256: (c + 1) * 256],
                r["rho"], ALU.mult))
        for c in range(8):
            b = lambda h, c=c: h.tensor_scalar(
                t["SCB2"][:, c * 256: (c + 1) * 256], t["SCB2"][:, c * 256: (c + 1) * 256],
                t["VEC"][:, 0 + c: 1 + c], t["VEC"][:, 8 + c: 9 + c], ALU.mult, ALU.add)
            (ph.addi("vector", b, dv) if c == 7 else ph.add("vector", b))
        beta_mm(cx.beta_pre(w0, 256), 256)



def emit_p3(cx, ph):
    t = cx.t
    pe, dv, sc, sd, gd, gp = cx.pe, cx.dv, cx.sc, cx.sd, cx.gd, cx.gp

    evac_n = []
    for n in range(64):
        bank = "PST" if n % 2 == 0 else "PST2"
        pst = t[bank][0:64, 0:128]
        if n >= 2:
            ph.wait("tensor", dv, evac_n[n - 2])
        ph.addi("tensor", lambda h, n=n, pst=pst: h.transpose(
            pst, cx.beta_chunk(n), t["EYEB"][:]), pe)
        ph.wait("vector", pe)
        evac_n.append(ph.addi("vector", lambda h, n=n, pst=pst: h.tensor_copy(
            t["SCB1"][0:64, n * 128:(n + 1) * 128], pst), dv))
    bett_done = cx.dv.n

    ph.wait("gpsimd", cx.dv, cx.n_pkf_done)

    def toep_dma(grp):
        base = 8192 + (grp % 2) * 3072
        for st in range(3):
            for kk in range(16):
                k = grp * 16 + kk
                src = bass.AP(t["PKF"], k * 576 + st * 192 + 63, [[576, 1], [-1, 64], [1, 64]])
                dst = bass.AP(t["SCB1"], base + st * 1024 + kk * 64, [[16640, 64], [1, 64]])
                ph.addi("gpsimd", lambda h, s2=src, d=dst: h.dma_start(d, s2), gd, 16)
        return cx.gd.n

    grp_dma_done = {0: toep_dma(0), 1: toep_dma(1)}

    intra_r_ap = lambda k: bass.AP(t["WR"], k * 64, [[14336, 64], [1, 64]])
    intra_i_ap = lambda k: bass.AP(t["WR"], 4096 + k * 64, [[14336, 64], [1, 64]])
    ph.wait("tensor", dv, bett_done)
    evac_k = []
    for grp in range(4):
        base = 8192 + (grp % 2) * 3072
        ph.wait("tensor", gd, grp_dma_done[grp])
        for kk in range(16):
            k = grp * 16 + kk
            lr = t["SCB1"][0:64, base + kk * 64: base + kk * 64 + 64]
            li = t["SCB1"][0:64, base + 1024 + kk * 64: base + 1024 + kk * 64 + 64]
            nli = t["SCB1"][0:64, base + 2048 + kk * 64: base + 2048 + kk * 64 + 64]
            br = bass.AP(t["SCB1"], k, [[16640, 64], [128, 64]])
            bi = bass.AP(t["SCB1"], 64 + k, [[16640, 64], [128, 64]])
            bb = 2 * (k % 2)
            ppr = t["PSA"][0:64, bb * 512: bb * 512 + 64]
            ppi = t["PSA"][0:64, (bb + 1) * 512: (bb + 1) * 512 + 64]
            if k >= 2:
                ph.wait("tensor", dv, evac_k[k - 2])
            ph.add("tensor", lambda h, a=ppr, l=lr, r=br: h.matmul(a, l, r, start=True, stop=False))
            ph.add("tensor", lambda h, a=ppr, l=nli, r=bi: h.matmul(a, l, r, start=False, stop=True))
            ph.add("tensor", lambda h, a=ppi, l=lr, r=bi: h.matmul(a, l, r, start=True, stop=False))
            ph.addi("tensor", lambda h, a=ppi, l=li, r=br: h.matmul(a, l, r, start=False, stop=True), pe)
            ph.wait("vector", pe)
            ph.add("vector", lambda h, k=k, a=ppr: h.tensor_copy(intra_r_ap(k), a))
            evac_k.append(ph.addi("vector", lambda h, k=k, a=ppi: h.tensor_copy(intra_i_ap(k), a), dv))
        if grp + 2 <= 3:
            ph.wait("gpsimd", pe, cx.pe.n)
            grp_dma_done[grp + 2] = toep_dma(grp + 2)

    ph.wait("tensor", dv)
    copy_done = []
    for n in range(64):
        bank = "PST" if n % 2 == 0 else "PST2"
        pt_ = t[bank][:, 0:64]
        in_ri = bass.AP(t["WR"], n, [[14336, 64], [64, 128]])
        if n >= 2:
            ph.wait("tensor", dv, copy_done[n - 2])
        ph.addi("tensor", lambda h, o=pt_, i=in_ri: h.transpose(
            o, i, t["EYEB"][0:64, 0:64]), pe)
        ph.wait("vector", pe)
        crn = t["CRCI"][0:64, n * 64:(n + 1) * 64]
        cin = t["CRCI"][64:128, n * 64:(n + 1) * 64]
        ph.add("vector", lambda h, o=crn, p=pt_: h.tensor_copy(o, p[0:64, :]))
        copy_done.append(ph.addi("vector", lambda h, o=cin, p=pt_: h.tensor_copy(
            o, p[64:128, :]), dv))
        if n == 0:
            continue
        sr = t["CRCI"][0:64, n * 64 - 1: n * 64]
        si = t["CRCI"][64:128, n * 64 - 1: n * 64]
        c0r = t["CRCI"][0:64, n * 64: n * 64 + 1]
        c0i = t["CRCI"][64:128, n * 64: n * 64 + 1]
        if n == 32:
            ph.add("vector", lambda h, a=sr, o=c0r: h.scalar_tensor_tensor(
                o, a, t["FLG"][0:64, 0:1], o, ALU.mult, ALU.add))
            ph.add("vector", lambda h, a=si, o=c0i: h.scalar_tensor_tensor(
                o, a, t["FLG"][64:128, 0:1], o, ALU.mult, ALU.add))
        else:
            ph.add("vector", lambda h, a=sr, o=c0r: h.tensor_tensor(o, o, a, ALU.add))
            ph.add("vector", lambda h, a=si, o=c0i: h.tensor_tensor(o, o, a, ALU.add))

    ph.add("vector", lambda h: h.tensor_scalar(
        t["CRCI"][:, 4096:4112], t["CRCI"][:, 0:16], t["FLG"][:, 0:1], None, ALU.mult))
    ph.addi("vector", lambda h: h.scalar_tensor_tensor(
        t["CRCI"][:, 4096:4112], t["CRCI"][:, 2048:2064], t["FLG"][:, 1:2],
        t["CRCI"][:, 4096:4112], ALU.mult, ALU.add), dv)


def emit_p4a(cx, ph):
    t = cx.t
    pe, dv, sc, sd, gd = cx.pe, cx.dv, cx.sc, cx.sd, cx.gd
    ph.wait("tensor", gd)

    for w0, wn in W256:
        ccol = (2048 + w0) if w0 < TC else 4096
        ph.wait("tensor", dv)
        pr = t["PSB"][0:64, 0:wn]
        pi = t["PSC"][0:64, 0:wn]
        ph.add("tensor", lambda h, p=pr, c=ccol, wn=wn: h.matmul(
            p, t["BDC"][0:64, :], t["CRCI"][0:64, c:c + wn], start=True, stop=True))
        ph.addi("tensor", lambda h, p=pi, c=ccol, wn=wn: h.matmul(
            p, t["BDC"][64:128, :], t["CRCI"][64:128, c:c + wn], start=True, stop=True), pe)
        ph.wait("vector", pe)
        ph.add("vector", lambda h, p=pr, wn=wn: h.tensor_copy(t["CC"][0:64, 0:wn], p))
        ph.addi("vector", lambda h, p=pi, wn=wn: h.tensor_copy(
            t["CC"][0:64, 256:256 + wn], p), dv)
        ph.wait("tensor", dv)
        pm = t["PSB"][:, 0:wn]
        ps_ = t["PSC"][:, 0:wn]
        ph.add("tensor", lambda h, p=pm, w0=w0, wn=wn: h.matmul(
            p, t["ONES1"][0:1, :], cx.row("ROWS", 0, w0, wn), start=True, stop=True))
        ph.addi("tensor", lambda h, p=ps_, w0=w0, wn=wn: h.matmul(
            p, t["ONES1"][32:33, :], cx.row("ROWS", 32, w0, wn), start=True, stop=True), pe)
        ph.wait("vector", pe)
        for c in range(8):
            ph.add("vector", lambda h, c=c, p=pm, w0=w0, wn=wn: h.tensor_tensor(
                t["SCF"][:, c * 256: c * 256 + wn], cx.res(w0, wn, c), p, ALU.subtract))
        for c in range(8):
            b = lambda h, c=c, p=ps_, wn=wn: h.tensor_tensor(
                t["SCF"][:, c * 256: c * 256 + wn], t["SCF"][:, c * 256: c * 256 + wn], p, ALU.mult)
            (ph.addi("vector", b, dv) if c == 7 else ph.add("vector", b))
        for half in range(2):
            ph.wait("tensor", dv)
            for mm in range(4):
                m = half * 4 + mm
                pa = t["PSA"][:, mm * 512: mm * 512 + wn]
                ph.add("tensor", lambda h, p=pa, m=m, wn=wn: h.matmul(
                    p, t["WOUT"][:, m * 128: (m + 1) * 128], t["CC"][0:64, 0:wn],
                    start=True, stop=False))
                b = lambda h, p=pa, m=m, wn=wn: h.matmul(
                    p, t["WOUT"][:, 1024 + m * 128: 1024 + (m + 1) * 128],
                    t["CC"][0:64, 256:256 + wn], start=False, stop=True)
                (ph.addi("tensor", b, pe) if mm == 3 else ph.add("tensor", b))
            ph.wait("vector", pe)
            for mm in range(4):
                m = half * 4 + mm
                pa = t["PSA"][:, mm * 512: mm * 512 + wn]
                ph.add("vector", lambda h, p=pa, m=m, w0=w0, wn=wn: h.tensor_tensor(
                    t["CC"][:, 0:wn] if False else t["SCB2"][:, 2048:2048 + wn],
                    cx.xng(w0, wn, m), p, ALU.mult))
                b = lambda h, m=m, wn=wn: h.tensor_tensor(
                    t["SCF"][:, m * 256: m * 256 + wn], t["SCF"][:, m * 256: m * 256 + wn],
                    t["SCB2"][:, 2048:2048 + wn], ALU.add)
                (ph.addi("vector", b, dv) if mm == 3 else ph.add("vector", b))
        resw = bass.AP(t["RES"], w0, [[8 * TT, 128], [TT, 8], [1, wn]])
        scfw = t["SCF"][:, 0:2048].rearrange("p (c n) -> p c n", c=8)[:, :, 0:wn]
        ph.add("vector", lambda h, r=resw, s2=scfw: h.tensor_tensor(r, r, s2, ALU.add))
        for c in range(8):
            b = lambda h, c=c, wn=wn: h.tensor_copy(
                t["SCB2"][:, c * wn: (c + 1) * wn], t["SCF"][:, c * 256: c * 256 + wn])
            (ph.addi("vector", b, dv) if c == 7 else ph.add("vector", b))
        x1b = [t["SCB2"][:, c * wn: (c + 1) * wn] for c in range(8)]
        r2 = ln_rep(cx, ph, x1b, wn, bf16_src=True)
        ph.wait("vector", cx.sc)
        for c in range(8):
            ph.add("vector", lambda h, c=c, r2=r2, x1b=x1b, wn=wn: h.tensor_tensor(
                t["SCB1"][:, 8448 + c * wn: 8448 + (c + 1) * wn], x1b[c],
                r2["m_rep"], ALU.subtract))
        for c in range(8):
            ph.add("vector", lambda h, c=c, r2=r2, wn=wn: h.tensor_tensor(
                t["SCB1"][:, 8448 + c * wn: 8448 + (c + 1) * wn],
                t["SCB1"][:, 8448 + c * wn: 8448 + (c + 1) * wn], r2["rstd"], ALU.mult))
        for c in range(8):
            b = lambda h, c=c, w0=w0, wn=wn: h.tensor_scalar(
                cx.xng(w0, wn, c), t["SCB1"][:, 8448 + c * wn: 8448 + (c + 1) * wn],
                t["VEC"][:, 16 + c: 17 + c], t["VEC"][:, 24 + c: 25 + c], ALU.mult, ALU.add)
            (ph.addi("vector", b, dv) if c == 7 else ph.add("vector", b))


W512M = [(0, 512), (512, 512), (1024, 512), (1536, 512), (2048, 16)]


def emit_p4b(cx, ph):
    t = cx.t
    pe, dv, sc, sd, gd = cx.pe, cx.dv, cx.sc, cx.sd, cx.gd

    def gdma(slot, src):
        return ph.addi("gpsimd", lambda h, d=cx.wr(slot), s2=src: h.dma_start(d, s2), gd, 16)

    for g in range(4):
        ph.wait("gpsimd", pe)
        dma_done = []
        for mi in range(8):
            m = g * 8 + mi
            dma_done.append(gdma(mi, t["w_mlp1T"][m].rearrange("p c m -> p (c m)")))
        hid_done = []
        gidx = 0
        for mi in range(8):
            m = g * 8 + mi
            cx.qf.feed(ph, 2)
            ph.wait("tensor", gd, dma_done[mi])
            for wi, (w0, wn) in enumerate(W512M):
                pa = t["PSA"][:, (gidx % 4) * 512: (gidx % 4) * 512 + wn]
                gidx += 1
                if len(hid_done) >= 4:
                    ph.wait("tensor", sc, hid_done[-4])
                for c in range(8):
                    b = lambda h, p=pa, mi=mi, c=c, w0=w0, wn=wn: h.matmul(
                        p, cx.wr(mi)[:, c * 128: (c + 1) * 128],
                        cx.xng(w0, wn, c), start=(c == 0), stop=(c == 7))
                    (ph.addi("tensor", b, pe) if c == 7 else ph.add("tensor", b))
                ph.wait("scalar", pe)
                hid_done.append(ph.addi("scalar", lambda h, p=pa, mi=mi, m=m, w0=w0, wn=wn: h.activation(
                    t["SCB1"][:, mi * 2080 + w0: mi * 2080 + w0 + wn], p, AF.Silu,
                    bias=t["VECB"][:, m: m + 1]), sc))
        dma2 = []
        ph.wait("gpsimd", sc, hid_done[-1])
        for mi in range(8):
            kc = g * 8 + mi
            dma2.append(gdma(mi, t["w_mlp2T"][kc]))
        ph.wait("tensor", gd, dma2[-1])
        ph.wait("tensor", sc, hid_done[-1])
        for w0, wn in W512M:
            cx.qf.feed(ph, 2)
            for mo in range(8):
                bank = "PSB" if mo % 2 == 0 else "PSC"
                pz = t[bank][:, 0:wn]
                ph.wait("tensor", dv)
                for mi in range(8):
                    b = lambda h, p=pz, mi=mi, mo=mo, w0=w0, wn=wn: h.matmul(
                        p, cx.wr(mi)[:, mo * 128: (mo + 1) * 128],
                        t["SCB1"][:, mi * 2080 + w0: mi * 2080 + w0 + wn],
                        start=(mi == 0), stop=(mi == 7))
                    (ph.addi("tensor", b, pe) if mi == 7 else ph.add("tensor", b))
                ph.wait("vector", pe)
                resw = t["RES"][:, mo * TT + w0: mo * TT + w0 + wn]
                if g == 0:
                    b = lambda h, p=pz, r=resw, mo=mo: h.scalar_tensor_tensor(
                        r, p, t["VEC"][:, 40 + mo: 41 + mo], r, ALU.add, ALU.add)
                else:
                    b = lambda h, p=pz, r=resw: h.tensor_tensor(r, r, p, ALU.add)
                ph.addi("vector", b, dv)


def emit_p5(cx, ph):
    t = cx.t
    pe, dv, sc, sd, gd = cx.pe, cx.dv, cx.sc, cx.sd, cx.gd

    emit_plain_ln(cx, ph)

    ph.wait("gpsimd", pe)
    wg_done = ph.addi("gpsimd", lambda h: h.dma_start(
        cx.wr(13, 128), t["WgT"][:].rearrange("p c m -> p (c m)")), gd, 16)
    ph.wait("tensor", gd, wg_done)
    for wi, (w0, wn) in enumerate(W512):
        ph.wait("tensor", dv)
        psc = t["PSC"][0:16, 0:wn]
        for c in range(8):
            b = lambda h, c=c, p=psc, w0=w0, wn=wn: h.matmul(
                p, cx.wr(13, 128)[:, c * 16:(c + 1) * 16],
                cx.xng(w0, wn, c), start=(c == 0), stop=(c == 7))
            (ph.addi("tensor", b, pe) if c == 7 else ph.add("tensor", b))
        ph.wait("vector", pe)
        ph.addi("vector", lambda h, p=psc, wn=wn: h.tensor_copy(t["SCF"][0:16, 0:wn], p), dv)
        ntile = (wn + 127) // 128
        for tt_ in range(ntile):
            tn = min(128, wn - tt_ * 128)
            ph.wait("tensor", dv)
            psb = t["PSB"][0:tn, 0:16]
            ph.addi("tensor", lambda h, p=psb, tt_=tt_, tn=tn: h.transpose(
                p, t["SCF"][0:16, tt_ * 128: tt_ * 128 + tn], t["EYEF"][0:16, 0:16]), pe)
            ph.wait("vector", pe)
            TOK = t["SCF"][0:tn, 512:528]
            CW = t["SCF"][0:tn, 528:544]
            R1 = t["SCF"][0:tn, 544:545]
            R2 = t["SCF"][0:tn, 545:546]
            ph.add("vector", lambda h, p=psb, TOK=TOK: h.tensor_copy(TOK, p))
            ph.add("vector", lambda h, TOK=TOK, R1=R1: h.reduce_max(R1, TOK, AX.X))
            ph.addi("vector", lambda h, TOK=TOK, CW=CW, R1=R1: h.tensor_scalar(
                CW, TOK, R1, None, ALU.subtract), dv)
            ph.wait("scalar", dv)
            ph.addi("scalar", lambda h, CW=CW: h.activation(CW, CW, AF.Exp), cx.sc)
            ph.wait("vector", cx.sc)
            ph.add("vector", lambda h, CW=CW, R2=R2: h.tensor_reduce(R2, CW, AX.X, ALU.add))
            ph.add("vector", lambda h, R1=R1, R2=R2: h.reciprocal(R1, R2))
            ph.add("vector", lambda h, TOK=TOK, CW=CW, R1=R1: h.tensor_scalar(TOK, CW, R1, None, ALU.mult))
            ph.add("vector", lambda h, TOK=TOK, R1=R1: h.reduce_max(R1, TOK, AX.X))
            ph.add("vector", lambda h, TOK=TOK, CW=CW, R1=R1: h.tensor_scalar(CW, TOK, R1, None, ALU.subtract))
            ph.add("vector", lambda h, CW=CW: h.tensor_scalar(CW, CW, 1e38, None, ALU.mult))
            ph.add("vector", lambda h, CW=CW: h.tensor_scalar(CW, CW, -1.0, 1.0, ALU.max, ALU.add))
            ph.add("vector", lambda h, TOK=TOK, CW=CW: h.scalar_tensor_tensor(CW, CW, -1e9, TOK, ALU.mult, ALU.add))
            ph.add("vector", lambda h, CW=CW, R2=R2: h.reduce_max(R2, CW, AX.X))
            ph.add("vector", lambda h, TOK=TOK, CW=CW, R2=R2: h.tensor_scalar(CW, TOK, R2, None, ALU.subtract))
            ph.add("vector", lambda h, CW=CW: h.tensor_scalar(CW, CW, 1e38, None, ALU.mult))
            ph.add("vector", lambda h, CW=CW: h.tensor_scalar(CW, CW, -1.0, 1.0, ALU.max, ALU.add))
            ph.add("vector", lambda h, CW=CW: h.tensor_scalar(CW, CW, 1.0, None, ALU.min))
            ph.addi("vector", lambda h, TOK=TOK, CW=CW: h.tensor_tensor(CW, CW, TOK, ALU.mult), dv)
            pst = t["PSB"][0:16, 256:256 + tn]
            ph.wait("tensor", dv)
            ph.addi("tensor", lambda h, p=pst, tn=tn: h.transpose(
                p, bass.AP(t["SCF"], 528, [[2048, tn], [1, 16]]), t["EYEF"][0:tn, 0:tn]), pe)
            ph.wait("vector", pe)
            col = w0 + tt_ * 128
            ph.addi("vector", lambda h, p=pst, col=col, tn=tn: h.tensor_copy(
                t["SCB2"][0:16, col:col + tn], p), dv)

    ys_dma = {}
    for j in range(8):
        su, sg = (j % 2) * 2, (j % 2) * 2 + 1
        if j >= 2:
            ph.wait("gpsimd", pe)
        ys_dma[j] = (ph.addi("gpsimd", lambda h, j=j, su=su: h.dma_start(
            cx.wr(su), t["Ws1T"][j].rearrange("p c m -> p (c m)")), gd, 16),
            ph.addi("gpsimd", lambda h, j=j, sg=sg: h.dma_start(
                cx.wr(sg), t["Ws1T"][j + 8].rearrange("p c m -> p (c m)")), gd, 16))
        su_, sg_ = su, sg
        ph.wait("tensor", gd, ys_dma[j][1])
        for w0, wn in W512:
            cx.qf.feed(ph, 2)
            ph.wait("tensor", cx.sc)
            ph.wait("tensor", dv)
            pu = t["PSA"][:, 0:wn]
            pg = t["PSA"][:, 512:512 + wn]
            for c in range(8):
                ph.add("tensor", lambda h, c=c, p=pu, su=su_, w0=w0, wn=wn: h.matmul(
                    p, cx.wr(su)[:, c * 128:(c + 1) * 128], cx.xng(w0, wn, c),
                    start=(c == 0), stop=(c == 7)))
            for c in range(8):
                b = lambda h, c=c, p=pg, sg=sg_, w0=w0, wn=wn: h.matmul(
                    p, cx.wr(sg)[:, c * 128:(c + 1) * 128], cx.xng(w0, wn, c),
                    start=(c == 0), stop=(c == 7))
                (ph.addi("tensor", b, pe) if c == 7 else ph.add("tensor", b))
            ph.wait("scalar", pe)
            ph.addi("scalar", lambda h, p=pg, wn=wn: h.activation(
                t["SCB2"][:, 2176:2176 + wn], p, AF.Silu), cx.sc)
            ph.wait("vector", cx.sc)
            b = lambda h, p=pu, j=j, w0=w0, wn=wn: h.tensor_tensor(
                t["SCB1"][:, j * 2080 + w0: j * 2080 + w0 + wn], p,
                t["SCB2"][:, 2176:2176 + wn], ALU.mult)
            ph.addi("vector", b, dv)
    ws2_dma = []
    ph.wait("gpsimd", dv)
    for kc in range(8):
        ws2_dma.append(ph.addi("gpsimd", lambda h, kc=kc: h.dma_start(
            cx.wr(4 + kc), t["Ws2T"][kc]), gd, 16))
    ph.wait("tensor", gd, ws2_dma[-1])
    ph.wait("tensor", dv)
    for w0, wn in W512:
        for mo in range(8):
            bank = "PSB" if mo % 2 == 0 else "PSC"
            pz = t[bank][:, 0:wn]
            ph.wait("tensor", dv)
            for kc in range(8):
                b = lambda h, p=pz, kc=kc, mo=mo, w0=w0, wn=wn: h.matmul(
                    p, cx.wr(4 + kc)[:, mo * 128:(mo + 1) * 128],
                    t["SCB1"][:, kc * 2080 + w0: kc * 2080 + w0 + wn],
                    start=(kc == 0), stop=(kc == 7))
                (ph.addi("tensor", b, pe) if kc == 7 else ph.add("tensor", b))
            ph.wait("vector", pe)
            resw = t["RES"][:, mo * TT + w0: mo * TT + w0 + wn]
            ph.addi("vector", lambda h, p=pz, r=resw: h.tensor_tensor(r, r, p, ALU.add), dv)

    for e in range(16):
        ph.wait("gpsimd", pe)
        d1 = []
        for mp in range(8):
            d1.append(ph.addi("gpsimd", lambda h, e=e, mp=mp: h.dma_start(
                cx.wr(mp), t["W1e"][e, mp].rearrange("p c m -> p (c m)")), gd, 16))
        d2 = []
        for kp in range(4):
            d2.append(ph.addi("gpsimd", lambda h, e=e, kp=kp: h.dma_start(
                cx.wr(8 + kp), t["W2e"][e, kp]), gd, 16))
        for w0, wn in W512:
            cx.qf.feed(ph, 2)
            cwb = t["SCB2"][:, 2688:2688 + wn]
            ph.wait("gpsimd", dv)
            cwb_done = ph.addi("gpsimd", lambda h, e=e, w0=w0, wn=wn, d=cwb: h.dma_start(
                d, bass.AP(t["SCB2"], 2048 * 0 + e * 4096 + w0, [[4096, 1], [0, 128], [1, wn]])), gd, 16)
            for j in range(4):
                ph.wait("tensor", gd, d1[j + 4])
                ph.wait("tensor", cx.sc)
                ph.wait("tensor", dv)
                pu = t["PSA"][:, (j % 2) * 1024: (j % 2) * 1024 + wn]
                pg = t["PSA"][:, (j % 2) * 1024 + 512: (j % 2) * 1024 + 512 + wn]
                for c in range(8):
                    ph.add("tensor", lambda h, c=c, p=pu, j=j, w0=w0, wn=wn: h.matmul(
                        p, cx.wr(j)[:, c * 128:(c + 1) * 128], cx.xng(w0, wn, c),
                        start=(c == 0), stop=(c == 7)))
                for c in range(8):
                    b = lambda h, c=c, p=pg, j=j, w0=w0, wn=wn: h.matmul(
                        p, cx.wr(j + 4)[:, c * 128:(c + 1) * 128], cx.xng(w0, wn, c),
                        start=(c == 0), stop=(c == 7))
                    (ph.addi("tensor", b, pe) if c == 7 else ph.add("tensor", b))
                ph.wait("scalar", pe)
                ph.wait("scalar", gd, cwb_done)
                ph.addi("scalar", lambda h, p=pg, wn=wn: h.activation(
                    t["SCB2"][:, 3584:3584 + wn], p, AF.Silu), cx.sc)
                ph.wait("vector", cx.sc)
                ph.add("vector", lambda h, p=pu, j=j, wn=wn: h.tensor_tensor(
                    t["SCB1"][:, j * 512: j * 512 + wn], p,
                    t["SCB2"][:, 3584:3584 + wn], ALU.mult))
                ph.addi("vector", lambda h, j=j, wn=wn, d=cwb: h.tensor_tensor(
                    t["SCB1"][:, j * 512: j * 512 + wn],
                    t["SCB1"][:, j * 512: j * 512 + wn], d, ALU.mult), dv)
            ph.wait("tensor", gd, d2[-1])
            ph.wait("tensor", dv)
            for mo in range(8):
                bank = "PSB" if mo % 2 == 0 else "PSC"
                pz = t[bank][:, 0:wn]
                ph.wait("tensor", dv)
                for kc in range(4):
                    b = lambda h, p=pz, kc=kc, mo=mo, wn=wn: h.matmul(
                        p, cx.wr(8 + kc)[:, mo * 128:(mo + 1) * 128],
                        t["SCB1"][:, kc * 512: kc * 512 + wn],
                        start=(kc == 0), stop=(kc == 3))
                    (ph.addi("tensor", b, pe) if kc == 3 else ph.add("tensor", b))
                ph.wait("vector", pe)
                resw = t["RES"][:, mo * TT + w0: mo * TT + w0 + wn]
                ph.addi("vector", lambda h, p=pz, r=resw: h.tensor_tensor(r, r, p, ALU.add), dv)


def emit_p6(cx, ph):
    t = cx.t
    pe, dv, sc, sd, gd = cx.pe, cx.dv, cx.sc, cx.sd, cx.gd
    cgrid = cx.wr(13, 1024)
    ph.addi("gpsimd", lambda h: h.iota(cgrid, [[0, 16], [1, 16], [0, 4]],
                                       base=-1, channel_multiplier=0,
                                       allow_small_or_imprecise_dtypes=True), cx.gp)

    emit_plain_ln(cx, ph)

    ph.wait("gpsimd", pe)
    dval = []
    for mp in range(8):
        dval.append(ph.addi("gpsimd", lambda h, mp=mp: h.dma_start(
            cx.wr(mp), t["W_valueT"][mp].rearrange("p c m -> p (c m)")), gd, 16))
    VT = t["SCB2"][:, 3900:4012]
    for mp in range(8):
        ph.wait("tensor", gd, dval[mp])
        ph.wait("tensor", dv)
        pv = t["PSB"][:, 0:14]
        for c in range(8):
            b = lambda h, c=c, mp=mp, p=pv: h.matmul(
                p, cx.wr(mp)[:, c * 128:(c + 1) * 128],
                t["XNG"][:, c * TT + 2048: c * TT + 2062], start=(c == 0), stop=(c == 7))
            (ph.addi("tensor", b, pe) if c == 7 else ph.add("tensor", b))
        ph.wait("vector", pe)
        ph.addi("vector", lambda h, p=pv, mp=mp: h.tensor_scalar(
            VT[:, mp * 14:(mp + 1) * 14], p, t["VEC2"][:, mp: mp + 1], None, ALU.add), dv)
    for c in range(8):
        pst = t["PST"][0:14, 0:128]
        ph.wait("tensor", dv)
        ph.addi("tensor", lambda h, c=c, p=pst: h.transpose(
            p, VT[:, c * 14:(c + 1) * 14], t["EYEB"][:]), pe)
        ph.wait("vector", pe)
        ph.addi("vector", lambda h, c=c, p=pst: h.tensor_copy(
            t["V14"][0:14, c * 128:(c + 1) * 128], p), dv)
    ph.addi("gpsimd", lambda h: h.memset(t["BDV"][:], 0.0), cx.gp)
    ph.wait("sync", dv)
    ph.wait("sync", cx.gp)
    for a in range(4):
        src0 = bass.AP(t["V14"], 0, [[1024, 14], [128, 8], [1, 64]])
        dst0 = bass.AP(t["BDV"], a * 32 * 1024, [[1024, 14], [128, 8], [1, 64]])
        src1 = bass.AP(t["V14"], 64, [[1024, 14], [128, 8], [1, 64]])
        dst1 = bass.AP(t["BDV"], (a * 32 + 16) * 1024 + 64, [[1024, 14], [128, 8], [1, 64]])
        ph.addi("sync", lambda h, d=dst0, s2=src0: h.dma_start(d, s2), sd, 16)
        ph.addi("sync", lambda h, d=dst1, s2=src1: h.dma_start(d, s2), sd, 16)
    bdv_done = cx.sd.n

    doa = []
    for pcs in range(2):
        doa.append(ph.addi("gpsimd", lambda h, pcs=pcs: h.dma_start(
            cx.wr(8 + pcs), t["W_oaT"][pcs].rearrange("p c m -> p (c m)")), gd, 16))
    ph.wait("tensor", gd, doa[-1])
    for wi, (w0, wn) in enumerate(W512):
        ph.wait("tensor", dv)
        po = t["PSA"][:, 0:wn]
        pa_ = t["PSA"][0:64, 512:512 + wn]
        for c in range(8):
            ph.add("tensor", lambda h, c=c, p=po, w0=w0, wn=wn: h.matmul(
                p, cx.wr(8)[:, c * 128:(c + 1) * 128], cx.xng(w0, wn, c),
                start=(c == 0), stop=(c == 7)))
        for c in range(8):
            b = lambda h, c=c, p=pa_, w0=w0, wn=wn: h.matmul(
                p, cx.wr(9)[:, c * 128: c * 128 + 64], cx.xng(w0, wn, c),
                start=(c == 0), stop=(c == 7))
            (ph.addi("tensor", b, pe) if c == 7 else ph.add("tensor", b))
        ph.wait("vector", pe)
        ph.add("vector", lambda h, p=po, wn=wn: h.tensor_scalar(
            t["SCF"][:, 0:wn], p, t["VEC2"][:, 24:25], None, ALU.add))
        ph.addi("vector", lambda h, p=pa_, wn=wn: h.tensor_scalar(
            t["SCF"][0:64, 512:512 + wn], p, bass.AP(t["VEC2"], 25, [[32, 64], [1, 1]]),
            None, ALU.add), dv)
        ntile = (wn + 127) // 128
        for tt_ in range(ntile):
            tn = min(128, wn - tt_ * 128)
            gtile = (w0 + tt_ * 128) // 128
            cx.qf.feed(ph, 3)
            ph.wait("tensor", dv)
            pot = t["PSB"][0:tn, 0:128]
            pat = t["PSC"][0:tn, 0:64]
            ph.add("tensor", lambda h, p=pot, tt_=tt_, tn=tn: h.transpose(
                p, t["SCF"][:, tt_ * 128: tt_ * 128 + tn], t["EYEF"][:]))
            ph.addi("tensor", lambda h, p=pat, tt_=tt_, tn=tn: h.transpose(
                p, t["SCF"][0:64, 512 + tt_ * 128: 512 + tt_ * 128 + tn],
                t["EYEF"][0:64, 0:64]), pe)
            ph.wait("vector", pe)
            OT = t["SCF"][0:tn, 1024:1152]
            AT = t["SCF"][0:tn, 1152:1216]
            AM = t["SCF"][0:tn, 1216:1232]
            AW = t["SCF"][0:tn, 1232:1296]
            IX = t["SCF"][0:tn, 1296:1360]
            IY = t["SCF"][0:tn, 1360:1424]
            WX = t["SCF"][0:tn, 1424:1488]
            SS = t["SCF"][0:tn, 1488:1744]
            UU = t["SCF"][0:tn, 1744:1808]
            ph.add("vector", lambda h, p=pot, OT=OT: h.tensor_copy(OT, p))
            ph.addi("vector", lambda h, p=pat, AT=AT: h.tensor_copy(AT, p), dv)
            ph.add("vector", lambda h, AT=AT, AM=AM, tn=tn: h.tensor_reduce(
                AM, AT.rearrange("p (a b) -> p a b", b=4), AX.X, ALU.max))
            amb = lambda AM, tn: bass.AP(AM.tensor if False else t["SCF"], 0, [[0, 0]])
            ph.addi("vector", lambda h, AT=AT, AM=AM, tn=tn: h.tensor_tensor(
                AT, AT, bass.AP(t["SCF"], 1216, [[2048, tn], [1, 16], [0, 4]]), ALU.subtract), dv)
            ph.wait("scalar", dv)
            ph.addi("scalar", lambda h, AT=AT: h.activation(AT, AT, AF.Exp), cx.sc)
            ph.wait("vector", cx.sc)
            ph.add("vector", lambda h, AT=AT, AM=AM: h.tensor_reduce(
                AM, AT.rearrange("p (a b) -> p a b", b=4), AX.X, ALU.add))
            ph.add("vector", lambda h, AM=AM: h.reciprocal(AM, AM))
            ph.add("vector", lambda h, AT=AT, AW=AW, tn=tn: h.tensor_tensor(
                AW, AT, bass.AP(t["SCF"], 1216, [[2048, tn], [1, 16], [0, 4]]), ALU.mult))
            ph.add("vector", lambda h, OT=OT, IX=IX, gtile=gtile, tn=tn: h.tensor_scalar(
                IX, bass.AP(t["SCF"], 1024, [[2048, tn], [2, 64]]),
                t["REFQ"][0:tn, gtile:gtile + 1], None, ALU.add))
            ph.add("vector", lambda h, IX=IX: h.tensor_scalar(IX, IX, -0.5, None, ALU.add))
            ph.add("vector", lambda h, OT=OT, IY=IY, tn=tn: h.tensor_scalar(
                IY, bass.AP(t["SCF"], 1025, [[2048, tn], [2, 64]]), -0.5, None, ALU.add))
            ph.add("vector", lambda h, IX=IX, WX=WX: h.tensor_scalar(WX, IX, -1.0, None, ALU.mult))
            ph.add("vector", lambda h, IX=IX, WX=WX: h.tensor_tensor(WX, WX, IX, ALU.max))
            ph.add("vector", lambda h, WX=WX: h.tensor_scalar(WX, WX, -1.0, 1.0, ALU.mult, ALU.add))
            ph.add("vector", lambda h, WX=WX: h.tensor_scalar(WX, WX, 0.0, None, ALU.max))
            ph.add("vector", lambda h, AW=AW, WX=WX: h.tensor_tensor(AW, AW, WX, ALU.mult))
            ph.addi("vector", lambda h, AW=AW, IY=IY, UU=UU: h.tensor_tensor(UU, AW, IY, ALU.mult), dv)
            TMP = t["PSA"][0:tn, 1024:2048]
            ph.wait("vector", cx.gp)
            gb = bass.AP(t["SCF"], 1232, [[2048, tn], [4, 16], [0, 16], [1, 4]])
            ub = bass.AP(t["SCF"], 1744, [[2048, tn], [4, 16], [0, 16], [1, 4]])
            ph.add("vector", lambda h, TMP=TMP, gb=gb, tn=tn: h.tensor_tensor(
                TMP, gb, cx.wr(13, 1024)[0:tn, :], ALU.mult))
            ph.add("vector", lambda h, TMP=TMP, ub=ub: h.tensor_tensor(TMP, ub, TMP, ALU.subtract))
            ph.add("vector", lambda h, TMP=TMP: h.tensor_scalar(TMP, TMP, 0.0, None, ALU.max))
            ph.add("vector", lambda h, TMP=TMP, SS=SS, tn=tn: h.tensor_reduce(
                SS, TMP.rearrange("p (a b) -> p a b", b=4), AX.X, ALU.add))
            sA0 = bass.AP(t["SCF"], 1488, [[2048, tn], [16, 16], [1, 14]])
            sA1 = bass.AP(t["SCF"], 1489, [[2048, tn], [16, 16], [1, 14]])
            sA2 = bass.AP(t["SCF"], 1490, [[2048, tn], [16, 16], [1, 14]])
            tS = t["SCF"][0:tn, 1808:2032].rearrange("p (a b) -> p a b", b=14)
            dA = bass.AP(t["SCB1"], 12576, [[16640, tn], [16, 16], [1, 14]])
            ph.add("vector", lambda h, tS=tS, sA0=sA0, sA2=sA2: h.tensor_tensor(
                tS, sA0, sA2, ALU.add))
            ph.addi("vector", lambda h, dA=dA, sA1=sA1, tS=tS: h.scalar_tensor_tensor(
                dA, sA1, -2.0, tS, ALU.mult, ALU.add), dv)
            tstart = w0 + tt_ * 128
            for hhalf in range(2):
                bank = "PST" if hhalf == 0 else "PST2"
                pq_ = t[bank][0:128, 0:tn]
                ph.wait("tensor", dv)
                ph.addi("tensor", lambda h, p=pq_, hhalf=hhalf, tn=tn: h.transpose(
                    p, t["SCB1"][0:tn, 12576 + hhalf * 128: 12576 + (hhalf + 1) * 128],
                    t["EYEB"][0:tn, 0:tn]), pe)
                ph.wait("vector", pe)
                ph.addi("vector", lambda h, p=pq_, hhalf=hhalf, tstart=tstart, tn=tn: h.tensor_copy(
                    t["SCB1"][:, 8448 + hhalf * 2064 + tstart: 8448 + hhalf * 2064 + tstart + tn], p), dv)

    dwo = []
    ph.wait("gpsimd", pe)
    for mp in range(8):
        dwo.append(ph.addi("gpsimd", lambda h, mp=mp: h.dma_start(
            cx.wr(mp), t["W_out1T"][mp].rearrange("p c m -> p (c m)")), gd, 16))
    ph.wait("tensor", sd, bdv_done)
    for w0, wn in W512:
        for p_ in range(8):
            bank = "PSB" if p_ % 2 == 0 else "PSC"
            pz = t[bank][:, 0:wn]
            ph.wait("tensor", dv)
            afm_col = 8448 + (p_ // 4) * 2064 + w0
            arow = 32 * (p_ % 4)
            b = lambda h, p=pz, p_=p_, afm_col=afm_col, arow=arow, wn=wn: h.matmul(
                p, t["BDV"][arow:arow + 32, p_ * 128:(p_ + 1) * 128],
                t["SCB1"][arow:arow + 32, afm_col:afm_col + wn], start=True, stop=True,
                tile_position=(arow, 0))
            ph.addi("tensor", b, pe)
            ph.wait("vector", pe)
            ph.addi("vector", lambda h, p=pz, p_=p_, wn=wn: h.tensor_copy(
                t["SCB1"][:, p_ * 512: p_ * 512 + wn], p), dv)
        for mo in range(8):
            bank = "PSB" if mo % 2 == 0 else "PSC"
            pz = t[bank][:, 0:wn]
            ph.wait("tensor", dv)
            ph.wait("tensor", gd, dwo[-1])
            for kc in range(8):
                b = lambda h, p=pz, kc=kc, mo=mo, wn=wn: h.matmul(
                    p, cx.wr(mo)[:, kc * 128:(kc + 1) * 128],
                    t["SCB1"][:, kc * 512: kc * 512 + wn], start=(kc == 0), stop=(kc == 7))
                (ph.addi("tensor", b, pe) if kc == 7 else ph.add("tensor", b))
            ph.wait("vector", pe)
            ph.addi("vector", lambda h, p=pz, mo=mo, w0=w0, wn=wn: h.tensor_scalar(
                cx.xng(w0, wn, mo), p, t["VEC2"][:, 8 + mo: 9 + mo], None, ALU.add), dv)

    dwp = []
    ph.wait("gpsimd", pe)
    for mp in range(8):
        dwp.append(ph.addi("gpsimd", lambda h, mp=mp: h.dma_start(
            cx.wr(mp), t["W_projT"][mp].rearrange("p c m -> p (c m)")), gd, 16))
    ph.wait("tensor", gd, dwp[-1])
    for w0, wn in W512:
        for mo in range(8):
            bank = "PSB" if mo % 2 == 0 else "PSC"
            pz = t[bank][:, 0:wn]
            ph.wait("tensor", dv)
            for kc in range(8):
                b = lambda h, p=pz, kc=kc, mo=mo, w0=w0, wn=wn: h.matmul(
                    p, cx.wr(mo)[:, kc * 128:(kc + 1) * 128],
                    cx.xng(w0, wn, kc), start=(kc == 0), stop=(kc == 7))
                (ph.addi("tensor", b, pe) if kc == 7 else ph.add("tensor", b))
            ph.wait("vector", pe)
            resw = t["RES"][:, mo * TT + w0: mo * TT + w0 + wn]
            ph.addi("vector", lambda h, p=pz, r=resw, mo=mo: h.scalar_tensor_tensor(
                r, p, t["VEC2"][:, 16 + mo: 17 + mo], r, ALU.add, ALU.add), dv)


def emit_p7(cx, ph):
    t = cx.t
    nc = cx.nc
    pe, dv, sc, sd, gd = cx.pe, cx.dv, cx.sc, cx.sd, cx.gd

    cx.qf.drain(ph)
    ph.wait("gpsimd", gd)

    emit_plain_ln(cx, ph)

    def token_absmax_rows(src_block_fn, nblocks, srow_part, irow_part, kmax):
        TR = t["SCB2"][:, 0:2064]
        TM = t["SCB1"][:, 0:2064]
        ph.add("vector", lambda h: h.memset(t["SCF"][:, 0:17], 0.0))
        ph.add("vector", lambda h: h.tensor_tensor(TR, src_block_fn(0), src_block_fn(1), ALU.max))
        ph.add("vector", lambda h: h.tensor_tensor(TM, src_block_fn(0), src_block_fn(1), ALU.min))
        for b2 in range(2, nblocks):
            ph.add("vector", lambda h, b2=b2: h.tensor_tensor(TR, TR, src_block_fn(b2), ALU.max))
            ph.add("vector", lambda h, b2=b2: h.tensor_tensor(TM, TM, src_block_fn(b2), ALU.min))
        ph.add("vector", lambda h: h.tensor_scalar(TM, TM, -1.0, None, ALU.mult))
        ph.addi("vector", lambda h: h.tensor_tensor(TR, TR, TM, ALU.max), dv)
        ph.wait("tensor", dv)
        for tt_ in range(17):
            tn = min(128, 2064 - tt_ * 128)
            bank = "PST" if tt_ % 2 == 0 else "PST2"
            pq_ = t[bank][0:tn, 0:128]
            if tt_ >= 2:
                ph.wait("tensor", dv)
            ph.addi("tensor", lambda h, p=pq_, tt_=tt_, tn=tn: h.transpose(
                p, TR[:, tt_ * 128: tt_ * 128 + tn], t["EYEB"][:]), pe)
            ph.wait("vector", pe)
            ph.addi("vector", lambda h, p=pq_, tt_=tt_, tn=tn: h.tensor_reduce(
                t["SCF"][0:tn, tt_:tt_ + 1], p, AX.X, ALU.max, apply_absolute_value=True), dv)
        ph.add("vector", lambda h: h.tensor_scalar(
            t["SCF"][:, 0:17], t["SCF"][:, 0:17], 1e-5, None, ALU.max))
        ph.add("vector", lambda h: h.tensor_scalar(
            t["SCF"][:, 17:34], t["SCF"][:, 0:17], 1.0 / float(kmax), None, ALU.mult))
        ph.add("vector", lambda h: h.reciprocal(t["SCF"][:, 34:51], t["SCF"][:, 0:17]))
        ph.addi("vector", lambda h: h.tensor_scalar(
            t["SCF"][:, 34:51], t["SCF"][:, 34:51], float(kmax), None, ALU.mult), dv)
        ph.wait("tensor", dv)
        psx = t["PSC"][0:34, 0:128]
        ph.addi("tensor", lambda h, p=psx: h.transpose(
            p, t["SCF"][:, 17:51], t["EYEF"][:]), pe)
        ph.wait("vector", pe)
        ph.addi("vector", lambda h, p=psx: h.tensor_copy(t["SCF"][0:34, 64:192], p), dv)
        ph.wait("sync", dv)
        for part, roff in ((irow_part, 0), (srow_part, 17)):
            src = bass.AP(t["SCF"], roff * 2048 + 64, [[2048, 16], [1, 128]])
            dstrow = bass.AP(t["ROWS"], part * 2064, [[2064, 1], [128, 16], [1, 128]])
            ph.addi("sync", lambda h, d=dstrow, s2=src: h.dma_start(d, s2), sd, 16)
            src2 = bass.AP(t["SCF"], (roff + 16) * 2048 + 64, [[2048, 1], [1, 16]])
            dst2 = bass.AP(t["ROWS"], part * 2064 + 2048, [[2064, 1], [1, 16]])
            ph.addi("sync", lambda h, d=dst2, s2=src2: h.dma_start(d, s2), sd, 16)
        ph.wait("vector", sd)
        ph.wait("tensor", sd)

    token_absmax_rows(lambda c: bass.AP(t["XNG"], c * TT, [[8 * TT, 128], [1, 2064]]),
                      8, 64, 96, 127.0)
    if os.environ.get("BASSMK_P7CUT") == "1":
        return

    for w0, wn in [(i * 256, 256) for i in range(8)] + [(2048, 16)]:
        ph.wait("tensor", dv)
        psb = t["PSB"][:, 0:wn]
        ph.addi("tensor", lambda h, p=psb, w0=w0, wn=wn: h.matmul(
            p, t["ONES1"][64:65, :], cx.row("ROWS", 64, w0, wn), start=True, stop=True), pe)
        ph.wait("vector", pe)
        for c in range(8):
            ph.add("vector", lambda h, c=c, p=psb, w0=w0, wn=wn: h.tensor_tensor(
                t["SCF"][:, c * 256: c * 256 + wn], cx.xng(w0, wn, c), p, ALU.mult))
        for c in range(8):
            ph.add("vector", lambda h, c=c, wn=wn: h.tensor_scalar(
                t["SCF"][:, c * 256: c * 256 + wn], t["SCF"][:, c * 256: c * 256 + wn],
                C23, C23, ALU.add, ALU.subtract))
        for c in range(8):
            b = lambda h, c=c, w0=w0, wn=wn: h.tensor_copy(
                cx.xng(w0, wn, c), t["SCF"][:, c * 256: c * 256 + wn])
            (ph.addi("vector", b, dv) if c == 7 else ph.add("vector", b))

    if os.environ.get("BASSMK_P7CUT") == "2":
        return

    if os.environ.get("BASSMK_P7CUT") == "3":
        return
    QW = [(0, 512), (512, 512), (1024, 512), (1536, 512), (2048, 16)]
    if os.environ.get("BASSMK_NQ"):
        QW = QW[:int(os.environ["BASSMK_NQ"])]
    for w0, wn in QW:
        ph.wait("tensor", dv)
        pib = t["PSB"][:, 0:wn]
        ph.addi("tensor", lambda h, p=pib, w0=w0, wn=wn: h.matmul(
            p, t["ONES1"][96:97, :], cx.row("ROWS", 96, w0, wn), start=True, stop=True,
            tile_position=(96, 0)), pe)
        sib = t["SCF"][:, 1024:1024 + wn]
        ph.wait("vector", pe)
        ph.addi("vector", lambda h, p=pib, s=sib: h.tensor_copy(s, p), dv)
        dmas = {}
        mm_done = {}
        psum_free = {}
        silu_done = {}

        def issue(i):
            buf = (i % 3) * 4
            if i - 3 in mm_done:
                ph.wait("gpsimd", pe, mm_done[i - 3])
            src = t["kwg"][i] if i < 8 else t["kwu"][i - 8]
            dmas[i] = ph.addi("gpsimd", lambda h, s2=src, buf=buf: h.dma_start(
                cx.wr(buf, 4096), s2), gd, 16)

        ph.wait("gpsimd", pe)
        for i in range(3):
            issue(i)
        for i in range(16):
            buf = (i % 3) * 4
            mb = i % 8
            ph.wait("tensor", gd, dmas[i])
            for ms in range(4):
                half = (i * 4 + ms) % 2
                pz = t["PSA"][:, half * 1024: half * 1024 + wn]
                key = (i * 4 + ms) - 2
                if key in psum_free:
                    ph.wait("tensor", dv, psum_free[key])
                for c in range(8):
                    b = lambda h, c=c, p=pz, buf=buf, ms=ms, w0=w0, wn=wn: h.matmul(
                        p, bass.AP(t["WR"], buf * 1024 + c * 512 + ms * 128, [[14336, 128], [1, 128]]),
                        cx.xng(w0, wn, c), start=(c == 0), stop=(c == 7))
                    (ph.addi("tensor", b, pe) if c == 7 else ph.add("tensor", b))
                if ms == 3:
                    mm_done[i] = cx.pe.n
                ph.wait("vector", pe)
                if i < 8:
                    gscr = t["SCB2"][:, 2048 + half * 512: 2048 + half * 512 + wn]
                    if half in silu_done:
                        ph.wait("vector", cx.sc, silu_done[half])
                    ph.add("vector", lambda h, p=pz, s=sib, g2=gscr: h.tensor_tensor(
                        g2, p, s, ALU.mult))
                    psum_free[i * 4 + ms] = ph.addi("vector", lambda h, g2=gscr: h.tensor_scalar(
                        g2, g2, t["VEC2"][:, 29:30], None, ALU.mult), dv)
                    ph.wait("scalar", dv)
                    silu_done[half] = ph.addi("scalar", lambda h, g2=gscr, mb=mb, ms=ms, wn=wn: h.activation(
                        t["SCB1"][:, mb * 2080 + ms * 512: mb * 2080 + ms * 512 + wn],
                        g2, AF.Silu), cx.sc)
                else:
                    uscr = t["SCB2"][:, 3072 + half * 512: 3072 + half * 512 + wn]
                    ph.add("vector", lambda h, p=pz, s=sib, u2=uscr: h.tensor_tensor(
                        u2, p, s, ALU.mult))
                    psum_free[i * 4 + ms] = ph.addi("vector", lambda h, u2=uscr: h.tensor_scalar(
                        u2, u2, t["VEC2"][:, 30:31], None, ALU.mult), dv)
                    if mb == 0 and ms == 0:
                        ph.wait("vector", cx.sc)
                    b = lambda h, mb=mb, ms=ms, wn=wn, u2=uscr: h.tensor_tensor(
                        t["SCB1"][:, mb * 2080 + ms * 512: mb * 2080 + ms * 512 + wn],
                        t["SCB1"][:, mb * 2080 + ms * 512: mb * 2080 + ms * 512 + wn],
                        u2, ALU.mult)
                    ph.addi("vector", b, dv)
            if i + 3 < 16:
                issue(i + 3)
        TRH = t["SCB2"][:, 0:512]
        TMH = t["SCB2"][:, 512:1024]
        ph.add("vector", lambda h: h.memset(t["SCF"][:, 0:4], 0.0))
        first = True
        for mb in range(8):
            for ms in range(4):
                sap = t["SCB1"][:, mb * 2080 + ms * 512: mb * 2080 + ms * 512 + wn]
                if first:
                    ph.add("vector", lambda h, s2=sap, wn=wn: h.tensor_copy(
                        TRH[:, 0:wn], s2))
                    ph.add("vector", lambda h, s2=sap, wn=wn: h.tensor_copy(
                        TMH[:, 0:wn], s2))
                    first = False
                else:
                    ph.add("vector", lambda h, s2=sap, wn=wn: h.tensor_tensor(
                        TRH[:, 0:wn], TRH[:, 0:wn], s2, ALU.max))
                    ph.add("vector", lambda h, s2=sap, wn=wn: h.tensor_tensor(
                        TMH[:, 0:wn], TMH[:, 0:wn], s2, ALU.min))
        ph.add("vector", lambda h, wn=wn: h.tensor_scalar(
            TMH[:, 0:wn], TMH[:, 0:wn], -1.0, None, ALU.mult))
        ph.add("vector", lambda h, wn=wn: h.tensor_tensor(
            TRH[:, 0:wn], TRH[:, 0:wn], TMH[:, 0:wn], ALU.max))
        ph.addi("vector", lambda h: h.engine_nop(), dv)
        ph.wait("tensor", dv)
        ntile = (wn + 127) // 128
        ph.add("vector", lambda h: h.memset(t["SCF"][:, 0:4], 0.0))
        for tt_ in range(ntile):
            tn = min(128, wn - tt_ * 128)
            bank = "PST" if tt_ % 2 == 0 else "PST2"
            pq_ = t[bank][0:tn, 0:128]
            if tt_ >= 2:
                ph.wait("tensor", dv)
            ph.addi("tensor", lambda h, p=pq_, tt_=tt_, tn=tn: h.transpose(
                p, TRH[:, tt_ * 128: tt_ * 128 + tn], t["EYEB"][:]), pe)
            ph.wait("vector", pe)
            ph.addi("vector", lambda h, p=pq_, tt_=tt_, tn=tn: h.tensor_reduce(
                t["SCF"][0:tn, tt_:tt_ + 1], p, AX.X, ALU.max, apply_absolute_value=True), dv)
        ph.add("vector", lambda h: h.tensor_scalar(
            t["SCF"][:, 0:4], t["SCF"][:, 0:4], 1e-5, None, ALU.max))
        ph.add("vector", lambda h: h.tensor_scalar(
            t["SCF"][:, 4:8], t["SCF"][:, 0:4], 1.0 / 127.0, None, ALU.mult))
        ph.add("vector", lambda h: h.reciprocal(t["SCF"][:, 8:12], t["SCF"][:, 0:4]))
        ph.addi("vector", lambda h: h.tensor_scalar(
            t["SCF"][:, 8:12], t["SCF"][:, 8:12], 127.0, None, ALU.mult), dv)
        ph.wait("tensor", dv)
        psx = t["PSC"][0:8, 0:128]
        ph.addi("tensor", lambda h, p=psx: h.transpose(
            p, t["SCF"][:, 4:12], t["EYEF"][:]), pe)
        ph.wait("vector", pe)
        ph.addi("vector", lambda h, p=psx: h.tensor_copy(t["SCF"][0:8, 64:192], p), dv)
        ph.wait("sync", dv)
        nfull = wn // 128
        for part, roff in ((0, 0), (32, 4)):
            if nfull > 0:
                src = bass.AP(t["SCF"], roff * 2048 + 64, [[2048, nfull], [1, 128]])
                dstrow = bass.AP(t["ROWS"], part * 2064 + w0, [[2064, 1], [128, nfull], [1, 128]])
                ph.addi("sync", lambda h, d=dstrow, s2=src: h.dma_start(d, s2), sd, 16)
            if wn % 128:
                src2 = bass.AP(t["SCF"], (roff + nfull) * 2048 + 64, [[2048, 1], [1, wn % 128]])
                dst2 = bass.AP(t["ROWS"], part * 2064 + w0 + nfull * 128, [[2064, 1], [1, wn % 128]])
                ph.addi("sync", lambda h, d=dst2, s2=src2: h.dma_start(d, s2), sd, 16)
        ph.wait("vector", sd)
        ph.wait("tensor", sd)
        ph.wait("tensor", dv)
        psh = t["PSB"][:, 0:wn]
        ph.addi("tensor", lambda h, p=psh, w0=w0, wn=wn: h.matmul(
            p, t["ONES1"][32:33, :], cx.row("ROWS", 32, w0, wn), start=True, stop=True), pe)
        ph.wait("vector", pe)
        for mb in range(8):
            for ms in range(4):
                sap = t["SCB1"][:, mb * 2080 + ms * 512: mb * 2080 + ms * 512 + wn]
                ph.add("vector", lambda h, s2=sap, p=psh, wn=wn: h.tensor_tensor(
                    t["SCF"][:, 0:wn], s2, p, ALU.mult))
                ph.add("vector", lambda h, wn=wn: h.tensor_scalar(
                    t["SCF"][:, 0:wn], t["SCF"][:, 0:wn], C23, C23, ALU.add, ALU.subtract))
                b = lambda h, s2=sap, wn=wn: h.tensor_copy(s2, t["SCF"][:, 0:wn])
                ph.addi("vector", b, dv)
        ph.wait("tensor", cx.sd)
        ph.wait("tensor", gd)
        ph.wait("tensor", dv)
        pih = t["PSB"][:, 0:wn]
        ph.addi("tensor", lambda h, p=pih, w0=w0, wn=wn: h.matmul(
            p, t["ONES1"][0:1, :], cx.row("ROWS", 0, w0, wn), start=True, stop=True), pe)
        sih = t["SCF"][:, 1024:1024 + wn]
        ph.wait("vector", pe)
        ph.addi("vector", lambda h, p=pih, s=sih: h.tensor_copy(s, p), dv)
        SLD = [[0, 1, 2, 3, 4, 5, 6, 7], [8, 9, 10, 11, 12, 13, 0, 1]]
        d_dma = {}
        d_mm = {}

        def issue_d(g2):
            lay = SLD[g2 % 2]
            dk = []
            for kk in range(8):
                if g2 % 2 == 1 and kk >= 6 and g2 - 1 in d_mm:
                    ph.wait("gpsimd", pe, d_mm[g2 - 1])
                if g2 % 2 == 0 and kk < 2 and g2 - 1 in d_mm:
                    ph.wait("gpsimd", pe, d_mm[g2 - 1])
                kc = g2 * 8 + kk
                dk.append(ph.addi("gpsimd", lambda h, kc=kc, sl=lay[kk]: h.dma_start(
                    cx.wr(sl), t["kwd"][kc]), gd, 16))
            d_dma[g2] = dk[-1]

        ph.wait("gpsimd", pe)
        issue_d(0)
        ev_dv = {-1: cx.dv.n}
        for g2 in range(4):
            lay = SLD[g2 % 2]
            ph.wait("tensor", gd, d_dma[g2])
            for mo in range(8):
                bank = "PSC" if mo % 2 == 0 else "PSB"
                pz = t[bank][:, 0:wn]
                if (g2 * 8 + mo) - 2 in ev_dv:
                    ph.wait("tensor", dv, ev_dv[(g2 * 8 + mo) - 2])
                for kk in range(8):
                    kc = g2 * 8 + kk
                    mbb, mss = kc // 4, kc % 4
                    b = lambda h, p=pz, sl=lay[kk], kk=kk, mo=mo, mbb=mbb, mss=mss, w0=w0, wn=wn: h.matmul(
                        p, bass.AP(t["WR"], sl * 1024 + mo * 128, [[14336, 128], [1, 128]]),
                        t["SCB1"][:, mbb * 2080 + mss * 512: mbb * 2080 + mss * 512 + wn],
                        start=(kk == 0), stop=(kk == 7))
                    (ph.addi("tensor", b, pe) if kk == 7 else ph.add("tensor", b))
                if mo == 7:
                    d_mm[g2] = cx.pe.n
                ph.wait("vector", pe)
                dscr = t["SCB2"][:, 2048 + (mo % 2) * 512: 2048 + (mo % 2) * 512 + wn]
                ph.add("vector", lambda h, p=pz, pih=sih, d2=dscr, wn=wn: h.tensor_tensor(
                    d2, p, pih, ALU.mult))
                ph.add("vector", lambda h, d2=dscr, wn=wn: h.tensor_scalar(
                    d2, d2, t["VEC2"][:, 31:32], None, ALU.mult))
                resw = t["RES"][:, mo * TT + w0: mo * TT + w0 + wn]
                ev_dv[g2 * 8 + mo] = ph.addi("vector", lambda h, r=resw, d2=dscr, wn=wn: h.tensor_tensor(
                    r, r, d2, ALU.add), dv)
            if g2 + 1 <= 3:
                issue_d(g2 + 1)


def emit_pf(cx, ph):
    t = cx.t
    src = bass.AP(t["RES"], 0, [[8 * TT, 128], [TT, 8], [1, TC]])
    ph.addi("sync", lambda h: h.dma_start(t["out"][:], src), cx.sd, 16)
    ph.wait("sync", cx.sd)



def prep_shared(inputs):
    g = lambda k: np.asarray(inputs[k], dtype=np.float32)
    d = {}
    d["eyef"] = np.eye(128, dtype=np.float32)
    d["eyeb"] = np.eye(128, dtype=np.float32)
    d["onesS"] = np.full((128, 128), 1.0 / 1024.0, np.float32)
    d["onesSb"] = np.full((128, 128), 1.0 / 1024.0, np.float32)
    d["ones1"] = np.ones((128, 128), np.float32)
    d["w_inT"] = lhsT_host(np.ascontiguousarray(g("L0_attn_w_in").T))
    d["w_gateT"] = lhsT_host(np.ascontiguousarray(g("L0_attn_w_gate").T))
    wot = np.ascontiguousarray(g("L0_attn_w_out").T)
    d["w_outT"] = np.ascontiguousarray(np.concatenate([wot[0:64], wot[64:128]], axis=1))
    coup = g("L0_attn_coupling")
    bd = np.zeros((64, 64), np.float32)
    for h in range(4):
        bd[h * 16:(h + 1) * 16, h * 16:(h + 1) * 16] = coup[h].T
    d["bdcoupT"] = np.concatenate([bd, bd], 0)
    d["log_decay"] = g("L0_attn_log_decay").reshape(64, 1)
    d["frequency"] = g("L0_attn_frequency").reshape(64, 1)
    w1t = lhsT_host(np.ascontiguousarray(g("L0_attn_w_mlp1").T))
    d["w_mlp1T"] = np.ascontiguousarray(
        w1t.reshape(128, 8, 32, 128).transpose(2, 0, 1, 3))
    w2t = lhsT_host(np.ascontiguousarray(g("L0_attn_w_mlp2").T))
    d["w_mlp2T"] = np.ascontiguousarray(w2t.transpose(1, 0, 2))
    for nm, key in [("ln1_g", "L0_attn_ln1_g"), ("ln1_b", "L0_attn_ln1_b"),
                    ("ln2_g", "L0_attn_ln2_g"), ("ln2_b", "L0_attn_ln2_b"),
                    ("b_gate", "L0_attn_b_gate"), ("b_mlp2", "L0_attn_b_mlp2")]:
        d[nm] = vec_host(g(key))
    d["b_mlp1"] = vec_host(g("L0_attn_b_mlp1"))
    d["WgT"] = lhsT_host(np.ascontiguousarray(g("L0_ffn_Wg").T))
    ws1 = lhsT_host(g("L0_ffn_Ws1"))
    d["Ws1T"] = np.ascontiguousarray(ws1.reshape(128, 8, 16, 128).transpose(2, 0, 1, 3))
    ws2 = lhsT_host(g("L0_ffn_Ws2"))
    d["Ws2T"] = np.ascontiguousarray(ws2.transpose(1, 0, 2))
    w1 = g("L0_ffn_W1")
    w1l = np.stack([lhsT_host(w1[e]) for e in range(16)])
    d["W1e"] = np.ascontiguousarray(
        w1l.reshape(16, 128, 8, 8, 128).transpose(0, 3, 1, 2, 4))
    w2 = g("L0_ffn_W2")
    w2l = np.stack([lhsT_host(w2[e]) for e in range(16)])
    d["W2e"] = np.ascontiguousarray(w2l.transpose(0, 2, 1, 3))
    for nm, key in [("W_valueT", "L1_attn_W_value"), ("W_out1T", "L1_attn_W_out"),
                    ("W_projT", "L1_attn_W_proj")]:
        wl = lhsT_host(g(key))
        d[nm] = np.ascontiguousarray(wl.reshape(128, 8, 8, 128).transpose(2, 0, 1, 3))
    woa = np.concatenate([g("L1_attn_W_off"), g("L1_attn_W_attn"),
                          np.zeros((1024, 64), np.float32)], 1)
    woal = lhsT_host(woa)
    d["W_oaT"] = np.ascontiguousarray(woal.reshape(128, 8, 2, 128).transpose(2, 0, 1, 3))
    d["b_value"] = vec_host(g("L1_attn_b_value"))
    d["b_out1"] = vec_host(g("L1_attn_b_out"))
    d["b_proj"] = vec_host(g("L1_attn_b_proj"))
    d["b_off"] = g("L1_attn_b_off").reshape(128, 1)
    d["b_attn"] = g("L1_attn_b_attn").reshape(64, 1)
    wq1 = lhsT_host(np.ascontiguousarray(g("L1_ffn_w_gate").T))
    d["wq_gateT"] = np.ascontiguousarray(wq1.reshape(128, 8, 8, 512).transpose(2, 0, 1, 3))
    wq2 = lhsT_host(np.ascontiguousarray(g("L1_ffn_w_up").T))
    d["wq_upT"] = np.ascontiguousarray(wq2.reshape(128, 8, 8, 512).transpose(2, 0, 1, 3))
    wq3 = lhsT_host(np.ascontiguousarray(g("L1_ffn_w_down").T))
    d["wq_downT"] = np.ascontiguousarray(wq3.transpose(1, 0, 2))
    return d


def prep_core(inputs, core):
    x = np.asarray(inputs["L0_attn_x"], dtype=np.float32)
    b, hlf = core // 2, core % 2
    d = {}
    main = x[b, hlf * TC:(hlf + 1) * TC]
    ghost = x[b, 0:GH]
    d["xmain"] = fm_host(np.concatenate([main, ghost], 0))
    d["xpre"] = fm_host(x[b, 0:TC])
    fl = np.zeros((128, 2), np.float32)
    fl[:, 0] = float(hlf)
    fl[:, 1] = 1.0 - float(hlf)
    d["flags"] = fl
    rq = np.zeros((128, 17), np.float32)
    for tile in range(17):
        if tile < 16:
            idx = hlf * TC + tile * 128 + np.arange(128)
        else:
            idx = np.arange(128) % GH
        rq[:, tile] = idx / float(T - 1)
    d["refq"] = rq
    return d


def gather_out(results):
    out = np.zeros((B, T, D), np.float32)
    for core in range(8):
        b, hlf = core // 2, core % 2
        r = results[core]["out"]
        out[b, hlf * TC:(hlf + 1) * TC] = r.transpose(1, 0, 2).reshape(D, TC).T
    return out


_in_names = None


def kernel(**inputs):
    global _in_names
    nc = build_program()
    shared = prep_shared(inputs)
    in_maps = []
    for core in range(8):
        m = dict(shared)
        m.update(prep_core(inputs, core))
        in_maps.append(m)
    res = run_bass_kernel_spmd(nc, in_maps, core_ids=list(range(8)))
    return gather_out(res.results)
